# version 01
# baseline (speedup 1.0000x reference)
import contextlib
import numpy as np
import ml_dtypes
import concourse.bass as bass
import concourse.mybir as mybir
from concourse.bass_utils import run_bass_kernel_spmd

F32 = mybir.dt.float32
BF16 = mybir.dt.bfloat16
Alu = mybir.AluOpType
Act = mybir.ActivationFunctionType
AX = mybir.AxisListType
BF = ml_dtypes.bfloat16

P = 128
T = 2048
D = 1024
NCH = D // P
SEQ = 4096
CH = 128
NCHK = T // CH
NH0 = 8
NH1 = 16
DH = 64
FF0 = 2816
FFE = 4096
EPS0 = 1e-5
EPS1 = 1e-3
WQC = 6 * P * D + P * FF0

PHASES = 5
P2STAGE = 2
TAP2 = None
import os
DBG = os.environ.get("KDBG", "")
P1STAGE = 3
TAP = None
GAMMAS = [1.0 - 2.0 ** (-5.0 - h) for h in range(NH1)]


class Sems:

    def __init__(self, nc, stack, names):
        self.h = {}
        self.n = {}
        self.byhandle = {}
        for nm in names:
            self.h[nm] = stack.enter_context(nc.semaphore(nm))
            self.n[nm] = 0
            self.byhandle[id(self.h[nm])] = nm

    def inc(self, instr, nm, k=1):
        instr.then_inc(self.h[nm], k)
        self.n[nm] += k
        return self.n[nm]

    def wait(self, eng, nm, val=None):
        eng.wait_ge(self.h[nm], self.n[nm] if val is None else val)




def _apname(x):
    try:
        return x.tensor.name
    except AttributeError:
        return None


class AutoEng:

    def __init__(self, S, nm):
        self.S = S
        self.nm = nm
        self.w = {}
        self.r = {}
        self.waited = {}
        self.raw = None

    def bind(self, raw):
        self.raw = raw
        return self

    def wait_ge(self, handle, val):
        nm = self.S.byhandle.get(id(handle))
        if nm is not None:
            if self.waited.get(nm, -1) >= val:
                return
            self.waited[nm] = max(self.waited.get(nm, -1), val)
        self.raw.wait_ge(handle, val)

    def _emit(self, ins_n, outs_n, call, inc):
        need = {}

        def add(e):
            if e is not None:
                need[e[0]] = max(need.get(e[0], 0), e[1])

        for t in ins_n:
            if t:
                add(self.w.get(t))
        for t in outs_n:
            if t:
                add(self.w.get(t))
                add(self.r.get(t))
        for semnm, val in need.items():
            if self.waited.get(semnm, -1) < val:
                self.raw.wait_ge(self.S.h[semnm], val)
                self.waited[semnm] = val
        i = call()
        semnm = inc or self.nm
        i.then_inc(self.S.h[semnm], 1)
        self.S.n[semnm] += 1
        v = self.S.n[semnm]
        for t in outs_n:
            if t:
                self.w[t] = (semnm, v)
        for t in ins_n:
            if t:
                self.r[t] = (semnm, v)
        return i

    def memset(self, ap, const, inc=None):
        return self._emit([], [_apname(ap)],
                          lambda: self.raw.memset(ap, const), inc)

    def tensor_copy(self, out, in_, inc=None):
        return self._emit([_apname(in_)], [_apname(out)],
                          lambda: self.raw.tensor_copy(out, in_), inc)

    def tensor_tensor(self, out, in0, in1, op, inc=None):
        return self._emit([_apname(in0), _apname(in1)], [_apname(out)],
                          lambda: self.raw.tensor_tensor(out, in0, in1, op),
                          inc)

    def tensor_scalar(self, out, in0, s1, s2, op0, op1=None, inc=None):
        ins = [_apname(in0), _apname(s1), _apname(s2)]
        if op1 is None:
            call = lambda: self.raw.tensor_scalar(out, in0, s1, s2, op0)
        else:
            call = lambda: self.raw.tensor_scalar(out, in0, s1, s2, op0, op1)
        return self._emit(ins, [_apname(out)], call, inc)

    def scalar_tensor_tensor(self, out, in0, scalar, in1, op0, op1, inc=None):
        return self._emit(
            [_apname(in0), _apname(scalar), _apname(in1)], [_apname(out)],
            lambda: self.raw.scalar_tensor_tensor(out, in0, scalar, in1,
                                                  op0, op1), inc)

    def tensor_reduce(self, out, in_, axis, op, apply_absolute_value=None,
                      inc=None):
        return self._emit(
            [_apname(in_)], [_apname(out)],
            lambda: self.raw.tensor_reduce(
                out, in_, axis, op,
                apply_absolute_value=apply_absolute_value), inc)

    def reciprocal(self, out, in_, inc=None):
        return self._emit([_apname(in_)], [_apname(out)],
                          lambda: self.raw.reciprocal(out, in_), inc)

    def activation(self, out, in_, func, bias=0.0, scale=1.0, accum_out=None,
                   inc=None):
        outs = [_apname(out), _apname(accum_out)]
        kw = {"bias": bias, "scale": scale}
        if accum_out is not None:
            kw["accum_out"] = accum_out
        return self._emit([_apname(in_), _apname(bias), _apname(scale)], outs,
                          lambda: self.raw.activation(out, in_, func, **kw),
                          inc)

    def copy(self, out, in_, inc=None):
        return self._emit([_apname(in_)], [_apname(out)],
                          lambda: self.raw.copy(out, in_), inc)


def build_program():
    nc = bass.Bass("TRN2", target_bir_lowering=False)
    st = contextlib.ExitStack()

    def din(name, shape, dtype=F32):
        return nc.dram_tensor(name, shape, dtype, kind="ExternalInput").ap()

    x_own = din("x_own", [D, T])
    x_part = din("x_part", [D, T])
    sgate = din("sgate", [P, 1])
    wq0 = din("wq0", [D, 512])
    wkv0 = din("wkv0", [D, 128])
    wo0 = din("wo0", [512, D])
    nvec0 = din("nvec0", [P, 4 * NCH])
    wg_sl = din("wg_sl", [384, D])
    wu_sl = din("wu_sl", [384, D])
    wd_sl = din("wd_sl", [P, FF0])
    wq1 = din("wq1", [D, D])
    wk1 = din("wk1", [D, D])
    wv1 = din("wv1", [D, D])
    wg1 = din("wg1", [D, D])
    wo1 = din("wo1", [D, D])
    nvec1 = din("nvec1", [P, 4 * NCH])
    rw = din("rw", [D, 8])
    rbias = din("rbias", [1, 8])
    upT = din("upT", [8, 32, D, P])
    dnT = din("dnT", [8, 8, FFE, P])
    identb = din("identb", [P, P], BF16)
    identf = din("identf", [P, P])
    gpos = din("gpos", [D, CH])
    gdecC = din("gdecC", [D, CH])
    mask3 = din("mask3", [NH1, CH, CH])
    gchunk = din("gchunk", [NH1, 16])

    y_out = nc.dram_tensor("y", [D, T], F32, kind="ExternalOutput").ap()
    dbg_out = (nc.dram_tensor("dbg", [T, D], F32, kind="ExternalOutput").ap()
               if TAP2 else None)

    wq_contrib = nc.dram_tensor("wq_contrib", [WQC], BF16, kind="Internal").ap()
    wq_all = nc.dram_tensor("wq_all", [8, WQC], BF16, kind="Internal").ap()
    st_contrib = nc.dram_tensor("st_contrib", [NH1 * DH * DH], F32,
                                kind="Internal").ap()
    wgq_c = nc.dram_tensor("wgq_c", [FF0, D], BF16, kind="Internal").ap()
    wuq_c = nc.dram_tensor("wuq_c", [FF0, D], BF16, kind="Internal").ap()
    wdq_c = nc.dram_tensor("wdq_c", [D, FF0], BF16, kind="Internal").ap()
    hq_dram = nc.dram_tensor("hq_dram", [T, FF0], BF16, kind="Internal").ap()
    xq_dram = nc.dram_tensor("xq_dram", [NCH, T // P, P, P], BF16,
                             kind="Internal").ap()
    ret_dram = nc.dram_tensor("ret_dram", [D, T], BF16, kind="Internal").ap()
    st_all = nc.dram_tensor("st_all", [2, NH1 * DH * DH], F32,
                            kind="Internal").ap()

    with st:
        def sb(name, shape, dt=F32, stk=st):
            return stk.enter_context(nc.sbuf_tensor(name, shape, dt))

        def psum(name, shape, dt=F32, stk=st):
            return stk.enter_context(nc.psum_tensor(name, shape, dt))

        X = sb("X", [P, NCH * T])
        XN2 = sb("XN2", [P, NCH * T], BF16)
        nv0 = sb("nv0", [P, 4 * NCH])
        nv1 = sb("nv1", [P, 4 * NCH])
        rbt = sb("rbt", [1, 8])
        sgt = sb("sgt", [P, 1])
        idb = sb("idb", [P, P], BF16)
        idf = sb("idf", [P, P])
        ones_st = sb("ones_st", [P, 1], BF16)
        ones_rep = sb("ones_rep", [1, P])
        R = [sb(f"R{i}", [1, 512]) for i in range(6)]
        TMP0 = sb("TMP0", [P, 512], F32)
        rows_stk = contextlib.ExitStack()
        mu_r = sb("mu_r", [1, T], F32, rows_stk)
        r_r = sb("r_r", [1, T], F32, rows_stk)

        S = Sems(nc, st, ["LD", "LDG", "TS", "VS", "SS", "GS", "ES", "OS", "OS2", "CC", "LD2", "LG2", "PH"])
        AV = AutoEng(S, "VS")
        ASS = AutoEng(S, "SS")
        blk = st.enter_context(nc.Block())

        def row_rsqrt(a_row, out_row, t_row, n=512):
            def ap(x):
                return x[:, 0:n] if hasattr(x, "ap") is False or True else x
            a, o, tr = a_row[:, 0:n], out_row[:, 0:n], t_row[:, 0:n]

            @blk.scalar
            def _(s):
                s = ASS.bind(s)
                S.wait(s, "VS")
                i = s.activation(tr, a, Act.Sqrt)

            @blk.vector
            def _(v):
                v = AV.bind(v)
                S.wait(v, "SS")
                v.reciprocal(o, tr)
                v.tensor_tensor(tr, o, o, Alu.mult)
                v.tensor_tensor(tr, tr, a, Alu.mult)
                v.tensor_scalar(tr, tr, -0.5, 1.5, Alu.mult, Alu.add)
                i = v.tensor_tensor(o, o, tr, Alu.mult)

        p0 = contextlib.ExitStack()
        xbfb = [sb(f"xbfb{i}", [P, NCH * 512], BF16, p0) for i in range(2)]
        rho_r = sb("rho_r", [1, T], F32, p0)
        xsqb = [sb(f"xsqb{i}", [P, NCH * 512], BF16, p0) for i in range(2)]
        psums = [psum(f"p0sum{i}", [1, 512], F32, p0) for i in range(2)]
        psumq = [psum(f"p0sq{i}", [1, 512], F32, p0) for i in range(2)]
        prep = [psum(f"p0rep{i}", [P, 512], F32, p0) for i in range(4)]

        @blk.sync
        def _(q):
            for c in range(NCH):
                i = q.dma_start(out=X[:, c * T:(c + 1) * T],
                                in_=x_own[c * P:(c + 1) * P, :])
                S.inc(i, "LD", 16)
            for dst, src in ((nv0, nvec0), (nv1, nvec1), (rbt, rbias),
                             (sgt, sgate), (idb, identb), (idf, identf)):
                i = q.dma_start(out=dst[:], in_=src[:])
                S.inc(i, "LD", 16)
        ld_x = S.n["LD"]

        vs_copy = []
        ss_sq = []
        ts_stat = []
        ts0_base = S.n["TS"]

        def ts_stat_pred(sl):
            return ts0_base + sl + 1

        @blk.vector
        def _(v):
            v = AV.bind(v)
            v.memset(ones_st[:], 1.0)
            i = v.memset(ones_rep[:], 1.0)
            S.wait(v, "LD", ld_x)
            for sl in range(4):
                if sl >= 2:
                    v.wait_ge(S.h["TS"], ts_stat_pred(sl - 2))
                for c in range(NCH):
                    o = c * T + sl * 512
                    i = v.tensor_copy(xbfb[sl % 2][:, c * 512:(c + 1) * 512],
                                      X[:, o:o + 512])
                vs_copy.append(S.n["VS"])

        @blk.scalar
        def _(s):
            s = ASS.bind(s)
            S.wait(s, "LD", ld_x)
            for sl in range(4):
                if sl >= 2:
                    s.wait_ge(S.h["TS"], ts_stat_pred(sl - 2))
                for c in range(NCH):
                    o = c * T + sl * 512
                    i = s.activation(xsqb[sl % 2][:, c * 512:(c + 1) * 512],
                                     X[:, o:o + 512], Act.Square)
                ss_sq.append(S.n["SS"])

        p0_freed = []
        for sl in range(4):
            o = sl * 512

            @blk.tensor
            def _(t):
                t.wait_ge(S.h["VS"], vs_copy[sl])
                t.wait_ge(S.h["SS"], ss_sq[sl])
                if sl >= 2:
                    t.wait_ge(S.h["VS"], p0_freed[sl - 2])
                for c in range(NCH):
                    i = t.matmul(psums[sl % 2][:], ones_st[:],
                                 xbfb[sl % 2][:, c * 512:(c + 1) * 512],
                                 start=(c == 0), stop=(c == NCH - 1))
                    i = t.matmul(psumq[sl % 2][:], ones_st[:],
                                 xsqb[sl % 2][:, c * 512:(c + 1) * 512],
                                 start=(c == 0), stop=(c == NCH - 1))
                S.inc(i, "TS")
                ts_stat.append(S.n["TS"])

            @blk.vector
            def _(v):
                v = AV.bind(v)
                v.wait_ge(S.h["TS"], ts_stat[sl])
                v.tensor_scalar(mu_r[:, o:o + 512], psums[sl % 2][:], 1.0 / D,
                                None, Alu.mult)
                v.tensor_scalar(R[0][:], psumq[sl % 2][:], 1.0 / D,
                                None, Alu.mult)
                p0_freed.append(S.n["VS"])
                v.tensor_tensor(R[1][:], mu_r[:, o:o + 512],
                                mu_r[:, o:o + 512], Alu.mult)
                v.tensor_tensor(R[0][:], R[0][:], R[1][:], Alu.subtract)
                i = v.tensor_scalar(R[1][:], R[0][:], EPS0, None, Alu.add)

            row_rsqrt(R[1], r_r[:, o:o + 512], R[2], n=512)

            @blk.vector
            def _(v):
                v = AV.bind(v)
                rr = r_r[:, o:o + 512]
                v.tensor_tensor(R[2][:], rr, rr, Alu.mult)
                v.tensor_tensor(R[1][:], R[0][:], R[2][:], Alu.mult)
                i = v.tensor_scalar(R[1][:], R[1][:], EPS0, None, Alu.add)

            row_rsqrt(R[1], R[3], R[2], n=512)

            @blk.vector
            def _(v):
                v = AV.bind(v)
                i = v.tensor_tensor(rho_r[:, o:o + 512], R[3][:],
                                    r_r[:, o:o + 512], Alu.mult)
        vs_rho = S.n["VS"]

        ts_rep = []
        xn2_freed = []
        for sl in range(4):
            o = sl * 512
            b = (sl % 2) * 2

            @blk.tensor
            def _(t):
                S.wait(t, "VS")
                if sl >= 2:
                    t.wait_ge(S.h["VS"], xn2_freed[sl - 2])
                i = t.matmul(prep[b][:], ones_rep[:], mu_r[:, o:o + 512],
                             start=True, stop=True)
                i = t.matmul(prep[b + 1][:], ones_rep[:], rho_r[:, o:o + 512],
                             start=True, stop=True)
                S.inc(i, "TS")
                ts_rep.append(S.n["TS"])

            @blk.vector
            def _(v):
                v = AV.bind(v)
                v.wait_ge(S.h["TS"], ts_rep[sl])
                for c in range(NCH):
                    xo = c * T + o
                    v.tensor_tensor(TMP0[:], X[:, xo:xo + 512], prep[b][:],
                                    Alu.subtract)
                    v.tensor_tensor(TMP0[:], TMP0[:], prep[b + 1][:], Alu.mult)
                    i = v.scalar_tensor_tensor(
                        XN2[:, xo:xo + 512], TMP0[:], nv0[:, c:c + 1],
                        nv0[:, NCH + c:NCH + c + 1].broadcast_to((P, 512)),
                        Alu.mult, Alu.add)
                xn2_freed.append(S.n["VS"])

                p0.close()
        g_p0 = {k: S.n[k] for k in ("TS", "VS", "SS")}

        class StopEmission(Exception):
            pass

        open_stacks = []

        if PHASES >= 2:
          p2 = contextlib.ExitStack()
          open_stacks.append(p2)
          def mk_qscratch(stk, qmax, tag):
              return {
                  "QA": sb(f"QA{tag}", [P, qmax], F32, stk),
                  "VVq": sb(f"VVq{tag}", [P, qmax], F32, stk),
                  "MBq": sb(f"MBq{tag}", [P, qmax // 16], F32, stk),
                  "MB2q": sb(f"MB2q{tag}", [P, qmax // 16], F32, stk),
                  "QB1": sb(f"QB1{tag}", [P, qmax], BF16, stk),
                  "QB2": sb(f"QB2{tag}", [P, qmax], BF16, stk),
                  "QB3": sb(f"QB3{tag}", [P, qmax], BF16, stk),
                  "prev": [None],
              }

          def quant_tile(qs, src, dst, W, src_wait, scale_col=None,
                         extra_wait=()):
              QA, VVq, MBq, MB2q, QB1, QB2, QB3, _qt_prev = (
                  qs["QA"], qs["VVq"], qs["MBq"], qs["MB2q"], qs["QB1"],
                  qs["QB2"], qs["QB3"], qs["prev"])
              """fp4 quant-dequant: src f32 [128 x W] -> dst bf16 [128 x W].
              src_wait: (semname, val) marking src ready (for S and V).
              extra_wait: additional (semname, val) waits on V (e.g. dst WAR)."""
              if not hasattr(src, "tensor"):
                  src = src[:]
              if not hasattr(dst, "tensor"):
                  dst = dst[:]
              nb = W // 16

              @blk.scalar
              def _(s):
                  s = ASS.bind(s)
                  if src_wait is not None:
                      s.wait_ge(S.h[src_wait[0]], src_wait[1])
                  if _qt_prev[0] is not None:
                      s.wait_ge(S.h["VS"], _qt_prev[0])
                  s.activation(QA[:, 0:W], src, Act.Abs)
              ss_abs = S.n["SS"]

              @blk.vector
              def _(v):
                  v = AV.bind(v)
                  if src_wait is not None:
                      v.wait_ge(S.h[src_wait[0]], src_wait[1])
                  for nm, val in extra_wait:
                      v.wait_ge(S.h[nm], val)
                  v.wait_ge(S.h["SS"], ss_abs)
                  qa3 = QA[:, 0:W].rearrange("p (b s) -> p b s", s=16)
                  v.tensor_reduce(MBq[:, 0:nb], qa3, AX.X, Alu.max)
                  v.tensor_scalar(MBq[:, 0:nb], MBq[:, 0:nb], 6e-12, None,
                                  Alu.max)
                  v.reciprocal(MB2q[:, 0:nb], MBq[:, 0:nb])
                  v.tensor_scalar(MB2q[:, 0:nb], MB2q[:, 0:nb], 6.0, None,
                                  Alu.mult)
                  bc = MB2q[:, 0:nb].unsqueeze(2).broadcast_to((P, nb, 16))
                  vv3 = VVq[:, 0:W].rearrange("p (b s) -> p b s", s=16)
                  v.tensor_tensor(vv3, qa3, bc, Alu.mult)
                  VV = VVq[:, 0:W]
                  b1, b2, b3 = QB1[:, 0:W], QB2[:, 0:W], QB3[:, 0:W]
                  v.tensor_scalar(b1, VV, 0.25, None, Alu.is_gt)
                  for th in (0.75, 1.25, 1.75):
                      v.tensor_scalar(b2, VV, th, None, Alu.is_gt)
                      v.tensor_tensor(b1, b1, b2, Alu.add)
                  v.tensor_scalar(b2, VV, 2.5, None, Alu.is_gt)
                  v.tensor_scalar(b3, VV, 3.5, None, Alu.is_gt)
                  v.tensor_tensor(b2, b2, b3, Alu.add)
                  v.scalar_tensor_tensor(b3, b1, 0.5, b2, Alu.mult, Alu.add)
                  v.tensor_scalar(b1, VV, 5.0, None, Alu.is_gt)
                  v.scalar_tensor_tensor(b2, b1, 2.0, b3, Alu.mult, Alu.add)
                  if scale_col is None:
                      v.tensor_scalar(MB2q[:, 0:nb], MBq[:, 0:nb], 1.0 / 6.0,
                                      None, Alu.mult)
                  else:
                      v.tensor_scalar(MB2q[:, 0:nb], MBq[:, 0:nb], scale_col,
                                      1.0 / 6.0, Alu.mult, Alu.mult)
                  bc2 = MB2q[:, 0:nb].unsqueeze(2).broadcast_to((P, nb, 16))
                  b23 = QB2[:, 0:W].rearrange("p (b s) -> p b s", s=16)
                  v.tensor_tensor(vv3, b23, bc2, Alu.mult)
                  v.tensor_scalar(b1, src, 0.0, None, Alu.is_ge)
                  v.tensor_tensor(QA[:, 0:W], b1, VV, Alu.mult)
                  v.scalar_tensor_tensor(dst, QA[:, 0:W], 2.0, VV,
                                         Alu.mult, Alu.subtract)
              _qt_prev[0] = S.n["VS"]
              return S.n["VS"]

          qs1 = mk_qscratch(p2, FF0 // 2, "a")

          NW = 6 * 1024 + FF0
          p2a = contextlib.ExitStack()
          open_stacks.append(p2a)
          WSL = sb("WSL", [P, 4864], F32, p2a)
          WQB = sb("WQB", [P, NW], BF16, p2a)

          @blk.sync
          def _(q):
              q.wait_ge(S.h["LD"], S.n["LD"])
              q.wait_ge(S.h["TS"], g_p0["TS"])
              q.wait_ge(S.h["VS"], g_p0["VS"])
              q.wait_ge(S.h["SS"], g_p0["SS"])
              for j in range(3):
                  i = q.dma_start(out=WSL[:, j * 1024:(j + 1) * 1024],
                                  in_=wg_sl[j * P:(j + 1) * P, :])
                  S.inc(i, "LD", 16)
              i = q.dma_start(out=WSL[:, 3 * 1024:4 * 1024],
                              in_=wu_sl[0:P, :])
              S.inc(i, "LD", 16)
          ld_wsl = S.n["LD"]

          wq_marks = []
          for j in range(4):
              wq_marks.append(quant_tile(qs1, WSL[:, j * 1024:(j + 1) * 1024],
                                         WQB[:, j * 1024:(j + 1) * 1024],
                                         1024, ("LD", ld_wsl)))

          @blk.sync
          def _(q):
              q.wait_ge(S.h["LD"], S.n["LD"])
              q.wait_ge(S.h["VS"], wq_marks[-1])
              for j in range(2):
                  i = q.dma_start(out=WSL[:, j * 1024:(j + 1) * 1024],
                                  in_=wu_sl[(j + 1) * P:(j + 2) * P, :])
                  S.inc(i, "LD", 16)
              i = q.dma_start(out=WSL[:, 2 * 1024:2 * 1024 + FF0],
                              in_=wd_sl[:])
              S.inc(i, "LD", 16)
          ld_wsl2 = S.n["LD"]

          for j in range(2):
              wq_marks.append(quant_tile(qs1, WSL[:, j * 1024:(j + 1) * 1024],
                                         WQB[:, (4 + j) * 1024:(5 + j) * 1024],
                                         1024, ("LD", ld_wsl2)))
          for hh in range(2):
              wq_marks.append(quant_tile(
                  qs1,
                  WSL[:, 2 * 1024 + hh * 1408:2 * 1024 + (hh + 1) * 1408],
                  WQB[:, 6 * 1024 + hh * 1408:6 * 1024 + (hh + 1) * 1408],
                  1408, ("LD", ld_wsl2)))

          ldg_contrib = []

          @blk.gpsimd
          def _(g):
              g.wait_ge(S.h["LDG"], S.n["LDG"])
              g.wait_ge(S.h["VS"], wq_marks[-1])
              ofs = 0
              for j in range(7):
                  w = 1024 if j < 6 else FF0
                  i = g.dma_start(
                      out=wq_contrib[ofs * P:(ofs + w) * P].rearrange(
                          "(a b) -> a b", a=P),
                      in_=WQB[:, ofs:ofs + w])
                  S.inc(i, "LDG", 16)
                  ofs += w
              g.wait_ge(S.h["LDG"], S.n["LDG"])
              ldg_contrib.append(S.n["LDG"])
              i = g.collective_compute(
                  "AllGather", Alu.bypass,
                  replica_groups=[[0, 1, 2, 3, 4, 5, 6, 7]],
                  ins=[wq_contrib[:]], outs=[wq_all[:]])
              S.inc(i, "CC", 1)
          p2a.close()
          open_stacks.remove(p2a)

          MUCOL = sb("MUCOL", [P, 16], F32, p2)
          RCOL = sb("RCOL", [P, 16], F32, p2)
          XQF_st = sb("XQF_st", [P, D], BF16, p2)
          p2b = contextlib.ExitStack()
          open_stacks.append(p2b)
          pcolm = psum("pcolm", [P, 16], F32, p2b)
          pcolr = psum("pcolr", [P, 16], F32, p2b)

          @blk.tensor
          def _(t):
              t.wait_ge(S.h["VS"], g_p0["VS"])
              t.wait_ge(S.h["SS"], g_p0["SS"])
              t.wait_ge(S.h["LDG"], ldg_contrib[0])
              for tc in range(16):
                  i = t.transpose(pcolm[:, tc:tc + 1],
                                  mu_r[0:1, tc * P:(tc + 1) * P],
                                  idf[0:1, 0:1])
                  i = t.transpose(pcolr[:, tc:tc + 1],
                                  r_r[0:1, tc * P:(tc + 1) * P],
                                  idf[0:1, 0:1])
              S.inc(i, "TS")
          ts_col = S.n["TS"]

          @blk.vector
          def _(v):
              v = AV.bind(v)
              v.wait_ge(S.h["TS"], ts_col)
              v.tensor_copy(MUCOL[:], pcolm[:])
              v.tensor_copy(RCOL[:], pcolr[:])
          vs_col = S.n["VS"]

          XCt = sb("XCt", [P, D], F32, p2b)
          XQt = sb("XQt", [P, D], BF16, p2b)
          pxt = [psum(f"pxt{i}", [P, P], F32, p2b) for i in range(2)]
          pqx = [psum(f"pqx{i}", [P, P], BF16, p2b) for i in range(2)]

          xtr = {}
          xct_dumped_prev = [0]
          xqst_mark = {}
          gs_b = S.n["GS"]

          def gs_xct(tc, c):
              return gs_b + tc * 16 + c + 1

          def gs_xqf(tc, c):
              return gs_b + tc * 16 + 8 + c + 1

          qm_prev = [None]
          for tc in range(16):
              @blk.tensor
              def _(t):
                  if tc == 0:
                      t.wait_ge(S.h["VS"], vs_col)
                  for c in range(NCH):
                      gidx = tc * NCH + c
                      if gidx >= 2:
                          pi = gidx - 2
                          t.wait_ge(S.h["GS"], gs_xct(pi // NCH, pi % NCH))
                      i = t.transpose(pxt[c % 2][:],
                                      X[:, c * T + tc * P:c * T + (tc + 1) * P],
                                      idf[:])
                      S.inc(i, "TS")
                      xtr[tc * NCH + c] = S.n["TS"]

              @blk.vector
              def _(v):
                  v = AV.bind(v)
                  if TAP2 == "xct" and tc >= 1:
                      v.wait_ge(S.h["LDG"], xct_dumped_prev[0])
                  for c in range(NCH):
                      v.wait_ge(S.h["TS"], xtr[tc * NCH + c])
                      v.tensor_scalar(XCt[:, c * P:(c + 1) * P], pxt[c % 2][:],
                                      MUCOL[:, tc:tc + 1], None, Alu.subtract,
                                      inc="GS")
              xct_done = ("GS", S.n["GS"])

              if TAP2 == "xct":
                  @blk.gpsimd
                  def _(g):
                      g.wait_ge(S.h["LDG"], S.n["LDG"])
                      g.wait_ge(S.h["GS"], xct_done[1])
                      i = g.dma_start(out=dbg_out[tc * P:(tc + 1) * P, :],
                                      in_=XCt[:])
                      S.inc(i, "LDG", 16)
                  xct_dumped_prev[0] = S.n["LDG"]

              ew = ()
              if tc >= 1:
                  ew = (("TS", xtr[200 + (tc - 1) * NCH + NCH - 1]),)
              if TAP2 == "xct":
                  ew = ew + (("LDG", xct_dumped_prev[0]),)
              qm = quant_tile(qs1, XCt, XQt, D, xct_done,
                              scale_col=RCOL[:, tc:tc + 1], extra_wait=ew)

              @blk.tensor
              def _(t):
                  t.wait_ge(S.h["VS"], qm)
                  for c in range(NCH):
                      gidx = tc * NCH + c
                      if gidx >= 2:
                          pi = gidx - 2
                          t.wait_ge(S.h["GS"], gs_xqf(pi // NCH, pi % NCH))
                      i = t.transpose(pqx[c % 2][:],
                                      XQt[:, c * P:(c + 1) * P], idb[:])
                      S.inc(i, "TS")
                      xtr[200 + tc * NCH + c] = S.n["TS"]

              @blk.vector
              def _(v):
                  v = AV.bind(v)
                  if tc >= 1 and "noxqst" not in DBG:
                      v.wait_ge(S.h["LDG"], xqst_mark[tc - 1])
                  for c in range(NCH):
                      v.wait_ge(S.h["TS"], xtr[200 + tc * NCH + c])
                      v.tensor_copy(XQF_st[:, c * P:(c + 1) * P],
                                    pqx[c % 2][:], inc="GS")

              @blk.gpsimd
              def _(g):
                  if "noxqst" in DBG:
                      return
                  g.wait_ge(S.h["LDG"], S.n["LDG"])
                  g.wait_ge(S.h["GS"], gs_xqf(tc, NCH - 1))
                  i = g.dma_start(
                      out=xq_dram[:, tc].rearrange("c p j -> p c j"),
                      in_=XQF_st[:].rearrange("p (c j) -> p c j", j=P))
                  S.inc(i, "LDG", 16)
              xqst_mark[tc] = S.n["LDG"]
          xq_st_done = S.n["LDG"]
          p2b.close()
          open_stacks.remove(p2b)
          p2.close()
          open_stacks.remove(p2)

        g_p0b = {k: S.n[k] for k in ("TS", "VS", "SS")}
        gs_p1v = S.n["GS"]

        try:
          if PHASES >= 1:
            p1 = contextlib.ExitStack()
            open_stacks.append(p1)
            k_ext = sb("k_ext", [65, SEQ], BF16, p1)
            vexE = sb("vexE", [P, 32 * 65], BF16, p1)
            vexO = sb("vexO", [P, 32 * 128], BF16, p1)
            AO = sb("AO", [P, 4 * T], BF16, p1)
            KMX = sb("KMX", [65, 4], F32, p1)
            CT = sb("CT", [65, 512], F32, p1)
            E65 = sb("E65", [P, 65], BF16, p1)
            E65h = sb("E65h", [P, 65], BF16, p1)
            ones64 = sb("ones64", [65, P], F32, p1)
            PM, PR, PS1, PS2, PS3 = R[4], R[5], R[1], R[2], R[3]

            @blk.vector
            def _(v):
                v = AV.bind(v)
                v.wait_ge(S.h["TS"], g_p0b["TS"])
                v.wait_ge(S.h["SS"], g_p0b["SS"])
                v.memset(k_ext[64:65, :], 1.0)
                v.memset(vexE[:], 0.0)
                v.memset(vexO[:], 0.0)
                for kc in range(32):
                    v.memset(vexE[:, kc * 65 + 64:kc * 65 + 65], 1.0)
                    v.memset(vexO[:, kc * 128:kc * 128 + 1], 1.0)
                v.memset(E65[:], 0.0)
                v.memset(E65h[:], 0.0)
                v.memset(E65[0:64, 64:65], 1.0)
                v.memset(E65h[64:128, 64:65], 1.0)
                i = v.memset(ones64[64:65, :], 1.0)

            a1 = contextlib.ExitStack()
            open_stacks.append(a1)
            pbf = sb("pbf", [P, NCH * 512], BF16, a1)
            pxn2 = sb("pxn2", [P, NCH * 512], BF16, a1)
            psq2 = pxn2
            VTb = sb("VTb", [64, 512], BF16, a1)
            ksqb = sb("ksqb", [64, 512], BF16, a1)
            wkv0b = sb("wkv0b", [P, NCH * 128], BF16, a1)

            @blk.gpsimd
            def _(g):
                g.wait_ge(S.h["LDG"], S.n["LDG"])
                g.wait_ge(S.h["TS"], g_p0b["TS"])
                g.wait_ge(S.h["VS"], g_p0b["VS"])
                g.wait_ge(S.h["SS"], g_p0b["SS"])
                for c in range(NCH):
                    i = g.dma_start(out=wkv0b[:, c * 128:(c + 1) * 128],
                                    in_=wkv0[c * P:(c + 1) * P, :])
                    S.inc(i, "LDG", 16)
            ldg_wkv = S.n["LDG"]
            pst = [psum(f"pst{i}", [1, 512], F32, a1) for i in range(2)]
            prp = [psum(f"prp{i}", [P, 512], F32, a1) for i in range(2)]
            pkv = [psum(f"pkv{i}", [P, 512], F32, a1) for i in range(2)]
            pkn = psum("pkn", [65, 512], F32, a1)
            pvt = psum("pvt", [P, 64], BF16, a1)

            kvmm = {}
            knmm = {}
            vtmm = {}
            kev = {}
            vcp = {}
            kmaxv = {}
            pbuffree = {}
            affdone = {}
            sqd = {}

            for sl in range(8):
                own = sl < 4
                if not own:
                    sp = sl - 4

                    @blk.gpsimd
                    def _(g):
                        g.wait_ge(S.h["LDG"], S.n["LDG"])
                        if sl > 4:
                            S.wait(g, "VS", affdone[sl - 1])
                        for c in range(NCH):
                            i = g.dma_start(
                                out=pbf[:, c * 512:(c + 1) * 512],
                                in_=x_part[c * P:(c + 1) * P,
                                           sp * 512:(sp + 1) * 512])
                            S.inc(i, "LDG", 16)
                    ld_pb = S.n["LDG"]
                    vs_pbf = S.n["VS"]

                    @blk.scalar
                    def _(s):
                        s = ASS.bind(s)
                        if sl == 4:
                            s.wait_ge(S.h["TS"], g_p0b["TS"])
                            s.wait_ge(S.h["VS"], g_p0b["VS"])
                        S.wait(s, "LDG", ld_pb)
                        if sl > 4:
                            s.wait_ge(S.h["TS"], kvmm[sl - 1])
                        for c in range(NCH):
                            i = s.activation(psq2[:, c * 512:(c + 1) * 512],
                                             pbf[:, c * 512:(c + 1) * 512],
                                             Act.Square)
                    ss_psq = S.n["SS"]

                    @blk.tensor
                    def _(t):
                        t.wait_ge(S.h["LDG"], ld_pb)
                        t.wait_ge(S.h["SS"], ss_psq)
                        for c in range(NCH):
                            i = t.matmul(pst[0][:], ones_st[:],
                                         pbf[:, c * 512:(c + 1) * 512],
                                         start=(c == 0), stop=(c == NCH - 1))
                            i = t.matmul(pst[1][:], ones_st[:],
                                         psq2[:, c * 512:(c + 1) * 512],
                                         start=(c == 0), stop=(c == NCH - 1))
                        S.inc(i, "TS")
                    ts_pstat = S.n["TS"]

                    @blk.vector
                    def _(v):
                        v = AV.bind(v)
                        v.wait_ge(S.h["TS"], ts_pstat)
                        v.tensor_scalar(PM[:], pst[0][:], 1.0 / D, None, Alu.mult)
                        v.tensor_scalar(PS1[:], pst[1][:], 1.0 / D, None, Alu.mult)
                        v.tensor_tensor(PS2[:], PM[:], PM[:], Alu.mult)
                        v.tensor_tensor(PS1[:], PS1[:], PS2[:], Alu.subtract)
                        i = v.tensor_scalar(PS1[:], PS1[:], EPS0, None, Alu.add)

                    row_rsqrt(PS1, PR, PS2, n=512)

                    @blk.vector
                    def _(v):
                        v = AV.bind(v)
                        v.tensor_scalar(PS1[:], PS1[:], -EPS0, None, Alu.add)
                        v.tensor_tensor(PS2[:], PR[:], PR[:], Alu.mult)
                        v.tensor_tensor(PS1[:], PS1[:], PS2[:], Alu.mult)
                        i = v.tensor_scalar(PS1[:], PS1[:], EPS0, None, Alu.add)

                    row_rsqrt(PS1, PS2, PS3, n=512)

                    @blk.vector
                    def _(v):
                        v = AV.bind(v)
                        i = v.tensor_tensor(PR[:], PR[:], PS2[:], Alu.mult)
                    vs_prho = S.n["VS"]

                    @blk.tensor
                    def _(t):
                        t.wait_ge(S.h["VS"], vs_prho)
                        i = t.matmul(prp[0][:], ones_rep[:], PM[:],
                                     start=True, stop=True)
                        i = t.matmul(prp[1][:], ones_rep[:], PR[:],
                                     start=True, stop=True)
                        S.inc(i, "TS")
                    ts_prep = S.n["TS"]

                    @blk.vector
                    def _(v):
                        v = AV.bind(v)
                        v.wait_ge(S.h["TS"], ts_prep)
                        v.wait_ge(S.h["TS"], ts_pstat)
                        for c in range(NCH):
                            o = c * 512
                            v.tensor_tensor(TMP0[:], pbf[:, o:o + 512],
                                            prp[0][:], Alu.subtract)
                            v.tensor_tensor(TMP0[:], TMP0[:], prp[1][:],
                                            Alu.mult)
                            i = v.scalar_tensor_tensor(
                                pxn2[:, o:o + 512], TMP0[:], nv0[:, c:c + 1],
                                nv0[:, NCH + c:NCH + c + 1].broadcast_to(
                                    (P, 512)), Alu.mult, Alu.add)
                    affdone[sl] = S.n["VS"]

                @blk.tensor
                def _(t):
                    if own:
                        t.wait_ge(S.h["VS"], S.n["VS"])
                        t.wait_ge(S.h["LDG"], ldg_wkv)
                    else:
                        t.wait_ge(S.h["VS"], affdone[sl])
                    if sl >= 2:
                        t.wait_ge(S.h["VS"], kev[sl - 2])
                        t.wait_ge(S.h["SS"], sqd[sl - 2])
                    for c in range(NCH):
                        if own:
                            mv = XN2[:, c * T + sl * 512:c * T + sl * 512 + 512]
                        else:
                            mv = pxn2[:, c * 512:(c + 1) * 512]
                        i = t.matmul(pkv[sl % 2][:],
                                     wkv0b[:, c * 128:(c + 1) * 128], mv,
                                     start=(c == 0), stop=(c == NCH - 1))
                    S.inc(i, "TS")
                kvmm[sl] = S.n["TS"]

                @blk.vector
                def _(v):
                    v = AV.bind(v)
                    v.wait_ge(S.h["TS"], kvmm[sl])
                    if sl >= 1:
                        v.wait_ge(S.h["TS"], vtmm[(sl - 1, 3)])
                    i = v.tensor_copy(k_ext[0:64, sl * 512:(sl + 1) * 512],
                                      pkv[sl % 2][0:64, :])
                    i = v.tensor_copy(VTb[:], pkv[sl % 2][64:128, :])
                kev[sl] = S.n["VS"]

                @blk.scalar
                def _(s):
                    s = ASS.bind(s)
                    s.wait_ge(S.h["VS"], kev[sl])
                    if sl >= 1:
                        s.wait_ge(S.h["TS"], knmm[sl - 1])
                    i = s.activation(ksqb[:],
                                     k_ext[0:64, sl * 512:(sl + 1) * 512],
                                     Act.Square)
                sqd[sl] = S.n["SS"]


                @blk.tensor
                def _(t):
                    t.wait_ge(S.h["SS"], sqd[sl])
                    if sl >= 1:
                        t.wait_ge(S.h["VS"], kmaxv[sl - 1])
                    i = t.matmul(pkn[:], E65[0:64, :], ksqb[:],
                                 start=True, stop=True)
                    S.inc(i, "TS")
                    knmm[sl] = S.n["TS"]
                    t.wait_ge(S.h["VS"], kev[sl])
                    for j in range(4):
                        if (sl, j) != (0, 0):
                            t.wait_ge(S.h["GS"], gs_p1v + 4 * sl + j)
                        i = t.transpose(pvt[:], VTb[:, j * 128:(j + 1) * 128],
                                        idb[0:64, 0:64])
                        S.inc(i, "TS")
                        vtmm[(sl, j)] = S.n["TS"]

                @blk.vector
                def _(v):
                    v = AV.bind(v)
                    v.wait_ge(S.h["TS"], knmm[sl])
                    v.tensor_reduce(KMX[64:65, 1:2], pkn[64:65, :], AX.X,
                                    Alu.max)
                    if sl == 0:
                        i = v.tensor_copy(KMX[64:65, 0:1], KMX[64:65, 1:2])
                    else:
                        i = v.tensor_tensor(KMX[64:65, 0:1], KMX[64:65, 0:1],
                                            KMX[64:65, 1:2], Alu.max)
                    kmaxv[sl] = S.n["VS"]
                    for j in range(4):
                        kc = sl * 4 + j
                        v.wait_ge(S.h["TS"], vtmm[(sl, j)])
                        v.tensor_copy(vexE[:, kc * 65:kc * 65 + 64], pvt[:])
                        i = v.tensor_copy(
                            vexO[:, kc * 128 + 64:(kc + 1) * 128], pvt[:],
                            inc="GS")

            @blk.scalar
            def _(s):
                s = ASS.bind(s)
                S.wait(s, "VS")
                i = s.activation(KMX[64:65, 2:3], KMX[64:65, 0:1], Act.Sqrt)
            ss_kmax = S.n["SS"]
            a1.close()
            open_stacks.remove(a1)
            g_a1 = {k: S.n[k] for k in ("TS", "VS", "SS")}

            if P1STAGE < 1.3:
                raise StopEmission
            a2 = contextlib.ExitStack()
            open_stacks.append(a2)
            wq0b = sb("wq0b", [P, NCH * 512], BF16, a2)
            qex = [sb(f"qex{i}", [65, NH0 * 512], BF16, a2) for i in range(2)]

            @blk.gpsimd
            def _(g):
                g.wait_ge(S.h["LDG"], S.n["LDG"])
                g.wait_ge(S.h["TS"], g_a1["TS"])
                g.wait_ge(S.h["VS"], g_a1["VS"])
                g.wait_ge(S.h["SS"], g_a1["SS"])
                for c in range(NCH):
                    i = g.dma_start(out=wq0b[:, c * 512:(c + 1) * 512],
                                    in_=wq0[c * P:(c + 1) * P, :])
                    S.inc(i, "LDG", 16)
            ldg_wq = S.n["LDG"]
            qsqb = sb("qsqb", [P, 512], BF16, a2)
            expb = [sb(f"expb{i}", [P, 512], BF16, a2) for i in range(2)]
            REPT = TMP0
            AR0 = PS3
            AR64 = sb("AR64", [65, 512], F32, a2)
            pq = [psum(f"pq{i}", [P, 512], F32, a2) for i in range(2)]
            pqn = psum("pqn", [65, 512], F32, a2)
            psim = [psum(f"psim{i}", [P, 512], F32, a2) for i in range(2)]
            pav = [psum(f"pav{i}", [P, 512], F32, a2) for i in range(2)]
            prep1 = psum("prep1", [P, 512], F32, a2)

            qmmv = {}
            qevv = {}
            qsqv = {}
            qnv = {}
            crowv = {}
            ctsq = {}
            simv = {}
            expv = {}
            avv = {}
            repv = {}
            arv = {}
            reptv = {}
            nrmv = {}

            def sh_prev(slab, h, k):
                idx = slab * NH0 + h - k
                if idx < 0:
                    return None
                return (idx // NH0, idx % NH0)

            os_q0 = S.n["OS"]
            os_q0b = S.n["OS2"]
            for slab in (() if "noqmm" in DBG else range(4)):
                for oc in range(4):
                    h_lo, h_hi = 2 * oc, 2 * oc + 1

                    @blk.tensor
                    def _(t):
                        if (slab, oc) == (0, 0):
                            t.wait_ge(S.h["LDG"], ldg_wq)
                            t.wait_ge(S.h["VS"], g_a1["VS"])
                            t.wait_ge(S.h["SS"], g_a1["SS"])
                        if oc == 0 and slab >= 2 and P1STAGE >= 2:
                            t.wait_ge(S.h["TS"], simv[(slab - 2, NH0 - 1, 31)])
                        if (slab, oc) not in ((0, 0), (0, 1)):
                            po = slab * 4 + oc - 2
                            t.wait_ge(S.h["OS"], os_q0 + po + 1)
                        for c in range(NCH):
                            i = t.matmul(
                                pq[oc % 2][:],
                                wq0b[:, c * 512 + oc * 128:
                                     c * 512 + (oc + 1) * 128],
                                XN2[:, c * T + slab * 512:
                                    c * T + (slab + 1) * 512],
                                start=(c == 0), stop=(c == NCH - 1))
                        S.inc(i, "TS")
                        qmmv[(slab, oc)] = S.n["TS"]

                    @blk.vector
                    def _(v):
                        if "noqev" in DBG:
                            return
                        v = AV.bind(v)
                        if (slab, oc) == (0, 0):
                            v.wait_ge(S.h["SS"], g_a1["SS"])
                        v.wait_ge(S.h["TS"], qmmv[(slab, oc)])
                        v.tensor_scalar(
                            qex[slab % 2][0:64, h_lo * 512:(h_lo + 1) * 512],
                            pq[oc % 2][0:64, :], 0.125, None, Alu.mult)
                        i = v.tensor_scalar(
                            qex[slab % 2][0:64, h_hi * 512:(h_hi + 1) * 512],
                            pq[oc % 2][64:128, :], 0.125, None, Alu.mult,
                            inc="OS")

                    for par, h in (((0, h_lo), (1, h_hi)) if P1STAGE >= 1.6 else ()):
                        @blk.scalar
                        def _(s):
                            s = ASS.bind(s)
                            s.wait_ge(S.h["OS"], os_q0 + slab * 4 + oc + 1)
                            pv = sh_prev(slab, h, 1)
                            if pv is not None:
                                s.wait_ge(S.h["TS"], qnv[pv])
                            i = s.activation(qsqb[0:64, :],
                                             qex[slab % 2][0:64,
                                                           h * 512:(h + 1) * 512],
                                             Act.Square, inc="OS2")

                        @blk.tensor
                        def _(t):
                            t.wait_ge(S.h["OS2"],
                                      os_q0b + (slab * NH0 + h) + 1)
                            pv = sh_prev(slab, h, 1)
                            if pv is not None:
                                t.wait_ge(S.h["SS"], ctsq[pv])
                            i = t.matmul(pqn[:], E65[0:64, :], qsqb[0:64, :],
                                         start=True, stop=True)
                            S.inc(i, "TS")
                            qnv[(slab, h)] = S.n["TS"]

                        @blk.scalar
                        def _(s):
                            s = ASS.bind(s)
                            s.wait_ge(S.h["TS"], qnv[(slab, h)])
                            s.wait_ge(S.h["SS"], ss_kmax)
                            pv = sh_prev(slab, h, 1)
                            if pv is not None:
                                s.wait_ge(S.h["VS"], crowv[pv])
                            i = s.activation(CT[64:65, :], pqn[64:65, :],
                                             Act.Sqrt)
                            ctsq[(slab, h)] = S.n["SS"]

                        @blk.vector
                        def _(v):
                            v = AV.bind(v)
                            v.wait_ge(S.h["SS"], ctsq[(slab, h)])
                            v.tensor_scalar(CT[64:65, :], CT[64:65, :],
                                            KMX[64:65, 2:3], -30.0,
                                            Alu.mult, Alu.add)
                            i = v.tensor_scalar(
                                qex[slab % 2][64:65, h * 512:(h + 1) * 512],
                                CT[64:65, :], 0.0, -1.0, Alu.max, Alu.mult)
                            crowv[(slab, h)] = S.n["VS"]

                for h in (range(NH0) if P1STAGE >= 2 else ()):
                    par = h % 2
                    vext = vexE if par == 0 else vexO
                    vw = 65 if par == 0 else 128
                    bav = (slab * NH0 + h) % 2

                    @blk.tensor
                    def _(t):
                        t.wait_ge(S.h["VS"], crowv[(slab, h)])
                        t.wait_ge(S.h["OS"],
                                  os_q0 + slab * 4 + h // 2 + 1)
                        pv = sh_prev(slab, h, 2)
                        if pv is not None:
                            t.wait_ge(S.h["VS"], nrmv[pv])
                        for kc in range(32):
                            eidx = (slab * NH0 + h) * 32 + kc
                            if eidx >= 2:
                                t.wait_ge(S.h["ES"], eidx - 1)
                            i = t.matmul(
                                psim[kc % 2][:],
                                k_ext[0:65, kc * 128:(kc + 1) * 128],
                                qex[slab % 2][0:65, h * 512:(h + 1) * 512],
                                start=True, stop=True)
                            S.inc(i, "TS")
                            simv[(slab, h, kc)] = S.n["TS"]
                            if kc >= 1:
                                t.wait_ge(S.h["ES"],
                                          (slab * NH0 + h) * 32 + kc)
                                i = t.matmul(pav[bav][0:vw, :],
                                             vext[:, (kc - 1) * vw:kc * vw],
                                             expb[(kc - 1) % 2][:],
                                             start=(kc - 1 == 0), stop=False)
                                S.inc(i, "TS")
                                avv[(slab, h, kc - 1)] = S.n["TS"]
                        t.wait_ge(S.h["ES"], (slab * NH0 + h) * 32 + 32)
                        i = t.matmul(pav[bav][0:vw, :], vext[:, 31 * vw:32 * vw],
                                     expb[31 % 2][:], start=False, stop=True)
                        S.inc(i, "TS")
                        avv[(slab, h, 31)] = S.n["TS"]

                    @blk.scalar
                    def _(s):
                        s = ASS.bind(s)
                        for kc in range(32):
                            s.wait_ge(S.h["TS"], simv[(slab, h, kc)])
                            if kc >= 2:
                                s.wait_ge(S.h["TS"], avv[(slab, h, kc - 2)])
                            elif sh_prev(slab, h, 1) is not None:
                                pp = sh_prev(slab, h, 1)
                                s.wait_ge(S.h["TS"], avv[(pp[0], pp[1],
                                                          30 + kc)])
                            i = s.activation(expb[kc % 2][:], psim[kc % 2][:],
                                             Act.Exp, inc="ES")

                    sumrow = 64 if par == 0 else 0
                    arr = AR64[64:65, :] if par == 0 else AR0[:]

                    @blk.vector
                    def _(v):
                        v = AV.bind(v)
                        v.wait_ge(S.h["TS"], avv[(slab, h, 31)])
                        i = v.reciprocal(arr, pav[bav][sumrow:sumrow + 1, :])
                        arv[(slab, h)] = S.n["VS"]

                    @blk.tensor
                    def _(t):
                        t.wait_ge(S.h["VS"], arv[(slab, h)])
                        pv = sh_prev(slab, h, 1)
                        if pv is not None:
                            t.wait_ge(S.h["VS"], reptv[pv])
                        st_ = ones64[64:65, :] if par == 0 else ones_rep[:]
                        i = t.matmul(prep1[:], st_, arr, start=True, stop=True)
                        S.inc(i, "TS")
                        repv[(slab, h)] = S.n["TS"]

                    @blk.vector
                    def _(v):
                        v = AV.bind(v)
                        v.wait_ge(S.h["TS"], repv[(slab, h)])
                        i = v.tensor_copy(REPT[:], prep1[:])
                        reptv[(slab, h)] = S.n["VS"]
                        rows = slice(0, 64) if par == 0 else slice(64, 128)
                        dst = AO[64 * par:64 * par + 64,
                                 (h // 2) * T + slab * 512:
                                 (h // 2) * T + (slab + 1) * 512]
                        i = v.tensor_tensor(dst, pav[bav][rows, :],
                                            REPT[rows, :], Alu.mult)
                        nrmv[(slab, h)] = S.n["VS"]

            a2.close()
            open_stacks.remove(a2)
            g_a2 = {k: S.n[k] for k in ("TS", "VS", "SS")}

            if P1STAGE < 3:
                raise StopEmission
            a4 = contextlib.ExitStack()
            open_stacks.append(a4)
            wo0b = sb("wo0b", [P, 4 * D], BF16, a4)
            y0sl = sb("y0sl", [P, NCH * 512], BF16, a4)

            @blk.gpsimd
            def _(g):
                g.wait_ge(S.h["LDG"], S.n["LDG"])
                g.wait_ge(S.h["TS"], g_a2["TS"])
                g.wait_ge(S.h["VS"], g_a2["VS"])
                g.wait_ge(S.h["SS"], g_a2["SS"])
                for ic in range(4):
                    i = g.dma_start(out=wo0b[:, ic * D:(ic + 1) * D],
                                    in_=wo0[ic * P:(ic + 1) * P, :])
                    S.inc(i, "LDG", 16)
            ldg_wo = S.n["LDG"]
            y0sq = sb("y0sq", [P, NCH * 512], BF16, a4)
            po = [psum(f"po{i}", [P, 512], F32, a4) for i in range(2)]
            pst4 = [psum(f"pst4{i}", [1, 512], F32, a4) for i in range(2)]
            prp4 = [psum(f"prp4{i}", [P, 512], F32, a4) for i in range(2)]

            oev = {}
            osq = {}
            omm = {}
            ost = {}
            orow = {}
            orep = {}
            oadd = {}

            os_y0 = S.n["OS"]
            os_y0b = S.n["OS2"]
            for slab in range(4):
                for oc in range(NCH):
                    @blk.tensor
                    def _(t):
                        if (slab, oc) == (0, 0):
                            t.wait_ge(S.h["LDG"], ldg_wo)
                            t.wait_ge(S.h["VS"], g_a2["VS"])
                            t.wait_ge(S.h["SS"], g_a2["SS"])
                        if (slab, oc) not in ((0, 0), (0, 1)):
                            pidx = slab * NCH + oc - 2
                            t.wait_ge(S.h["OS"], os_y0 + pidx + 1)
                            t.wait_ge(S.h["OS2"], os_y0b + pidx + 1)
                        for ic in range(4):
                            stat = wo0b[:, ic * D + oc * P:
                                        ic * D + (oc + 1) * P]
                            mov = AO[:, ic * T + slab * 512:
                                     ic * T + (slab + 1) * 512]
                            i = t.matmul(po[oc % 2][:], stat, mov,
                                         start=(ic == 0), stop=(ic == 3))
                        S.inc(i, "TS")
                        omm[(slab, oc)] = S.n["TS"]

                    @blk.vector
                    def _(v):
                        v = AV.bind(v)
                        if (slab, oc) == (0, 0):
                            v.wait_ge(S.h["SS"], g_a2["SS"])
                        v.wait_ge(S.h["TS"], omm[(slab, oc)])
                        i = v.tensor_copy(y0sl[:, oc * 512:(oc + 1) * 512],
                                          po[oc % 2][:], inc="OS")

                    @blk.scalar
                    def _(s):
                        s = ASS.bind(s)
                        if (slab, oc) == (0, 0):
                            s.wait_ge(S.h["VS"], g_a2["VS"])
                        s.wait_ge(S.h["OS"], os_y0 + slab * NCH + oc + 1)
                        i = s.activation(y0sq[:, oc * 512:(oc + 1) * 512],
                                         y0sl[:, oc * 512:(oc + 1) * 512],
                                         Act.Square, inc="OS2")

                @blk.tensor
                def _(t):
                    t.wait_ge(S.h["OS"], os_y0 + slab * NCH + NCH)
                    t.wait_ge(S.h["OS2"], os_y0b + slab * NCH + NCH)
                    if slab >= 1:
                        t.wait_ge(S.h["VS"], orow[slab - 1])
                    for c in range(NCH):
                        i = t.matmul(pst4[0][:], ones_st[:],
                                     y0sl[:, c * 512:(c + 1) * 512],
                                     start=(c == 0), stop=(c == NCH - 1))
                        i = t.matmul(pst4[1][:], ones_st[:],
                                     y0sq[:, c * 512:(c + 1) * 512],
                                     start=(c == 0), stop=(c == NCH - 1))
                    S.inc(i, "TS")
                    ost[slab] = S.n["TS"]

                @blk.vector
                def _(v):
                    v = AV.bind(v)
                    v.wait_ge(S.h["TS"], ost[slab])
                    v.tensor_scalar(PM[:], pst4[0][:], 1.0 / D, None, Alu.mult)
                    v.tensor_scalar(PS1[:], pst4[1][:], 1.0 / D, None, Alu.mult)
                    v.tensor_tensor(PS2[:], PM[:], PM[:], Alu.mult)
                    v.tensor_tensor(PS1[:], PS1[:], PS2[:], Alu.subtract)
                    i = v.tensor_scalar(PS1[:], PS1[:], EPS0, None, Alu.add)

                row_rsqrt(PS1, PR, PS2, n=512)

                orow[slab] = S.n["VS"]

                @blk.tensor
                def _(t):
                    t.wait_ge(S.h["VS"], orow[slab])
                    if slab >= 1:
                        t.wait_ge(S.h["VS"], oadd[slab - 1])
                    i = t.matmul(prp4[0][:], ones_rep[:], PM[:],
                                 start=True, stop=True)
                    i = t.matmul(prp4[1][:], ones_rep[:], PR[:],
                                 start=True, stop=True)
                    S.inc(i, "TS")
                    orep[slab] = S.n["TS"]

                @blk.vector
                def _(v):
                    v = AV.bind(v)
                    v.wait_ge(S.h["TS"], orep[slab])
                    for c in range(NCH):
                        xo = c * T + slab * 512
                        yo = c * 512
                        v.tensor_tensor(TMP0[:], y0sl[:, yo:yo + 512],
                                        prp4[0][:], Alu.subtract)
                        v.tensor_tensor(TMP0[:], TMP0[:], prp4[1][:], Alu.mult)
                        v.scalar_tensor_tensor(X[:, xo:xo + 512], TMP0[:],
                                               nv0[:, 2 * NCH + c:2 * NCH + c + 1],
                                               X[:, xo:xo + 512],
                                               Alu.mult, Alu.add)
                        i = v.tensor_scalar(X[:, xo:xo + 512], X[:, xo:xo + 512],
                                            nv0[:, 3 * NCH + c:3 * NCH + c + 1],
                                            None, Alu.add)
                    oadd[slab] = S.n["VS"]

            a4.close()
            open_stacks.remove(a4)
            p1.close()
            open_stacks.remove(p1)

          if PHASES < 2 or P2STAGE < 1:
              raise StopEmission
          g_p1 = {k: S.n[k] for k in ("TS", "VS", "SS")}

          @blk.gpsimd
          def _(g):
              g.wait_ge(S.h["LDG"], S.n["LDG"])
              g.wait_ge(S.h["CC"], S.n["CC"])
              for tt in range(22):
                  r, sl_ = tt // 3, tt % 3
                  i = g.dma_start(out=wgq_c[tt * P:(tt + 1) * P, :],
                                  in_=wq_all[r, sl_ * P * D:(sl_ + 1) * P * D
                                             ].rearrange("(a b) -> a b", a=P))
                  S.inc(i, "LDG", 16)
                  i = g.dma_start(
                      out=wuq_c[tt * P:(tt + 1) * P, :],
                      in_=wq_all[r, (3 + sl_) * P * D:(4 + sl_) * P * D
                                 ].rearrange("(a b) -> a b", a=P))
                  S.inc(i, "LDG", 16)
              for r in range(8):
                  i = g.dma_start(
                      out=wdq_c[r * P:(r + 1) * P, :],
                      in_=wq_all[r, 6 * P * D:].rearrange("(a b) -> a b", a=P))
                  S.inc(i, "LDG", 16)
              g.wait_ge(S.h["LDG"], S.n["LDG"])
              g.wait_ge(S.h["TS"], g_p1["TS"])
              g.wait_ge(S.h["VS"], g_p1["VS"])
              g.wait_ge(S.h["SS"], g_p1["SS"])
              for c in range(NCH):
                  i = g.dma_start(
                      out=XN2[:, c * T:(c + 1) * T].rearrange(
                          "p (tc j) -> p tc j", j=P),
                      in_=xq_dram[c].rearrange("tc p j -> p tc j"))
                  S.inc(i, "LDG", 16)
          ldg_wqc = S.n["LDG"] - 16 * NCH
          xq_done_ldg = S.n["LDG"]
          xqf = XN2

          p2d = contextlib.ExitStack()
          open_stacks.append(p2d)
          WG_S = [sb(f"WG_S{i}", [P, NCH * 512], BF16, p2d) for i in range(2)]
          WU_S = [sb(f"WU_S{i}", [P, NCH * 512], BF16, p2d) for i in range(2)]
          SGb = sb("SGb", [P, 512], BF16, p2d)
          HT = sb("HT", [P, 512], F32, p2d)
          HQ = sb("HQ", [P, 512], BF16, p2d)
          pg = [psum(f"pg{i}", [P, 512], F32, p2d) for i in range(2)]
          pu = [psum(f"pu{i}", [P, 512], F32, p2d) for i in range(2)]
          qs2 = mk_qscratch(p2d, 512, "h")

          FSW = [512] * 5 + [256]
          ld_strip = {}
          gm = {}
          um = {}
          hq_mark = {}
          hst = {}
          sil = {}

          for fs in range(6):
              W = FSW[fs]
              semn = "LD" if fs % 2 == 0 else "LD2"

              @blk.sync
              def _(q):
                  q.wait_ge(S.h[semn], S.n[semn])
                  if fs < 2:
                      q.wait_ge(S.h["LDG"], ldg_wqc)
                      q.wait_ge(S.h["TS"], g_p1["TS"])
                      q.wait_ge(S.h["VS"], g_p1["VS"])
                      q.wait_ge(S.h["SS"], g_p1["SS"])
                  else:
                      q.wait_ge(S.h["TS"], um[(fs - 2, 15)])
                  for dc in range(NCH):
                      i = q.dma_start(
                          out=WG_S[fs % 2][:, dc * 512:dc * 512 + W],
                          in_=wgq_c[fs * 512:fs * 512 + W,
                                    dc * P:(dc + 1) * P],
                          transpose=True)
                      S.inc(i, semn, 16)
                      i = q.dma_start(
                          out=WU_S[fs % 2][:, dc * 512:dc * 512 + W],
                          in_=wuq_c[fs * 512:fs * 512 + W,
                                    dc * P:(dc + 1) * P],
                          transpose=True)
                      S.inc(i, semn, 16)
              ld_strip[fs] = (semn, S.n[semn])

              for tci in range(16):
                  @blk.tensor
                  def _(t):
                      if tci == 0:
                          t.wait_ge(S.h[ld_strip[fs][0]], ld_strip[fs][1])
                          if fs == 0:
                              t.wait_ge(S.h["LDG"], xq_done_ldg)
                      if (fs, tci) > (0, 1):
                          pi = fs * 16 + tci - 2
                          t.wait_ge(S.h["SS"], sil[(pi // 16, pi % 16)])
                          t.wait_ge(S.h["VS"], hq_mark[(pi // 16, pi % 16)])
                      for dc in range(NCH):
                          i = t.matmul(pg[tci % 2][:, 0:W],
                                       xqf[:, dc * T + tci * P:
                                           dc * T + (tci + 1) * P],
                                       WG_S[fs % 2][:, dc * 512:dc * 512 + W],
                                       start=(dc == 0), stop=(dc == NCH - 1))
                      S.inc(i, "TS")
                      gm[(fs, tci)] = S.n["TS"]
                      for dc in range(NCH):
                          i = t.matmul(pu[tci % 2][:, 0:W],
                                       xqf[:, dc * T + tci * P:
                                           dc * T + (tci + 1) * P],
                                       WU_S[fs % 2][:, dc * 512:dc * 512 + W],
                                       start=(dc == 0), stop=(dc == NCH - 1))
                      S.inc(i, "TS")
                      um[(fs, tci)] = S.n["TS"]

                  @blk.scalar
                  def _(s):
                      s = ASS.bind(s)
                      s.wait_ge(S.h["TS"], gm[(fs, tci)])
                      s.activation(SGb[:, 0:W], pg[tci % 2][:, 0:W], Act.Silu)
                  sil[(fs, tci)] = S.n["SS"]

                  @blk.vector
                  def _(v):
                      v = AV.bind(v)
                      v.wait_ge(S.h["SS"], sil[(fs, tci)])
                      v.wait_ge(S.h["TS"], um[(fs, tci)])
                      v.tensor_tensor(HT[:, 0:W], SGb[:, 0:W],
                                      pu[tci % 2][:, 0:W], Alu.mult)
                  ht_mark = ("VS", S.n["VS"])

                  gi = fs * 16 + tci
                  ew = ()
                  if gi >= 1:
                      pi = gi - 1
                      ew = (("LDG", hst[(pi // 16, pi % 16)]),)
                  hq_mark[(fs, tci)] = quant_tile(qs2, HT[:, 0:W], HQ[:, 0:W], W,
                                                  ht_mark, extra_wait=ew)

                  @blk.gpsimd
                  def _(g):
                      g.wait_ge(S.h["LDG"], S.n["LDG"])
                      g.wait_ge(S.h["VS"], hq_mark[(fs, tci)])
                      i = g.dma_start(
                          out=hq_dram[tci * P:(tci + 1) * P,
                                      fs * 512:fs * 512 + W],
                          in_=HQ[:, 0:W])
                      S.inc(i, "LDG", 16)
                  hst[(fs, tci)] = S.n["LDG"]

          ldg_hq = S.n["LDG"]
          p2d.close()
          open_stacks.remove(p2d)

          if TAP2 in ("hq", "wgq"):
              @blk.gpsimd
              def _(g):
                  g.wait_ge(S.h["LDG"], S.n["LDG"])
                  if TAP2 == "hq":
                      i = g.dma_start(out=dbg_out[:], in_=hq_dram[:, 0:D])
                  else:
                      i = g.dma_start(out=dbg_out[:],
                                      in_=wgq_c[0:T, 0:D])
                  S.inc(i, "LDG", 16)

          if P2STAGE < 2:
              raise StopEmission

          p2e = contextlib.ExitStack()
          open_stacks.append(p2e)
          WDT = sb("WDT", [P, 22 * D], BF16, p2e)
          HQT = [sb(f"HQT{i}", [P, 22 * 256], BF16, p2e) for i in range(2)]
          pd = [psum(f"pd{i}", [P, 256], F32, p2e) for i in range(2)]

          @blk.sync
          def _(q):
              q.wait_ge(S.h["LD"], S.n["LD"])
              q.wait_ge(S.h["LD2"], S.n["LD2"])
              q.wait_ge(S.h["TS"], S.n["TS"])
              q.wait_ge(S.h["VS"], S.n["VS"])
              q.wait_ge(S.h["SS"], S.n["SS"])
              q.wait_ge(S.h["LDG"], S.n["LDG"])
              for ffc in range(22):
                  i = q.dma_start(out=WDT[:, ffc * D:(ffc + 1) * D],
                                  in_=wdq_c[:, ffc * P:(ffc + 1) * P],
                                  transpose=True)
                  S.inc(i, "LD", 16)
          ld_wdt = S.n["LD"]

          hqt_ld = {}
          dmm = {}
          dadd = {}
          for tsh in range(8):
              semn = "LD" if tsh % 2 == 0 else "LD2"

              @blk.sync
              def _(q):
                  q.wait_ge(S.h[semn], S.n[semn])
                  if tsh == 0:
                      q.wait_ge(S.h["LDG"], ldg_hq)
                  if tsh >= 2:
                      q.wait_ge(S.h["TS"], dmm[(tsh - 2, NCH - 1)])
                  for ffc in range(22):
                      i = q.dma_start(
                          out=HQT[tsh % 2][:, ffc * 256:(ffc + 1) * 256],
                          in_=hq_dram[tsh * 256:(tsh + 1) * 256,
                                      ffc * P:(ffc + 1) * P],
                          transpose=True)
                      S.inc(i, semn, 16)
              hqt_ld[tsh] = (semn, S.n[semn])

              for dc in range(NCH):
                  @blk.tensor
                  def _(t):
                      if dc == 0:
                          t.wait_ge(S.h[hqt_ld[tsh][0]], hqt_ld[tsh][1])
                          if tsh == 0:
                              t.wait_ge(S.h["LD"], ld_wdt)
                      if (tsh, dc) > (0, 1):
                          pi = tsh * NCH + dc - 2
                          t.wait_ge(S.h["VS"], dadd[(pi // NCH, pi % NCH)])
                      for ffc in range(22):
                          i = t.matmul(pd[dc % 2][:],
                                       WDT[:, ffc * D + dc * P:
                                           ffc * D + (dc + 1) * P],
                                       HQT[tsh % 2][:,
                                                    ffc * 256:(ffc + 1) * 256],
                                       start=(ffc == 0), stop=(ffc == 21))
                      S.inc(i, "TS")
                      dmm[(tsh, dc)] = S.n["TS"]

                  @blk.vector
                  def _(v):
                      v = AV.bind(v)
                      v.wait_ge(S.h["TS"], dmm[(tsh, dc)])
                      xo = dc * T + tsh * 256
                      v.tensor_tensor(X[:, xo:xo + 256], X[:, xo:xo + 256],
                                      pd[dc % 2][:], Alu.add)
                      dadd[(tsh, dc)] = S.n["VS"]

          p2e.close()
          open_stacks.remove(p2e)


          g_p2 = {k: S.n[k] for k in ("TS", "VS", "SS")}
          p3 = contextlib.ExitStack()
          open_stacks.append(p3)
          x3b = [sb(f"x3b{i}", [P, NCH * 512], BF16, p3) for i in range(2)]
          x3q = [sb(f"x3q{i}", [P, NCH * 512], BF16, p3) for i in range(2)]
          p3s = [psum(f"p3s{i}", [1, 512], F32, p3) for i in range(2)]
          p3q = [psum(f"p3q{i}", [1, 512], F32, p3) for i in range(2)]
          p3r = [psum(f"p3r{i}", [P, 512], F32, p3) for i in range(4)]

          vc3 = []
          sq3 = []
          st3 = []
          fr3 = []
          xn1m = []

          for sl in range(4):
              @blk.vector
              def _(v):
                  v = AV.bind(v)
                  if sl == 0:
                      v.wait_ge(S.h["TS"], g_p2["TS"])
                      v.wait_ge(S.h["SS"], g_p2["SS"])
                      v.wait_ge(S.h["LDG"], S.n["LDG"])
                  if sl >= 2:
                      v.wait_ge(S.h["TS"], st3[sl - 2])
                  for c in range(NCH):
                      o = c * T + sl * 512
                      i = v.tensor_copy(x3b[sl % 2][:, c * 512:(c + 1) * 512],
                                        X[:, o:o + 512])
                  vc3.append(S.n["VS"])

              @blk.scalar
              def _(s):
                  s = ASS.bind(s)
                  if sl == 0:
                      s.wait_ge(S.h["TS"], g_p2["TS"])
                      s.wait_ge(S.h["VS"], g_p2["VS"])
                      s.wait_ge(S.h["LDG"], S.n["LDG"])
                  if sl >= 2:
                      s.wait_ge(S.h["TS"], st3[sl - 2])
                  for c in range(NCH):
                      o = c * T + sl * 512
                      i = s.activation(x3q[sl % 2][:, c * 512:(c + 1) * 512],
                                       X[:, o:o + 512], Act.Square)
                  sq3.append(S.n["SS"])

              @blk.tensor
              def _(t):
                  t.wait_ge(S.h["VS"], vc3[sl])
                  t.wait_ge(S.h["SS"], sq3[sl])
                  if sl == 0:
                      t.wait_ge(S.h["LDG"], S.n["LDG"])
                  if sl >= 2:
                      t.wait_ge(S.h["VS"], fr3[sl - 2])
                  for c in range(NCH):
                      i = t.matmul(p3s[sl % 2][:], ones_st[:],
                                   x3b[sl % 2][:, c * 512:(c + 1) * 512],
                                   start=(c == 0), stop=(c == NCH - 1))
                      i = t.matmul(p3q[sl % 2][:], ones_st[:],
                                   x3q[sl % 2][:, c * 512:(c + 1) * 512],
                                   start=(c == 0), stop=(c == NCH - 1))
                  S.inc(i, "TS")
                  st3.append(S.n["TS"])

              @blk.vector
              def _(v):
                  v = AV.bind(v)
                  v.wait_ge(S.h["TS"], st3[sl])
                  o = sl * 512
                  v.tensor_scalar(mu_r[:, o:o + 512], p3s[sl % 2][:], 1.0 / D,
                                  None, Alu.mult)
                  v.tensor_scalar(R[0][:], p3q[sl % 2][:], 1.0 / D,
                                  None, Alu.mult)
                  fr3.append(S.n["VS"])
                  v.tensor_tensor(R[1][:], mu_r[:, o:o + 512],
                                  mu_r[:, o:o + 512], Alu.mult)
                  v.tensor_tensor(R[0][:], R[0][:], R[1][:], Alu.subtract)
                  i = v.tensor_scalar(R[1][:], R[0][:], EPS0, None, Alu.add)

              row_rsqrt(R[1], r_r[:, sl * 512:sl * 512 + 512], R[2], n=512)

          ts_r3 = []
          xn1_done = []
          for sl in range(4):
              o = sl * 512
              b = (sl % 2) * 2

              @blk.tensor
              def _(t):
                  S.wait(t, "VS")
                  if sl >= 2:
                      t.wait_ge(S.h["VS"], xn1_done[sl - 2])
                  i = t.matmul(p3r[b][:], ones_rep[:], mu_r[:, o:o + 512],
                               start=True, stop=True)
                  i = t.matmul(p3r[b + 1][:], ones_rep[:], r_r[:, o:o + 512],
                               start=True, stop=True)
                  S.inc(i, "TS")
                  ts_r3.append(S.n["TS"])

              @blk.vector
              def _(v):
                  v = AV.bind(v)
                  v.wait_ge(S.h["TS"], ts_r3[sl])
                  for c in range(NCH):
                      xo = c * T + o
                      v.tensor_tensor(TMP0[:], X[:, xo:xo + 512], p3r[b][:],
                                      Alu.subtract)
                      i = v.tensor_tensor(XN2[:, xo:xo + 512], TMP0[:],
                                          p3r[b + 1][:], Alu.mult)
                  xn1_done.append(S.n["VS"])
          xn1_all = S.n["VS"]
          p3.close()
          open_stacks.remove(p3)
          g_p3 = {k: S.n[k] for k in ("TS", "VS", "SS")}

          rows_stk.close()
          if PHASES >= 4:
            p4 = contextlib.ExitStack()
            open_stacks.append(p4)
            CMB = sb("CMB", [P, 16 * 8], F32, p4)
            p4r = contextlib.ExitStack()
            open_stacks.append(p4r)
            LGF = sb("LGF", [8, T], F32, p4r)
            LGT = sb("LGT", [P, 16 * 8], F32, p4r)
            RW = sb("RW", [P, NCH * 8], BF16, p4r)
            RC1 = sb("RC1", [P, 1], F32, p4r)
            RC2 = sb("RC2", [P, 8], F32, p4r)
            RC3 = sb("RC3", [P, 8], F32, p4r)
            p4l = [psum(f"p4l{i}", [8, 512], F32, p4r) for i in range(2)]
            p4t = [psum(f"p4t{i}", [P, 8], F32, p4r) for i in range(2)]
            p4ct = [psum(f"p4ct{i}", [8, P], F32, p4r) for i in range(2)]
            RB128 = sb("RB128", [P, 8], F32, p4r)

            @blk.gpsimd
            def _(g):
                g.wait_ge(S.h["LDG"], S.n["LDG"])
                g.wait_ge(S.h["TS"], g_p3["TS"])
                g.wait_ge(S.h["VS"], g_p3["VS"])
                g.wait_ge(S.h["SS"], g_p3["SS"])
                for c in range(NCH):
                    i = g.dma_start(out=RW[:, c * 8:(c + 1) * 8],
                                    in_=rw[c * P:(c + 1) * P, :])
                    S.inc(i, "LDG", 16)
            ldg_rw = S.n["LDG"]

            rmm = []
            rev = []
            for sl in range(4):
                @blk.tensor
                def _(t):
                    if sl == 0:
                        t.wait_ge(S.h["LDG"], ldg_rw)
                    t.wait_ge(S.h["VS"], xn1_done[sl])
                    if sl >= 2:
                        t.wait_ge(S.h["VS"], rev[sl - 2])
                    for c in range(NCH):
                        i = t.matmul(p4l[sl % 2][:], RW[:, c * 8:(c + 1) * 8],
                                     XN2[:, c * T + sl * 512:
                                         c * T + (sl + 1) * 512],
                                     start=(c == 0), stop=(c == NCH - 1))
                    S.inc(i, "TS")
                    rmm.append(S.n["TS"])

                @blk.vector
                def _(v):
                    v = AV.bind(v)
                    v.wait_ge(S.h["TS"], rmm[sl])
                    i = v.tensor_copy(LGF[:, sl * 512:(sl + 1) * 512],
                                      p4l[sl % 2][:])
                    rev.append(S.n["VS"])

            @blk.tensor
            def _(t):
                S.wait(t, "VS")
                i = t.matmul(p4t[0][:], ones_rep[:], rbt[:], start=True,
                             stop=True)
                S.inc(i, "TS")
            ts_rb = S.n["TS"]

            @blk.vector
            def _(v):
                v = AV.bind(v)
                v.wait_ge(S.h["TS"], ts_rb)
                i = v.tensor_copy(RB128[:], p4t[0][:])
            vs_rb = S.n["VS"]

            ltr = []
            lev = []
            for tc in range(16):
                @blk.tensor
                def _(t):
                    if tc == 0:
                        t.wait_ge(S.h["VS"], vs_rb)
                    t.wait_ge(S.h["VS"], rev[tc // 4])
                    if tc >= 2:
                        t.wait_ge(S.h["VS"], lev[tc - 2])
                    i = t.transpose(p4t[tc % 2][:],
                                    LGF[0:8, tc * P:(tc + 1) * P],
                                    idf[0:8, 0:8])
                    S.inc(i, "TS")
                    ltr.append(S.n["TS"])

                @blk.vector
                def _(v):
                    v = AV.bind(v)
                    v.wait_ge(S.h["TS"], ltr[tc])
                    i = v.tensor_tensor(LGT[:, tc * 8:(tc + 1) * 8],
                                        p4t[tc % 2][:], RB128[:], Alu.add)
                    lev.append(S.n["VS"])

            cmb_done = []
            for tc in range(16):
                @blk.vector
                def _(v):
                    v = AV.bind(v)
                    lg = LGT[:, tc * 8:(tc + 1) * 8]
                    v.tensor_reduce(RC1[:], lg, AX.X, Alu.max)
                    i = v.tensor_scalar(RC1[:], RC1[:], -1.0, None, Alu.mult)
                vs_m = S.n["VS"]

                @blk.scalar
                def _(s):
                    s = ASS.bind(s)
                    s.wait_ge(S.h["VS"], vs_m)
                    i = s.activation(RC2[:],
                                     LGT[:, tc * 8:(tc + 1) * 8], Act.Exp,
                                     bias=RC1[:])
                ss_e = S.n["SS"]

                @blk.vector
                def _(v):
                    v = AV.bind(v)
                    v.wait_ge(S.h["SS"], ss_e)
                    cs = CMB[:, tc * 8:(tc + 1) * 8]
                    v.tensor_reduce(RC1[:], RC2[:], AX.X, Alu.add)
                    v.reciprocal(RC1[:], RC1[:])
                    v.tensor_scalar(RC2[:], RC2[:], RC1[:], None, Alu.mult)
                    v.tensor_reduce(RC1[:], RC2[:], AX.X, Alu.max)
                    v.tensor_tensor(cs, RC2[:],
                                    RC1[:].broadcast_to((P, 8)), Alu.is_ge)
                    v.scalar_tensor_tensor(RC3[:], cs, -3.0, RC2[:],
                                           Alu.mult, Alu.add)
                    v.tensor_reduce(RC1[:], RC3[:], AX.X, Alu.max)
                    v.tensor_tensor(RC3[:], RC3[:],
                                    RC1[:].broadcast_to((P, 8)), Alu.is_ge)
                    v.tensor_tensor(RC3[:], RC3[:], cs, Alu.add)
                    i = v.tensor_tensor(cs, RC2[:], RC3[:], Alu.mult)
                cmb_done.append(S.n["VS"])

            comb_all = S.n["VS"]


            p4r.close()
            open_stacks.remove(p4r)

            p4b = contextlib.ExitStack()
            open_stacks.append(p4b)
            HM = sb("HM", [P, 32 * 1024], BF16, p4b)
            UPS = [sb(f"UPS{i}", [P, NCH * P], BF16, p4b) for i in range(3)]
            DNS = [sb(f"DNS{i}", [P, 32 * P], BF16, p4b) for i in range(2)]
            REPC = [sb(f"REPC{i}", [P, 512], F32, p4b) for i in range(2)]
            TMPD = TMP0
            pup = [psum(f"pup{i}", [P, 512], F32, p4b) for i in range(4)]
            pdn = [psum(f"pdn{i}", [P, 512], F32, p4b) for i in range(2)]
            prc = psum("prc", [P, 512], F32, p4b)
            prow = psum("prow", [1, 512], F32, p4b)

            last_repc = [0]
            rp_rowv = [0]
            rpt = {}
            prev_dn = [None]
            upld = {}
            dnld = {}
            upmm = {}
            gel = {}
            dnmm = {}
            dnev = {}
            rpl = {}
            rpe = {}

            nld = [0]

            def strip_sem():
                nld[0] += 1
                return "LDG" if nld[0] % 2 == 0 else "LG2"

            for e in range(8):
              for th in range(2):
                for f in range(32):
                    semn = strip_sem()

                    @blk.gpsimd
                    def _(g):
                        g.wait_ge(S.h[semn], S.n[semn])
                        gi = (e * 2 + th) * 32 + f
                        if gi == 0:
                            g.wait_ge(S.h["TS"], g_p3["TS"])
                            g.wait_ge(S.h["SS"], g_p3["SS"])
                            g.wait_ge(S.h["VS"], comb_all)
                        if gi >= 3:
                            pi = gi - 3
                            t_ = upmm.get((pi // 64, (pi // 32) % 2,
                                           pi % 32))
                            if t_ is not None:
                                g.wait_ge(S.h["TS"], t_)
                        i = g.dma_start(
                            out=UPS[f % 3][:].rearrange(
                                "p (dc j) -> p dc j", j=P),
                            in_=upT[e, f].rearrange("(dc p) j -> p dc j",
                                                    p=P))
                        S.inc(i, semn, 16)
                    upld[(e, th, f)] = (semn, S.n[semn])

                    @blk.tensor
                    def _(t):
                        t.wait_ge(S.h[upld[(e, th, f)][0]],
                                  upld[(e, th, f)][1])
                        gi = (e * 2 + th) * 32 + f
                        if gi >= 2:
                            pi = gi - 2
                            t.wait_ge(S.h["SS"], gel[(pi // 64,
                                                      (pi // 32) % 2,
                                                      pi % 32)])
                        for hf in range(2):
                            for c in range(NCH):
                                i = t.matmul(
                                    pup[(f % 2) * 2 + hf][:],
                                    UPS[f % 3][:, c * P:(c + 1) * P],
                                    XN2[:, c * T + th * 1024 + hf * 512:
                                        c * T + th * 1024 + hf * 512 + 512],
                                    start=(c == 0), stop=(c == NCH - 1))
                            S.inc(i, "TS")
                        upmm[(e, th, f)] = S.n["TS"]

                    @blk.scalar
                    def _(s):
                        s = ASS.bind(s)
                        if f == 0 and (e, th) != (0, 0):
                            pe, pt = (e, th - 1) if th == 1 else (e - 1, 1)
                            s.wait_ge(S.h["TS"], dnmm[(pe, pt, NCH - 1, 1)])
                        s.wait_ge(S.h["TS"], upmm[(e, th, f)])
                        for hf in range(2):
                            i = s.activation(
                                HM[:, f * 1024 + hf * 512:
                                   f * 1024 + (hf + 1) * 512],
                                pup[(f % 2) * 2 + hf][:], Act.Gelu)
                        gel[(e, th, f)] = S.n["SS"]

                for hf in range(2):
                    tc0 = th * 8 + hf * 4

                    @blk.tensor
                    def _(t):
                        t.wait_ge(S.h["VS"], comb_all)
                        if (e, th, hf) != (0, 0, 0):
                            t.wait_ge(S.h["VS"], rp_rowv[0])
                        for j in range(4):
                            i = t.transpose(prow[0:1, j * P:(j + 1) * P],
                                            CMB[:, (tc0 + j) * 8 + e:
                                                (tc0 + j) * 8 + e + 1],
                                            idf[:])
                        S.inc(i, "TS")
                        rpt[(e, th, hf)] = S.n["TS"]

                    @blk.vector
                    def _(v):
                        v = AV.bind(v)
                        v.wait_ge(S.h["TS"], rpt[(e, th, hf)])
                        i = v.tensor_copy(R[3][:], prow[0:1, :])
                        rp_rowv[0] = S.n["VS"]
                    rp_row = S.n["VS"]

                    @blk.tensor
                    def _(t):
                        t.wait_ge(S.h["VS"], rp_row)
                        if (e, th, hf) != (0, 0, 0):
                            t.wait_ge(S.h["VS"], last_repc[0])
                        i = t.matmul(prc[:], ones_rep[:], R[3][:],
                                     start=True, stop=True)
                        S.inc(i, "TS")
                        rpl[(e, th, hf)] = S.n["TS"]

                    @blk.vector
                    def _(v):
                        v = AV.bind(v)
                        v.wait_ge(S.h["TS"], rpl[(e, th, hf)])
                        i = v.tensor_copy(REPC[hf][:], prc[:])
                        last_repc[0] = S.n["VS"]
                        rpe[(e, th, hf)] = S.n["VS"]

                for dc in range(NCH):
                    semn = strip_sem()

                    @blk.gpsimd
                    def _(g):
                        g.wait_ge(S.h[semn], S.n[semn])
                        if dc >= 2:
                            g.wait_ge(S.h["TS"], dnmm[(e, th, dc - 2, 1)])
                        elif (e, th) != (0, 0):
                            pe, pt = (e, 0) if th == 1 else (e - 1, 1)
                            g.wait_ge(S.h["TS"], dnmm[(pe, pt, dc + 6, 1)])
                        i = g.dma_start(
                            out=DNS[dc % 2][:].rearrange(
                                "p (fc j) -> p fc j", j=P),
                            in_=dnT[e, dc].rearrange(
                                "(fc p) j -> p fc j", p=P))
                        S.inc(i, semn, 16)
                    dnld[(e, th, dc)] = (semn, S.n[semn])

                    for hf in range(2):
                        @blk.tensor
                        def _(t):
                            t.wait_ge(S.h[dnld[(e, th, dc)][0]],
                                      dnld[(e, th, dc)][1])
                            t.wait_ge(S.h["SS"], gel[(e, th, 31)])
                            pk = (e, th, dc - 1, hf)
                            if dc >= 1 and pk in dnev:
                                t.wait_ge(S.h["VS"], dnev[pk])
                            elif dc == 0 and (e, th) != (0, 0):
                                pe, pt = (e, 0) if th == 1 else (e - 1, 1)
                                t.wait_ge(S.h["VS"],
                                          dnev[(pe, pt, NCH - 1, hf)])
                            for fc in range(32):
                                i = t.matmul(
                                    pdn[hf][:],
                                    DNS[dc % 2][:, fc * P:(fc + 1) * P],
                                    HM[:, fc * 1024 + hf * 512:
                                       fc * 1024 + (hf + 1) * 512],
                                    start=(fc == 0), stop=(fc == 31))
                            S.inc(i, "TS")
                            dnmm[(e, th, dc, hf)] = S.n["TS"]

                        @blk.vector
                        def _(v):
                            v = AV.bind(v)
                            v.wait_ge(S.h["TS"], dnmm[(e, th, dc, hf)])
                            v.wait_ge(S.h["VS"], rpe[(e, th, hf)])
                            xo = dc * T + th * 1024 + hf * 512
                            v.tensor_tensor(TMPD[:], pdn[hf][:], REPC[hf][:],
                                            Alu.mult)
                            i = v.tensor_tensor(X[:, xo:xo + 512],
                                                X[:, xo:xo + 512],
                                                TMPD[:], Alu.add)
                            dnev[(e, th, dc, hf)] = S.n["VS"]

            p4b.close()
            open_stacks.remove(p4b)
            p4.close()
            open_stacks.remove(p4)


          if PHASES >= 5:
            g_p4 = {k: S.n[k] for k in ("TS", "VS", "SS")}
            p5 = contextlib.ExitStack()
            open_stacks.append(p5)
            QG = sb("QG", [P, 4096], BF16, p5)
            WB5 = sb("WB5", [P, NCH * D], BF16, p5)
            SREC = sb("SREC", [P, 512], F32, p5)
            SRB = sb("SRB", [P, 64], BF16, p5)
            RB5 = sb("RB5", [64, CH], F32, p5)
            SGC = sb("SGC", [P, 1], F32, p5)
            GP5 = sb("GP5", [P, NCH * CH], F32, p5)
            GD5 = sb("GD5", [P, NCH * CH], F32, p5)
            pq5 = [psum(f"pq5{i}", [P, 512], F32, p5) for i in range(2)]
            p5a = contextlib.ExitStack()
            open_stacks.append(p5a)
            KG = sb("KG", [P, 4096], BF16, p5a)
            KT = sb("KT", [P, 4096], BF16, p5a)
            VT5 = sb("VT5", [P, 4096], BF16, p5a)
            M3 = sb("M3", [P, NH1 * CH], F32, p5a)
            MSK = sb("MSK", [P, CH], BF16, p5a)
            SMAT = sb("SMAT", [P, 512], F32, p5a)
            SBF = sb("SBF", [P, 512], BF16, p5a)
            psc = psum("psc", [P, CH], F32, p5a)
            pri = psum("pri", [64, CH], F32, p5a)
            prj = psum("prj", [64, CH], F32, p5a)
            psu = psum("psu", [64, 64], F32, p5a)
            pkt = psum("pkt", [P, 64], BF16, p5a)

            @blk.sync
            def _(q):
                q.wait_ge(S.h["LD"], S.n["LD"])
                q.wait_ge(S.h["LD2"], S.n["LD2"])
                q.wait_ge(S.h["TS"], g_p4["TS"])
                q.wait_ge(S.h["VS"], g_p4["VS"])
                q.wait_ge(S.h["SS"], g_p4["SS"])
                i = q.dma_start(out=M3[:].rearrange("p (h i) -> p h i", i=CH),
                                in_=mask3[:].rearrange("h j i -> j h i"))
                S.inc(i, "LD", 16)
                i = q.dma_start(out=SGC[:], in_=sgate[:])
                S.inc(i, "LD", 16)
                i = q.dma_start(out=GP5[:].rearrange("p (c i) -> p c i", i=CH),
                                in_=gpos[:].rearrange("(c p) i -> p c i", p=P))
                S.inc(i, "LD", 16)
                i = q.dma_start(out=GD5[:].rearrange("p (c i) -> p c i", i=CH),
                                in_=gdecC[:].rearrange("(c p) i -> p c i", p=P))
                S.inc(i, "LD", 16)
            ld_m3 = S.n["LD"]

            @blk.vector
            def _(v):
                v = AV.bind(v)
                v.wait_ge(S.h["TS"], g_p4["TS"])
                v.wait_ge(S.h["SS"], g_p4["SS"])
                v.wait_ge(S.h["LDG"], S.n["LDG"])
                i = v.memset(SMAT[:], 0.0)
                i = v.memset(SBF[:], 0.0)
            vs_s0 = S.n["VS"]

            gs5_b = S.n["GS"]
            prevhc = [None]
            previ = [None]
            fix_prev = [None]
            retst = {}
            srbv = {}
            fixt = {}
            rsq5 = {}
            swv = {}
            oproj5 = {}
            pq5_hist = []
            qev5 = {}
            kev5 = {}
            vev5 = {}
            ktv = {}
            mskv = {}
            rbv = {}
            retv = {}
            stv = {}
            sbv = {}
            ldw5 = {}
            pmm5 = {}
            ktr = {}
            scv = {}
            suv = {}
            riv = {}
            rjv = {}

            for g5 in range(4):
                gt0 = g5 * 512
                for wi, (wt, dst, pat) in enumerate(
                        ((wq1, QG, GP5), (wk1, KG, GD5), (wv1, VT5, None))):
                    semn = "LDG" if (g5 * 3 + wi) % 2 == 0 else "LG2"

                    @blk.gpsimd
                    def _(q):
                        q.wait_ge(S.h[semn], S.n[semn])
                        if g5 == 0 and wi == 0:
                            q.wait_ge(S.h["LD"], ld_m3)
                            q.wait_ge(S.h["TS"], g_p4["TS"])
                            q.wait_ge(S.h["VS"], g_p4["VS"])
                            q.wait_ge(S.h["SS"], g_p4["SS"])
                        if (g5, wi) != (0, 0):
                            t_ = pmm5[(g5 * 3 + wi) - 1]
                            q.wait_ge(S.h["TS"], t_)
                        for c in range(NCH):
                            i = q.dma_start(out=WB5[:, c * D:(c + 1) * D],
                                            in_=wt[c * P:(c + 1) * P, :])
                            S.inc(i, semn, 16)
                    ldw5[(g5, wi)] = (semn, S.n[semn])

                    if pat is not None:
                        for oc in range(NCH):
                            @blk.tensor
                            def _(t):
                                if oc == 0:
                                    t.wait_ge(S.h[ldw5[(g5, wi)][0]],
                                              ldw5[(g5, wi)][1])
                                    if g5 == 0 and wi == 0:
                                        t.wait_ge(S.h["VS"], vs_s0)
                                    if g5 >= 1 and wi == 0:
                                        t.wait_ge(S.h["VS"], retv[
                                            (g5 - 1, 3, NH1 - 1)])
                                        t.wait_ge(S.h["VS"], stv[
                                            (g5 - 1, 3, NH1 - 1)])
                                        t.wait_ge(S.h["LDG"], retst[
                                            (g5 - 1, 3, NH1 - 1)])
                                if len(pq5_hist) >= 2:
                                    t.wait_ge(S.h[pq5_hist[-2][0]],
                                              pq5_hist[-2][1])
                                for c in range(NCH):
                                    i = t.matmul(
                                        pq5[oc % 2][:],
                                        WB5[:, c * D + oc * P:
                                            c * D + (oc + 1) * P],
                                        XN2[:, c * T + gt0:c * T + gt0 + 512],
                                        start=(c == 0), stop=(c == NCH - 1))
                                S.inc(i, "TS")
                                pmm5[g5 * 3 + wi] = S.n["TS"]
                                pmm5[(g5, wi, oc)] = S.n["TS"]

                            @blk.vector
                            def _(v):
                                v = AV.bind(v)
                                v.wait_ge(S.h["TS"], pmm5[(g5, wi, oc)])
                                pp = pat[:, oc * CH:(oc + 1) * CH]
                                bc = pp.unsqueeze(1).broadcast_to((P, 4, CH))
                                ps3 = pq5[oc % 2][:].rearrange(
                                    "p (c i) -> p c i", i=CH)
                                d3 = dst[:, oc * 512:(oc + 1) * 512].rearrange(
                                    "p (c i) -> p c i", i=CH)
                                i = v.tensor_tensor(d3, ps3, bc, Alu.mult)
                                qev5[(g5, wi, oc)] = S.n["VS"]
                                pq5_hist.append(("VS", S.n["VS"]))
                    else:
                        for tc4 in range(4):
                            for sl2 in range(2):
                                @blk.tensor
                                def _(t):
                                    if tc4 == 0 and sl2 == 0:
                                        t.wait_ge(S.h[ldw5[(g5, wi)][0]],
                                                  ldw5[(g5, wi)][1])
                                    if len(pq5_hist) >= 2:
                                        t.wait_ge(S.h[pq5_hist[-2][0]],
                                                  pq5_hist[-2][1])
                                    for c in range(NCH):
                                        i = t.matmul(
                                            pq5[(tc4 * 2 + sl2) % 2][:],
                                            XN2[:, c * T + gt0 + tc4 * P:
                                                c * T + gt0 + (tc4 + 1) * P],
                                            WB5[:, c * D + sl2 * 512:
                                                c * D + (sl2 + 1) * 512],
                                            start=(c == 0),
                                            stop=(c == NCH - 1))
                                    S.inc(i, "TS")
                                    pmm5[g5 * 3 + wi] = S.n["TS"]
                                    pmm5[(g5, wi, tc4, sl2)] = S.n["TS"]

                                @blk.vector
                                def _(v):
                                    v = AV.bind(v)
                                    v.wait_ge(S.h["TS"],
                                              pmm5[(g5, wi, tc4, sl2)])
                                    i = v.tensor_copy(
                                        VT5[:, tc4 * D + sl2 * 512:
                                            tc4 * D + (sl2 + 1) * 512],
                                        pq5[(tc4 * 2 + sl2) % 2][:])
                                    vev5[(g5, tc4, sl2)] = S.n["VS"]
                                    pq5_hist.append(("VS", S.n["VS"]))

                for cl in range(4):
                    for h in range(NH1):
                        @blk.tensor
                        def _(t):
                            t.wait_ge(S.h["VS"], qev5[(g5, 1, h // 2)])
                            idx5 = cl * NH1 + h
                            if (g5, idx5) != (0, 0):
                                t.wait_ge(S.h["GS"], gs5_b + g5 * 64 + idx5)
                            b5 = 64 * (h % 2)
                            i = t.transpose(
                                pkt[:, 0:64],
                                KG[b5:b5 + 64,
                                   (h // 2) * 512 + cl * CH:
                                   (h // 2) * 512 + (cl + 1) * CH],
                                idb[b5:b5 + 64, b5:b5 + 64])
                            S.inc(i, "TS")
                            ktr[(g5, cl, h)] = S.n["TS"]

                        @blk.vector
                        def _(v):
                            v = AV.bind(v)
                            v.wait_ge(S.h["TS"], ktr[(g5, cl, h)])
                            i = v.tensor_copy(
                                KT[:, cl * D + h * 64:cl * D + (h + 1) * 64],
                                pkt[:, 0:64], inc="GS")

                for cl in range(4):
                    for h in range(NH1):
                        b5 = 64 * (h % 2)
                        hc = (h // 2) * 512 + cl * CH

                        @blk.tensor
                        def _(t):
                            t.wait_ge(S.h["VS"], qev5[(g5, 0, h // 2)])
                            t.wait_ge(S.h["VS"], qev5[(g5, 1, h // 2)])
                            if (g5, cl, h) != (0, 0, 0):
                                t.wait_ge(S.h["VS"], mskv[prevhc[0]])
                            i = t.matmul(psc[:], KG[b5:b5 + 64, hc:hc + CH],
                                         QG[b5:b5 + 64, hc:hc + CH],
                                         start=True, stop=True)
                            S.inc(i, "TS")
                            scv[(g5, cl, h)] = S.n["TS"]

                        @blk.vector
                        def _(v):
                            v = AV.bind(v)
                            v.wait_ge(S.h["TS"], scv[(g5, cl, h)])
                            i = v.tensor_tensor(MSK[:], psc[:],
                                                M3[:, h * CH:(h + 1) * CH],
                                                Alu.mult)
                            mskv[(g5, cl, h)] = S.n["VS"]
                            prevhc[0] = (g5, cl, h)

                        @blk.tensor
                        def _(t):
                            t.wait_ge(S.h["VS"], mskv[(g5, cl, h)])
                            t.wait_ge(S.h["VS"], vev5[(g5, cl, 0)])
                            t.wait_ge(S.h["VS"], vev5[(g5, cl, 1)])
                            if (g5, cl, h) != (0, 0, 0):
                                t.wait_ge(S.h["VS"], rbv[previ[0]])
                                t.wait_ge(S.h["VS"], retv[previ[0]])
                            i = t.matmul(pri[:],
                                         VT5[:, cl * D + h * 64:
                                             cl * D + (h + 1) * 64],
                                         MSK[:], start=True, stop=True)
                            S.inc(i, "TS")
                            riv[(g5, cl, h)] = S.n["TS"]
                            if previ[0] is not None:
                                t.wait_ge(S.h["VS"], stv[previ[0]])
                            i = t.matmul(prj[:],
                                         SBF[b5:b5 + 64,
                                             (h // 2) * 64:(h // 2) * 64 + 64],
                                         QG[b5:b5 + 64, hc:hc + CH],
                                         start=True, stop=True)
                            S.inc(i, "TS")
                            rjv[(g5, cl, h)] = S.n["TS"]
                            i = t.matmul(psu[:],
                                         KT[:, cl * D + h * 64:
                                            cl * D + (h + 1) * 64],
                                         VT5[:, cl * D + h * 64:
                                             cl * D + (h + 1) * 64],
                                         start=True, stop=True)
                            S.inc(i, "TS")
                            suv[(g5, cl, h)] = S.n["TS"]

                        @blk.vector
                        def _(v):
                            v = AV.bind(v)
                            if previ[0] is not None:
                                v.wait_ge(S.h["LDG"], retst[previ[0]])
                            v.wait_ge(S.h["TS"], riv[(g5, cl, h)])
                            i = v.tensor_copy(RB5[:], pri[:])
                            rbv[(g5, cl, h)] = S.n["VS"]
                            v.wait_ge(S.h["TS"], rjv[(g5, cl, h)])
                            i = v.tensor_tensor(RB5[:], RB5[:], prj[:],
                                                Alu.add)
                            retv[(g5, cl, h)] = S.n["VS"]
                            v.wait_ge(S.h["TS"], suv[(g5, cl, h)])
                            gC = float(GAMMAS[h] ** CH)
                            v.scalar_tensor_tensor(
                                SMAT[b5:b5 + 64,
                                     (h // 2) * 64:(h // 2) * 64 + 64],
                                SMAT[b5:b5 + 64,
                                     (h // 2) * 64:(h // 2) * 64 + 64],
                                gC, psu[:], Alu.mult, Alu.add)
                            i = v.tensor_copy(
                                SBF[b5:b5 + 64,
                                    (h // 2) * 64:(h // 2) * 64 + 64],
                                SMAT[b5:b5 + 64,
                                     (h // 2) * 64:(h // 2) * 64 + 64])
                            stv[(g5, cl, h)] = S.n["VS"]
                            previ[0] = (g5, cl, h)

                        @blk.gpsimd
                        def _(g):
                            g.wait_ge(S.h["LDG"], S.n["LDG"])
                            g.wait_ge(S.h["VS"], retv[(g5, cl, h)])
                            i = g.dma_start(
                                out=ret_dram[64 * h:64 * (h + 1),
                                             gt0 + cl * CH:
                                             gt0 + (cl + 1) * CH],
                                in_=RB5[:])
                            S.inc(i, "LDG", 16)
                        retst[(g5, cl, h)] = S.n["LDG"]

            ldg_ret = S.n["LDG"]

            st3o = st_contrib[:].rearrange("(h p j) -> h p j", p=64, j=64)
            sta3 = st_all[0].rearrange("(h p j) -> h p j", p=64, j=64)

            @blk.gpsimd
            def _(g):
                g.wait_ge(S.h["LDG"], S.n["LDG"])
                g.wait_ge(S.h["VS"], stv[(3, 3, NH1 - 1)])
                i = g.dma_start(
                    out=st3o[0::2].transpose([1, 0, 2]),
                    in_=SMAT[0:64, :].rearrange("p (q j) -> p q j", j=64))
                S.inc(i, "LDG", 16)
                i = g.dma_start(
                    out=st3o[1::2].transpose([1, 0, 2]),
                    in_=SMAT[64:128, :].rearrange("p (q j) -> p q j", j=64))
                S.inc(i, "LDG", 16)
                g.wait_ge(S.h["LDG"], S.n["LDG"])
                i = g.collective_compute(
                    "AllGather", Alu.bypass,
                    replica_groups=[[0, 1], [2, 3], [4, 5], [6, 7]],
                    ins=[st_contrib[:]], outs=[st_all[:]])
                S.inc(i, "CC", 1)
                g.wait_ge(S.h["CC"], S.n["CC"])
                i = g.dma_start(
                    out=SREC[0:64, :].rearrange("p (q j) -> p q j", j=64),
                    in_=sta3[0::2].transpose([1, 0, 2]))
                S.inc(i, "LDG", 16)
                i = g.dma_start(
                    out=SREC[64:128, :].rearrange("p (q j) -> p q j", j=64),
                    in_=sta3[1::2].transpose([1, 0, 2]))
                S.inc(i, "LDG", 16)
            ldg_srec = S.n["LDG"]

            @blk.vector
            def _(v):
                v = AV.bind(v)
                v.wait_ge(S.h["LDG"], ldg_srec)
                i = v.tensor_scalar(SREC[:], SREC[:], SGC[:, 0:1], None,
                                    Alu.mult)
            vs_srec = S.n["VS"]

            g_p5a = {k: S.n[k] for k in ("TS", "VS", "SS")}
            p5a.close()
            open_stacks.remove(p5a)

            p5b = contextlib.ExitStack()
            open_stacks.append(p5b)
            RSL = sb("RSL", [P, NCH * 512], BF16, p5b)
            RSQ = sb("RSQ", [P, NCH * 512], BF16, p5b)
            SW = sb("SW", [P, NCH * 512], BF16, p5b)
            LNG = sb("LNG", [P, NCH * 512], BF16, p5b)
            pst5 = [psum(f"pst5{i}", [1, 512], F32, p5b) for i in range(2)]
            prep5 = [psum(f"prep5{i}", [P, 512], F32, p5b) for i in range(2)]
            pr2 = psum("pr2", [64, CH], F32, p5b)

            wbx_hist = [(None, pmm5[11])]

            def load_wbx(wt, tag, extra=()):
                semn = "LDG" if (len(wbx_hist) % 2 == 0) else "LG2"

                @blk.gpsimd
                def _(q):
                    q.wait_ge(S.h[semn], S.n[semn])
                    for nm, val in extra:
                        q.wait_ge(S.h[nm], val)
                    if wbx_hist:
                        q.wait_ge(S.h["TS"], wbx_hist[-1][1])
                    for c in range(NCH):
                        i = q.dma_start(out=WB5[:, c * D:(c + 1) * D],
                                        in_=wt[c * P:(c + 1) * P, :])
                        S.inc(i, semn, 16)
                return (semn, S.n[semn])

            fixv = {}
            for ts5 in range(4):
                gt0 = ts5 * 512

                semn = "LD" if ts5 % 2 == 0 else "LD2"

                @blk.sync
                def _(q):
                    q.wait_ge(S.h[semn], S.n[semn])
                    if ts5 == 0:
                        q.wait_ge(S.h["LDG"], ldg_ret)
                        q.wait_ge(S.h["TS"], g_p5a["TS"])
                        q.wait_ge(S.h["VS"], g_p5a["VS"])
                        q.wait_ge(S.h["SS"], g_p5a["SS"])
                    else:
                        q.wait_ge(S.h["TS"], oproj5[(ts5 - 1, NCH - 1)])
                        q.wait_ge(S.h["SS"], rsq5[ts5 - 1])
                    for c in range(NCH):
                        i = q.dma_start(
                            out=RSL[:, c * 512:(c + 1) * 512],
                            in_=ret_dram[c * P:(c + 1) * P, gt0:gt0 + 512])
                        S.inc(i, semn, 16)
                rsl_ld = (semn, S.n[semn])

                wql = load_wbx(wq1, "q")
                for oc in range(NCH):
                    @blk.tensor
                    def _(t):
                        if oc == 0:
                            t.wait_ge(S.h[wql[0]], wql[1])
                        if len(pq5_hist) >= 2:
                            t.wait_ge(S.h[pq5_hist[-2][0]],
                                      pq5_hist[-2][1])
                        for c in range(NCH):
                            i = t.matmul(
                                pq5[oc % 2][:],
                                WB5[:, c * D + oc * P:c * D + (oc + 1) * P],
                                XN2[:, c * T + gt0:c * T + gt0 + 512],
                                start=(c == 0), stop=(c == NCH - 1))
                        S.inc(i, "TS")
                        pmm5[(9, ts5, oc)] = S.n["TS"]

                    @blk.vector
                    def _(v):
                        v = AV.bind(v)
                        v.wait_ge(S.h["TS"], pmm5[(9, ts5, oc)])
                        pp = GP5[:, oc * CH:(oc + 1) * CH]
                        bc = pp.unsqueeze(1).broadcast_to((P, 4, CH))
                        ps3 = pq5[oc % 2][:].rearrange("p (c i) -> p c i",
                                                       i=CH)
                        d3 = QG[:, oc * 512:(oc + 1) * 512].rearrange(
                            "p (c i) -> p c i", i=CH)
                        i = v.tensor_tensor(d3, ps3, bc, Alu.mult)
                        pq5_hist.append(("VS", S.n["VS"]))
                qg_done = S.n["VS"]
                wbx_hist.append((wql, S.n["TS"]))

                for h in range(NH1):
                    b5 = 64 * (h % 2)
                    for cl in range(4):
                        gc = ts5 * 4 + cl
                        hc = (h // 2) * 512 + cl * CH

                        @blk.vector
                        def _(v):
                            v = AV.bind(v)
                            if ts5 == 0 and h == 0 and cl == 0:
                                v.wait_ge(S.h["VS"], vs_srec)
                            if fix_prev[0] is not None:
                                v.wait_ge(S.h["TS"], fixt[fix_prev[0]])
                            gcv = float(GAMMAS[h] ** (CH * gc))
                            i = v.tensor_scalar(
                                SRB[b5:b5 + 64, :],
                                SREC[b5:b5 + 64,
                                     (h // 2) * 64:(h // 2) * 64 + 64],
                                gcv, None, Alu.mult)
                            srbv[(ts5, h, cl)] = S.n["VS"]

                        @blk.tensor
                        def _(t):
                            t.wait_ge(S.h["VS"], srbv[(ts5, h, cl)])
                            t.wait_ge(S.h["VS"], qg_done)
                            if fix_prev[0] is not None:
                                t.wait_ge(S.h["VS"], fixv[fix_prev[0]])
                            i = t.matmul(pr2[:], SRB[b5:b5 + 64, :],
                                         QG[b5:b5 + 64, hc:hc + CH],
                                         start=True, stop=True)
                            S.inc(i, "TS")
                            fixt[(ts5, h, cl)] = S.n["TS"]

                        @blk.vector
                        def _(v):
                            v = AV.bind(v)
                            v.wait_ge(S.h["TS"], fixt[(ts5, h, cl)])
                            v.wait_ge(S.h[rsl_ld[0]], rsl_ld[1])
                            rp = RSL[b5:b5 + 64,
                                     (h // 2) * 512 + cl * CH:
                                     (h // 2) * 512 + (cl + 1) * CH]
                            i = v.tensor_tensor(rp, rp, pr2[:], Alu.add)
                            fixv[(ts5, h, cl)] = S.n["VS"]
                            fix_prev[0] = (ts5, h, cl)
                fix_all = S.n["VS"]

                @blk.scalar
                def _(s):
                    s = ASS.bind(s)
                    s.wait_ge(S.h["VS"], fix_all)
                    for c in range(NCH):
                        i = s.activation(RSQ[:, c * 512:(c + 1) * 512],
                                         RSL[:, c * 512:(c + 1) * 512],
                                         Act.Square)
                rsq5[ts5] = S.n["SS"]

                @blk.tensor
                def _(t):
                    t.wait_ge(S.h["SS"], rsq5[ts5])
                    t.wait_ge(S.h["VS"], fix_all)
                    for c in range(NCH):
                        i = t.matmul(pst5[0][:], ones_st[:],
                                     RSL[:, c * 512:(c + 1) * 512],
                                     start=(c == 0), stop=(c == NCH - 1))
                        i = t.matmul(pst5[1][:], ones_st[:],
                                     RSQ[:, c * 512:(c + 1) * 512],
                                     start=(c == 0), stop=(c == NCH - 1))
                    S.inc(i, "TS")
                st5 = S.n["TS"]

                @blk.vector
                def _(v):
                    v = AV.bind(v)
                    v.wait_ge(S.h["TS"], st5)
                    v.tensor_scalar(R[4][:], pst5[0][:], 1.0 / D, None,
                                    Alu.mult)
                    v.tensor_scalar(R[0][:], pst5[1][:], 1.0 / D, None,
                                    Alu.mult)
                    v.tensor_tensor(R[1][:], R[4][:], R[4][:], Alu.mult)
                    v.tensor_tensor(R[0][:], R[0][:], R[1][:], Alu.subtract)
                    i = v.tensor_scalar(R[1][:], R[0][:], EPS1, None, Alu.add)

                row_rsqrt(R[1], R[5], R[2], n=512)

                wgl = load_wbx(wg1, "g")
                for oc in range(NCH):
                    @blk.tensor
                    def _(t):
                        if oc == 0:
                            t.wait_ge(S.h[wgl[0]], wgl[1])
                        if len(pq5_hist) >= 2:
                            t.wait_ge(S.h[pq5_hist[-2][0]],
                                      pq5_hist[-2][1])
                        for c in range(NCH):
                            i = t.matmul(
                                pq5[oc % 2][:],
                                WB5[:, c * D + oc * P:c * D + (oc + 1) * P],
                                XN2[:, c * T + gt0:c * T + gt0 + 512],
                                start=(c == 0), stop=(c == NCH - 1))
                        S.inc(i, "TS")
                        pmm5[(8, ts5, oc)] = S.n["TS"]

                    @blk.scalar
                    def _(s):
                        s = ASS.bind(s)
                        s.wait_ge(S.h["TS"], pmm5[(8, ts5, oc)])
                        i = s.activation(SW[:, oc * 512:(oc + 1) * 512],
                                         pq5[oc % 2][:], Act.Silu,
                                         bias=nv1[:, oc:oc + 1])
                        swv[(ts5, oc)] = S.n["SS"]
                        pq5_hist.append(("SS", S.n["SS"]))
                wbx_hist.append((wgl, S.n["TS"]))

                @blk.tensor
                def _(t):
                    S.wait(t, "VS")
                    i = t.matmul(prep5[0][:], ones_rep[:], R[4][:],
                                 start=True, stop=True)
                    i = t.matmul(prep5[1][:], ones_rep[:], R[5][:],
                                 start=True, stop=True)
                    S.inc(i, "TS")
                rep5 = S.n["TS"]

                @blk.vector
                def _(v):
                    v = AV.bind(v)
                    v.wait_ge(S.h["TS"], rep5)
                    v.wait_ge(S.h["SS"], swv[(ts5, NCH - 1)])
                    for c in range(NCH):
                        co = c * 512
                        v.tensor_tensor(TMP0[:], RSL[:, co:co + 512],
                                        prep5[0][:], Alu.subtract)
                        v.tensor_tensor(TMP0[:], TMP0[:], prep5[1][:],
                                        Alu.mult)
                        v.scalar_tensor_tensor(
                            LNG[:, co:co + 512], TMP0[:],
                            nv1[:, 2 * NCH + c:2 * NCH + c + 1],
                            nv1[:, 3 * NCH + c:3 * NCH + c + 1
                                ].broadcast_to((P, 512)),
                            Alu.mult, Alu.add)
                        i = v.tensor_tensor(LNG[:, co:co + 512],
                                            LNG[:, co:co + 512],
                                            SW[:, co:co + 512], Alu.mult)
                lng5 = S.n["VS"]

                wol = load_wbx(wo1, "o")
                for oc in range(NCH):
                    @blk.tensor
                    def _(t):
                        if oc == 0:
                            t.wait_ge(S.h[wol[0]], wol[1])
                            t.wait_ge(S.h["VS"], lng5)
                        if len(pq5_hist) >= 2:
                            t.wait_ge(S.h[pq5_hist[-2][0]],
                                      pq5_hist[-2][1])
                        for c in range(NCH):
                            i = t.matmul(
                                pq5[oc % 2][:],
                                WB5[:, c * D + oc * P:c * D + (oc + 1) * P],
                                LNG[:, c * 512:(c + 1) * 512],
                                start=(c == 0), stop=(c == NCH - 1))
                        S.inc(i, "TS")
                        oproj5[(ts5, oc)] = S.n["TS"]

                    @blk.vector
                    def _(v):
                        v = AV.bind(v)
                        v.wait_ge(S.h["TS"], oproj5[(ts5, oc)])
                        xo = oc * T + gt0
                        v.tensor_tensor(X[:, xo:xo + 512], X[:, xo:xo + 512],
                                        pq5[oc % 2][:], Alu.add)
                        i = v.tensor_scalar(X[:, xo:xo + 512],
                                            X[:, xo:xo + 512],
                                            nv1[:, NCH + oc:NCH + oc + 1],
                                            None, Alu.add)
                        pq5_hist.append(("VS", S.n["VS"]))
                wbx_hist.append((wol, S.n["TS"]))

            p5b.close()
            open_stacks.remove(p5b)
            p5.close()
            open_stacks.remove(p5)
        except StopEmission:
            for stk in reversed(open_stacks):
                stk.close()


        if TAP in ("xn2", "xq"):
            tap_marks = []
            for c in range(NCH):
                for sl4 in range(4):
                    @blk.vector
                    def _(v):
                        v = AV.bind(v)
                        S.wait(v, "TS")
                        S.wait(v, "SS")
                        if tap_marks:
                            v.wait_ge(S.h["LDG"], tap_marks[-1])
                        v.tensor_copy(
                            TMP0[:],
                            XN2[:, c * T + sl4 * 512:c * T + (sl4 + 1) * 512])
                    m = S.n["VS"]

                    @blk.gpsimd
                    def _(g):
                        g.wait_ge(S.h["LDG"], S.n["LDG"])
                        g.wait_ge(S.h["VS"], m)
                        i = g.dma_start(
                            out=y_out[c * P:(c + 1) * P,
                                      sl4 * 512:(sl4 + 1) * 512],
                            in_=TMP0[:])
                        S.inc(i, "LDG", 16)
                    tap_marks.append(S.n["LDG"])

            @blk.gpsimd
            def _(g):
                g.wait_ge(S.h["LDG"], S.n["LDG"])
        else:
            @blk.gpsimd
            def _(g):
                g.wait_ge(S.h["LDG"], S.n["LDG"])
                S.wait(g, "LD", ld_x)
                S.wait(g, "VS")
                S.wait(g, "TS")
                S.wait(g, "SS")
                for c in range(NCH):
                    i = g.dma_start(out=y_out[c * P:(c + 1) * P, :],
                                    in_=X[:, c * T:(c + 1) * T])
                    S.inc(i, "LDG", 16)
                g.wait_ge(S.h["LDG"], S.n["LDG"])

    return nc


def _prep_inputs(inputs):
    f32 = np.float32
    x = np.ascontiguousarray(inputs["L0_attn_x"], dtype=f32)

    def col8(vname):
        return np.ascontiguousarray(
            inputs[vname].astype(f32).reshape(NCH, P).T)

    nvec0 = np.concatenate([col8("L0_attn_norm_w"), col8("L0_attn_norm_b"),
                            col8("L0_attn_out_norm_w"),
                            col8("L0_attn_out_norm_b")], axis=1)
    nvec1 = np.concatenate([col8("L1_attn_bg"), col8("L1_attn_bo"),
                            col8("L1_attn_ln_gamma"),
                            col8("L1_attn_ln_beta")], axis=1)

    wg = inputs["L0_ffn_w_gate"].astype(f32)
    wu = inputs["L0_ffn_w_up"].astype(f32)
    wd = inputs["L0_ffn_w_down"].astype(f32)
    wg_p = np.zeros((8 * 384, D), f32)
    wg_p[:FF0] = wg
    wu_p = np.zeros((8 * 384, D), f32)
    wu_p[:FF0] = wu

    up = inputs["L1_ffn_expert_up"].astype(f32)
    dn = inputs["L1_ffn_expert_down"].astype(f32)
    upT = np.ascontiguousarray(
        up.transpose(0, 2, 1).reshape(8, D, 32, P).transpose(0, 2, 1, 3))
    dnT = np.ascontiguousarray(
        dn.transpose(0, 2, 1).reshape(8, FFE, NCH, P).transpose(0, 2, 1, 3))

    identb = np.eye(P, dtype=f32).astype(BF)
    identf = np.eye(P, dtype=f32)

    gam = np.array(GAMMAS, dtype=np.float64)
    i_idx = np.arange(CH, dtype=np.float64)
    gpos = np.zeros((D, CH), f32)
    gdecC = np.zeros((D, CH), f32)
    for h in range(NH1):
        gpos[h * DH:(h + 1) * DH, :] = (gam[h] ** i_idx / 8.0).astype(f32)
        gdecC[h * DH:(h + 1) * DH, :] = (gam[h] ** (CH - i_idx)).astype(f32)
    ii = i_idx[None, :]
    jj = i_idx[:, None]
    mask3 = np.zeros((NH1, CH, CH), f32)
    for h in range(NH1):
        mask3[h] = ((ii >= jj) * gam[h] ** (-float(CH))).astype(f32)
    gchunk = np.zeros((NH1, 16), f32)
    for h in range(NH1):
        gchunk[h] = (gam[h] ** (CH * np.arange(16, dtype=np.float64))).astype(f32)

    shared = {
        "wq0": inputs["L0_attn_Wq"].astype(f32),
        "wkv0": inputs["L0_attn_Wkv"].astype(f32),
        "wo0": inputs["L0_attn_Wo"].astype(f32),
        "nvec0": nvec0, "nvec1": nvec1,
        "wq1": inputs["L1_attn_Wq"].astype(f32),
        "wk1": inputs["L1_attn_Wk"].astype(f32),
        "wv1": inputs["L1_attn_Wv"].astype(f32),
        "wg1": inputs["L1_attn_Wg"].astype(f32),
        "wo1": inputs["L1_attn_Wo"].astype(f32),
        "rw": inputs["L1_ffn_router_w"].astype(f32),
        "rbias": inputs["L1_ffn_router_b"].astype(f32).reshape(1, 8),
        "upT": upT, "dnT": dnT,
        "identb": identb, "identf": identf,
        "gpos": gpos, "gdecC": gdecC, "mask3": mask3, "gchunk": gchunk,
    }

    in_maps = []
    for c in range(8):
        s, h = c // 2, c % 2
        own = np.ascontiguousarray(x[s, h * T:(h + 1) * T, :].T)
        part = np.ascontiguousarray(x[s, (1 - h) * T:(2 - h) * T, :].T)
        m = dict(shared)
        m["x_own"] = own
        m["x_part"] = part
        m["sgate"] = np.full((P, 1), float(h), f32)
        m["wg_sl"] = np.ascontiguousarray(wg_p[c * 384:(c + 1) * 384])
        m["wu_sl"] = np.ascontiguousarray(wu_p[c * 384:(c + 1) * 384])
        m["wd_sl"] = np.ascontiguousarray(wd[c * P:(c + 1) * P])
        in_maps.append(m)
    return in_maps


_NC_CACHE = {}


def kernel(**inputs):
    in_maps = _prep_inputs(inputs)
    if PHASES not in _NC_CACHE:
        _NC_CACHE[PHASES] = build_program()
    nc = _NC_CACHE[PHASES]
    res = run_bass_kernel_spmd(nc, in_maps, core_ids=list(range(8)))
    out = np.empty((4, SEQ, D), np.float32)
    for c in range(8):
        s, h = c // 2, c % 2
        out[s, h * T:(h + 1) * T, :] = res.results[c]["y"].T
    return out


# version 03
# speedup vs baseline: 1.0261x; 1.0261x over previous
import contextlib
import numpy as np
import ml_dtypes
import concourse.bass as bass
import concourse.mybir as mybir
from concourse.bass_utils import run_bass_kernel_spmd

F32 = mybir.dt.float32
BF16 = mybir.dt.bfloat16
Alu = mybir.AluOpType
Act = mybir.ActivationFunctionType
AX = mybir.AxisListType
BF = ml_dtypes.bfloat16

P = 128
T = 2048
D = 1024
NCH = D // P
SEQ = 4096
CH = 128
NCHK = T // CH
NH0 = 8
NH1 = 16
DH = 64
FF0 = 2816
FFE = 4096
EPS0 = 1e-5
EPS1 = 1e-3
WQC = 6 * P * D + P * FF0

PHASES = 5
P2STAGE = 2
TAP2 = None
import os
DBG = os.environ.get("KDBG", "")
P1STAGE = 3
TAP = None
GAMMAS = [1.0 - 2.0 ** (-5.0 - h) for h in range(NH1)]


class Sems:

    def __init__(self, nc, stack, names):
        self.h = {}
        self.n = {}
        self.byhandle = {}
        for nm in names:
            self.h[nm] = stack.enter_context(nc.semaphore(nm))
            self.n[nm] = 0
            self.byhandle[id(self.h[nm])] = nm

    def inc(self, instr, nm, k=1):
        instr.then_inc(self.h[nm], k)
        self.n[nm] += k
        return self.n[nm]

    def wait(self, eng, nm, val=None):
        eng.wait_ge(self.h[nm], self.n[nm] if val is None else val)




def _apname(x):
    try:
        return x.tensor.name
    except AttributeError:
        return None


class AutoEng:

    def __init__(self, S, nm):
        self.S = S
        self.nm = nm
        self.w = {}
        self.r = {}
        self.waited = {}
        self.raw = None

    def bind(self, raw):
        self.raw = raw
        return self

    def wait_ge(self, handle, val):
        nm = self.S.byhandle.get(id(handle))
        if nm is not None:
            if self.waited.get(nm, -1) >= val:
                return
            self.waited[nm] = max(self.waited.get(nm, -1), val)
        self.raw.wait_ge(handle, val)

    def _emit(self, ins_n, outs_n, call, inc):
        need = {}

        def add(e):
            if e is not None:
                need[e[0]] = max(need.get(e[0], 0), e[1])

        for t in ins_n:
            if t:
                add(self.w.get(t))
        for t in outs_n:
            if t:
                add(self.w.get(t))
                add(self.r.get(t))
        for semnm, val in need.items():
            if self.waited.get(semnm, -1) < val:
                self.raw.wait_ge(self.S.h[semnm], val)
                self.waited[semnm] = val
        i = call()
        semnm = inc or self.nm
        i.then_inc(self.S.h[semnm], 1)
        self.S.n[semnm] += 1
        v = self.S.n[semnm]
        for t in outs_n:
            if t:
                self.w[t] = (semnm, v)
        for t in ins_n:
            if t:
                self.r[t] = (semnm, v)
        return i

    def memset(self, ap, const, inc=None):
        return self._emit([], [_apname(ap)],
                          lambda: self.raw.memset(ap, const), inc)

    def tensor_copy(self, out, in_, inc=None):
        return self._emit([_apname(in_)], [_apname(out)],
                          lambda: self.raw.tensor_copy(out, in_), inc)

    def tensor_tensor(self, out, in0, in1, op, inc=None):
        return self._emit([_apname(in0), _apname(in1)], [_apname(out)],
                          lambda: self.raw.tensor_tensor(out, in0, in1, op),
                          inc)

    def tensor_scalar(self, out, in0, s1, s2, op0, op1=None, inc=None):
        ins = [_apname(in0), _apname(s1), _apname(s2)]
        if op1 is None:
            call = lambda: self.raw.tensor_scalar(out, in0, s1, s2, op0)
        else:
            call = lambda: self.raw.tensor_scalar(out, in0, s1, s2, op0, op1)
        return self._emit(ins, [_apname(out)], call, inc)

    def scalar_tensor_tensor(self, out, in0, scalar, in1, op0, op1, inc=None):
        return self._emit(
            [_apname(in0), _apname(scalar), _apname(in1)], [_apname(out)],
            lambda: self.raw.scalar_tensor_tensor(out, in0, scalar, in1,
                                                  op0, op1), inc)

    def tensor_reduce(self, out, in_, axis, op, apply_absolute_value=None,
                      inc=None):
        return self._emit(
            [_apname(in_)], [_apname(out)],
            lambda: self.raw.tensor_reduce(
                out, in_, axis, op,
                apply_absolute_value=apply_absolute_value), inc)

    def reciprocal(self, out, in_, inc=None):
        return self._emit([_apname(in_)], [_apname(out)],
                          lambda: self.raw.reciprocal(out, in_), inc)

    def activation(self, out, in_, func, bias=0.0, scale=1.0, accum_out=None,
                   inc=None):
        outs = [_apname(out), _apname(accum_out)]
        kw = {"bias": bias, "scale": scale}
        if accum_out is not None:
            kw["accum_out"] = accum_out
        return self._emit([_apname(in_), _apname(bias), _apname(scale)], outs,
                          lambda: self.raw.activation(out, in_, func, **kw),
                          inc)

    def copy(self, out, in_, inc=None):
        return self._emit([_apname(in_)], [_apname(out)],
                          lambda: self.raw.copy(out, in_), inc)


def build_program():
    nc = bass.Bass("TRN2", target_bir_lowering=False)
    st = contextlib.ExitStack()

    def din(name, shape, dtype=F32):
        return nc.dram_tensor(name, shape, dtype, kind="ExternalInput").ap()

    x_own = din("x_own", [D, T])
    x_part = din("x_part", [D, T])
    sgate = din("sgate", [P, 1])
    wq0 = din("wq0", [D, 512])
    wkv0 = din("wkv0", [D, 128])
    wo0 = din("wo0", [512, D])
    nvec0 = din("nvec0", [P, 4 * NCH])
    wg_sl = din("wg_sl", [384, D])
    wu_sl = din("wu_sl", [384, D])
    wd_sl = din("wd_sl", [P, FF0])
    wq1 = din("wq1", [D, D])
    wk1 = din("wk1", [D, D])
    wv1 = din("wv1", [D, D])
    wg1 = din("wg1", [D, D])
    wo1 = din("wo1", [D, D])
    nvec1 = din("nvec1", [P, 4 * NCH])
    rw = din("rw", [D, 8])
    rbias = din("rbias", [1, 8])
    upT = din("upT", [8, 32, D, P])
    dnT = din("dnT", [8, 8, FFE, P])
    identb = din("identb", [P, P], BF16)
    identf = din("identf", [P, P])
    gpos = din("gpos", [D, CH])
    gdecC = din("gdecC", [D, CH])
    mask3 = din("mask3", [NH1, CH, CH])
    gchunk = din("gchunk", [NH1, 16])

    y_out = nc.dram_tensor("y", [D, T], F32, kind="ExternalOutput").ap()
    dbg_out = (nc.dram_tensor("dbg", [T, D], F32, kind="ExternalOutput").ap()
               if TAP2 else None)

    wq_contrib = nc.dram_tensor("wq_contrib", [WQC], BF16, kind="Internal").ap()
    wq_all = nc.dram_tensor("wq_all", [8, WQC], BF16, kind="Internal").ap()
    st_contrib = nc.dram_tensor("st_contrib", [NH1 * DH * DH], F32,
                                kind="Internal").ap()
    wgq_c = nc.dram_tensor("wgq_c", [FF0, D], BF16, kind="Internal").ap()
    wuq_c = nc.dram_tensor("wuq_c", [FF0, D], BF16, kind="Internal").ap()
    wdq_c = nc.dram_tensor("wdq_c", [D, FF0], BF16, kind="Internal").ap()
    hq_dram = nc.dram_tensor("hq_dram", [T, FF0], BF16, kind="Internal").ap()
    xq_dram = nc.dram_tensor("xq_dram", [NCH, T // P, P, P], BF16,
                             kind="Internal").ap()
    ret_dram = nc.dram_tensor("ret_dram", [D, T], BF16, kind="Internal").ap()
    st_all = nc.dram_tensor("st_all", [2, NH1 * DH * DH], F32,
                            kind="Internal").ap()

    with st:
        def sb(name, shape, dt=F32, stk=st):
            return stk.enter_context(nc.sbuf_tensor(name, shape, dt))

        def psum(name, shape, dt=F32, stk=st):
            return stk.enter_context(nc.psum_tensor(name, shape, dt))

        X = sb("X", [P, NCH * T])
        XN2 = sb("XN2", [P, NCH * T], BF16)
        nv0 = sb("nv0", [P, 4 * NCH])
        nv1 = sb("nv1", [P, 4 * NCH])
        rbt = sb("rbt", [1, 8])
        sgt = sb("sgt", [P, 1])
        idb = sb("idb", [P, P], BF16)
        idf = sb("idf", [P, P])
        ones_st = sb("ones_st", [P, 1], BF16)
        ones_rep = sb("ones_rep", [1, P])
        R = [sb(f"R{i}", [1, 512]) for i in range(6)]
        TMP0 = sb("TMP0", [P, 512], F32)
        rows_stk = contextlib.ExitStack()
        mu_r = sb("mu_r", [1, T], F32, rows_stk)
        r_r = sb("r_r", [1, T], F32, rows_stk)

        S = Sems(nc, st, ["LD", "LDG", "TS", "VS", "SS", "GS", "ES", "OS", "OS2", "CC", "LD2", "LG2", "PH"])
        AV = AutoEng(S, "VS")
        ASS = AutoEng(S, "SS")
        blk = st.enter_context(nc.Block())

        def row_rsqrt(a_row, out_row, t_row, n=512):
            def ap(x):
                return x[:, 0:n] if hasattr(x, "ap") is False or True else x
            a, o, tr = a_row[:, 0:n], out_row[:, 0:n], t_row[:, 0:n]

            @blk.scalar
            def _(s):
                s = ASS.bind(s)
                S.wait(s, "VS")
                i = s.activation(tr, a, Act.Sqrt)

            @blk.vector
            def _(v):
                v = AV.bind(v)
                S.wait(v, "SS")
                v.reciprocal(o, tr)
                v.tensor_tensor(tr, o, o, Alu.mult)
                v.tensor_tensor(tr, tr, a, Alu.mult)
                v.tensor_scalar(tr, tr, -0.5, 1.5, Alu.mult, Alu.add)
                i = v.tensor_tensor(o, o, tr, Alu.mult)

        p0 = contextlib.ExitStack()
        xbfb = [sb(f"xbfb{i}", [P, NCH * 512], BF16, p0) for i in range(2)]
        rho_r = sb("rho_r", [1, T], F32, p0)
        xsqb = [sb(f"xsqb{i}", [P, NCH * 512], BF16, p0) for i in range(2)]
        psums = [psum(f"p0sum{i}", [1, 512], F32, p0) for i in range(2)]
        psumq = [psum(f"p0sq{i}", [1, 512], F32, p0) for i in range(2)]
        prep = [psum(f"p0rep{i}", [P, 512], F32, p0) for i in range(4)]

        @blk.sync
        def _(q):
            for c in range(NCH):
                i = q.dma_start(out=X[:, c * T:(c + 1) * T],
                                in_=x_own[c * P:(c + 1) * P, :])
                S.inc(i, "LD", 16)
            for dst, src in ((nv0, nvec0), (nv1, nvec1), (rbt, rbias),
                             (sgt, sgate), (idb, identb), (idf, identf)):
                i = q.dma_start(out=dst[:], in_=src[:])
                S.inc(i, "LD", 16)
        ld_x = S.n["LD"]

        vs_copy = []
        ss_sq = []
        ts_stat = []
        ts0_base = S.n["TS"]

        def ts_stat_pred(sl):
            return ts0_base + sl + 1

        @blk.vector
        def _(v):
            v = AV.bind(v)
            v.memset(ones_st[:], 1.0)
            i = v.memset(ones_rep[:], 1.0)
            S.wait(v, "LD", ld_x)
            for sl in range(4):
                if sl >= 2:
                    v.wait_ge(S.h["TS"], ts_stat_pred(sl - 2))
                for c in range(NCH):
                    o = c * T + sl * 512
                    i = v.tensor_copy(xbfb[sl % 2][:, c * 512:(c + 1) * 512],
                                      X[:, o:o + 512])
                vs_copy.append(S.n["VS"])

        @blk.scalar
        def _(s):
            s = ASS.bind(s)
            S.wait(s, "LD", ld_x)
            for sl in range(4):
                if sl >= 2:
                    s.wait_ge(S.h["TS"], ts_stat_pred(sl - 2))
                for c in range(NCH):
                    o = c * T + sl * 512
                    i = s.activation(xsqb[sl % 2][:, c * 512:(c + 1) * 512],
                                     X[:, o:o + 512], Act.Square)
                ss_sq.append(S.n["SS"])

        p0_freed = []
        for sl in range(4):
            o = sl * 512

            @blk.tensor
            def _(t):
                t.wait_ge(S.h["VS"], vs_copy[sl])
                t.wait_ge(S.h["SS"], ss_sq[sl])
                if sl >= 2:
                    t.wait_ge(S.h["VS"], p0_freed[sl - 2])
                for c in range(NCH):
                    i = t.matmul(psums[sl % 2][:], ones_st[:],
                                 xbfb[sl % 2][:, c * 512:(c + 1) * 512],
                                 start=(c == 0), stop=(c == NCH - 1))
                    i = t.matmul(psumq[sl % 2][:], ones_st[:],
                                 xsqb[sl % 2][:, c * 512:(c + 1) * 512],
                                 start=(c == 0), stop=(c == NCH - 1))
                S.inc(i, "TS")
                ts_stat.append(S.n["TS"])

            @blk.vector
            def _(v):
                v = AV.bind(v)
                v.wait_ge(S.h["TS"], ts_stat[sl])
                v.tensor_scalar(mu_r[:, o:o + 512], psums[sl % 2][:], 1.0 / D,
                                None, Alu.mult)
                v.tensor_scalar(R[0][:], psumq[sl % 2][:], 1.0 / D,
                                None, Alu.mult)
                p0_freed.append(S.n["VS"])
                v.tensor_tensor(R[1][:], mu_r[:, o:o + 512],
                                mu_r[:, o:o + 512], Alu.mult)
                v.tensor_tensor(R[0][:], R[0][:], R[1][:], Alu.subtract)
                i = v.tensor_scalar(R[1][:], R[0][:], EPS0, None, Alu.add)

            row_rsqrt(R[1], r_r[:, o:o + 512], R[2], n=512)

            @blk.vector
            def _(v):
                v = AV.bind(v)
                rr = r_r[:, o:o + 512]
                v.tensor_tensor(R[2][:], rr, rr, Alu.mult)
                v.tensor_tensor(R[1][:], R[0][:], R[2][:], Alu.mult)
                i = v.tensor_scalar(R[1][:], R[1][:], EPS0, None, Alu.add)

            row_rsqrt(R[1], R[3], R[2], n=512)

            @blk.vector
            def _(v):
                v = AV.bind(v)
                i = v.tensor_tensor(rho_r[:, o:o + 512], R[3][:],
                                    r_r[:, o:o + 512], Alu.mult)
        vs_rho = S.n["VS"]

        ts_rep = []
        xn2_freed = []
        for sl in range(4):
            o = sl * 512
            b = (sl % 2) * 2

            @blk.tensor
            def _(t):
                S.wait(t, "VS")
                if sl >= 2:
                    t.wait_ge(S.h["VS"], xn2_freed[sl - 2])
                i = t.matmul(prep[b][:], ones_rep[:], mu_r[:, o:o + 512],
                             start=True, stop=True)
                i = t.matmul(prep[b + 1][:], ones_rep[:], rho_r[:, o:o + 512],
                             start=True, stop=True)
                S.inc(i, "TS")
                ts_rep.append(S.n["TS"])

            @blk.vector
            def _(v):
                v = AV.bind(v)
                v.wait_ge(S.h["TS"], ts_rep[sl])
                for c in range(NCH):
                    xo = c * T + o
                    v.tensor_tensor(TMP0[:], X[:, xo:xo + 512], prep[b][:],
                                    Alu.subtract)
                    v.tensor_tensor(TMP0[:], TMP0[:], prep[b + 1][:], Alu.mult)
                    i = v.scalar_tensor_tensor(
                        XN2[:, xo:xo + 512], TMP0[:], nv0[:, c:c + 1],
                        nv0[:, NCH + c:NCH + c + 1].broadcast_to((P, 512)),
                        Alu.mult, Alu.add)
                xn2_freed.append(S.n["VS"])

                p0.close()
        g_p0 = {k: S.n[k] for k in ("TS", "VS", "SS")}

        class StopEmission(Exception):
            pass

        open_stacks = []

        if PHASES >= 2:
          p2 = contextlib.ExitStack()
          open_stacks.append(p2)
          def mk_qscratch(stk, qmax, tag):
              return {
                  "QA": sb(f"QA{tag}", [P, qmax], F32, stk),
                  "VVq": sb(f"VVq{tag}", [P, qmax], F32, stk),
                  "MBq": sb(f"MBq{tag}", [P, qmax // 16], F32, stk),
                  "MB2q": sb(f"MB2q{tag}", [P, qmax // 16], F32, stk),
                  "QB1": sb(f"QB1{tag}", [P, qmax], BF16, stk),
                  "QB2": sb(f"QB2{tag}", [P, qmax], BF16, stk),
                  "QB3": sb(f"QB3{tag}", [P, qmax], BF16, stk),
                  "prev": [None],
              }

          def quant_tile(qs, src, dst, W, src_wait, scale_col=None,
                         extra_wait=()):
              QA, VVq, MBq, MB2q, QB1, QB2, QB3, _qt_prev = (
                  qs["QA"], qs["VVq"], qs["MBq"], qs["MB2q"], qs["QB1"],
                  qs["QB2"], qs["QB3"], qs["prev"])
              """fp4 quant-dequant: src f32 [128 x W] -> dst bf16 [128 x W].
              src_wait: (semname, val) marking src ready (for S and V).
              extra_wait: additional (semname, val) waits on V (e.g. dst WAR)."""
              if not hasattr(src, "tensor"):
                  src = src[:]
              if not hasattr(dst, "tensor"):
                  dst = dst[:]
              nb = W // 16

              @blk.scalar
              def _(s):
                  s = ASS.bind(s)
                  if src_wait is not None:
                      s.wait_ge(S.h[src_wait[0]], src_wait[1])
                  if _qt_prev[0] is not None:
                      s.wait_ge(S.h["VS"], _qt_prev[0])
                  s.activation(QA[:, 0:W], src, Act.Abs)
              ss_abs = S.n["SS"]

              @blk.vector
              def _(v):
                  v = AV.bind(v)
                  if src_wait is not None:
                      v.wait_ge(S.h[src_wait[0]], src_wait[1])
                  for nm, val in extra_wait:
                      v.wait_ge(S.h[nm], val)
                  v.wait_ge(S.h["SS"], ss_abs)
                  qa3 = QA[:, 0:W].rearrange("p (b s) -> p b s", s=16)
                  v.tensor_reduce(MBq[:, 0:nb], qa3, AX.X, Alu.max)
                  v.tensor_scalar(MBq[:, 0:nb], MBq[:, 0:nb], 6e-12, None,
                                  Alu.max)
                  v.reciprocal(MB2q[:, 0:nb], MBq[:, 0:nb])
                  v.tensor_scalar(MB2q[:, 0:nb], MB2q[:, 0:nb], 6.0, None,
                                  Alu.mult)
                  bc = MB2q[:, 0:nb].unsqueeze(2).broadcast_to((P, nb, 16))
                  vv3 = VVq[:, 0:W].rearrange("p (b s) -> p b s", s=16)
                  v.tensor_tensor(vv3, qa3, bc, Alu.mult)
                  VV = VVq[:, 0:W]
                  b1, b2, b3 = QB1[:, 0:W], QB2[:, 0:W], QB3[:, 0:W]
                  v.tensor_scalar(b1, VV, 0.25, None, Alu.is_gt)
                  for th in (0.75, 1.25, 1.75):
                      v.tensor_scalar(b2, VV, th, None, Alu.is_gt)
                      v.tensor_tensor(b1, b1, b2, Alu.add)
                  v.tensor_scalar(b2, VV, 2.5, None, Alu.is_gt)
                  v.tensor_scalar(b3, VV, 3.5, None, Alu.is_gt)
                  v.tensor_tensor(b2, b2, b3, Alu.add)
                  v.scalar_tensor_tensor(b3, b1, 0.5, b2, Alu.mult, Alu.add)
                  v.tensor_scalar(b1, VV, 5.0, None, Alu.is_gt)
                  v.scalar_tensor_tensor(b2, b1, 2.0, b3, Alu.mult, Alu.add)
                  if scale_col is None:
                      v.tensor_scalar(MB2q[:, 0:nb], MBq[:, 0:nb], 1.0 / 6.0,
                                      None, Alu.mult)
                  else:
                      v.tensor_scalar(MB2q[:, 0:nb], MBq[:, 0:nb], scale_col,
                                      1.0 / 6.0, Alu.mult, Alu.mult)
                  bc2 = MB2q[:, 0:nb].unsqueeze(2).broadcast_to((P, nb, 16))
                  b23 = QB2[:, 0:W].rearrange("p (b s) -> p b s", s=16)
                  v.tensor_tensor(vv3, b23, bc2, Alu.mult)
                  v.tensor_scalar(b1, src, 0.0, None, Alu.is_ge)
                  v.tensor_tensor(QA[:, 0:W], b1, VV, Alu.mult)
                  v.scalar_tensor_tensor(dst, QA[:, 0:W], 2.0, VV,
                                         Alu.mult, Alu.subtract)
              _qt_prev[0] = S.n["VS"]
              return S.n["VS"]

          qs1 = mk_qscratch(p2, FF0 // 2, "a")

          NW = 6 * 1024 + FF0
          p2a = contextlib.ExitStack()
          open_stacks.append(p2a)
          WSL = sb("WSL", [P, 4864], F32, p2a)
          WQB = sb("WQB", [P, NW], BF16, p2a)

          @blk.sync
          def _(q):
              q.wait_ge(S.h["LD"], S.n["LD"])
              q.wait_ge(S.h["TS"], g_p0["TS"])
              q.wait_ge(S.h["VS"], g_p0["VS"])
              q.wait_ge(S.h["SS"], g_p0["SS"])
              for j in range(3):
                  i = q.dma_start(out=WSL[:, j * 1024:(j + 1) * 1024],
                                  in_=wg_sl[j * P:(j + 1) * P, :])
                  S.inc(i, "LD", 16)
              i = q.dma_start(out=WSL[:, 3 * 1024:4 * 1024],
                              in_=wu_sl[0:P, :])
              S.inc(i, "LD", 16)
          ld_wsl = S.n["LD"]

          wq_marks = []
          for j in range(4):
              wq_marks.append(quant_tile(qs1, WSL[:, j * 1024:(j + 1) * 1024],
                                         WQB[:, j * 1024:(j + 1) * 1024],
                                         1024, ("LD", ld_wsl)))

          @blk.sync
          def _(q):
              q.wait_ge(S.h["LD"], S.n["LD"])
              q.wait_ge(S.h["VS"], wq_marks[-1])
              for j in range(2):
                  i = q.dma_start(out=WSL[:, j * 1024:(j + 1) * 1024],
                                  in_=wu_sl[(j + 1) * P:(j + 2) * P, :])
                  S.inc(i, "LD", 16)
              i = q.dma_start(out=WSL[:, 2 * 1024:2 * 1024 + FF0],
                              in_=wd_sl[:])
              S.inc(i, "LD", 16)
          ld_wsl2 = S.n["LD"]

          for j in range(2):
              wq_marks.append(quant_tile(qs1, WSL[:, j * 1024:(j + 1) * 1024],
                                         WQB[:, (4 + j) * 1024:(5 + j) * 1024],
                                         1024, ("LD", ld_wsl2)))
          for hh in range(2):
              wq_marks.append(quant_tile(
                  qs1,
                  WSL[:, 2 * 1024 + hh * 1408:2 * 1024 + (hh + 1) * 1408],
                  WQB[:, 6 * 1024 + hh * 1408:6 * 1024 + (hh + 1) * 1408],
                  1408, ("LD", ld_wsl2)))

          ldg_contrib = []

          @blk.gpsimd
          def _(g):
              g.wait_ge(S.h["LDG"], S.n["LDG"])
              g.wait_ge(S.h["VS"], wq_marks[-1])
              ofs = 0
              for j in range(7):
                  w = 1024 if j < 6 else FF0
                  i = g.dma_start(
                      out=wq_contrib[ofs * P:(ofs + w) * P].rearrange(
                          "(a b) -> a b", a=P),
                      in_=WQB[:, ofs:ofs + w])
                  S.inc(i, "LDG", 16)
                  ofs += w
              g.wait_ge(S.h["LDG"], S.n["LDG"])
              ldg_contrib.append(S.n["LDG"])
              i = g.collective_compute(
                  "AllGather", Alu.bypass,
                  replica_groups=[[0, 1, 2, 3, 4, 5, 6, 7]],
                  ins=[wq_contrib[:]], outs=[wq_all[:]])
              S.inc(i, "CC", 1)
          p2a.close()
          open_stacks.remove(p2a)

          MUCOL = sb("MUCOL", [P, 16], F32, p2)
          RCOL = sb("RCOL", [P, 16], F32, p2)
          XQF_st = sb("XQF_st", [P, D], BF16, p2)
          p2b = contextlib.ExitStack()
          open_stacks.append(p2b)
          pcolm = psum("pcolm", [P, 16], F32, p2b)
          pcolr = psum("pcolr", [P, 16], F32, p2b)

          @blk.tensor
          def _(t):
              t.wait_ge(S.h["VS"], g_p0["VS"])
              t.wait_ge(S.h["SS"], g_p0["SS"])
              t.wait_ge(S.h["LDG"], ldg_contrib[0])
              for tc in range(16):
                  i = t.transpose(pcolm[:, tc:tc + 1],
                                  mu_r[0:1, tc * P:(tc + 1) * P],
                                  idf[0:1, 0:1])
                  i = t.transpose(pcolr[:, tc:tc + 1],
                                  r_r[0:1, tc * P:(tc + 1) * P],
                                  idf[0:1, 0:1])
              S.inc(i, "TS")
          ts_col = S.n["TS"]

          @blk.vector
          def _(v):
              v = AV.bind(v)
              v.wait_ge(S.h["TS"], ts_col)
              v.tensor_copy(MUCOL[:], pcolm[:])
              v.tensor_copy(RCOL[:], pcolr[:])
          vs_col = S.n["VS"]

          XCt = sb("XCt", [P, D], F32, p2b)
          XQt = sb("XQt", [P, D], BF16, p2b)
          pxt = [psum(f"pxt{i}", [P, P], F32, p2b) for i in range(2)]
          pqx = [psum(f"pqx{i}", [P, P], BF16, p2b) for i in range(2)]

          xtr = {}
          xct_dumped_prev = [0]
          xqst_mark = {}
          gs_b = S.n["GS"]

          def gs_xct(tc, c):
              return gs_b + tc * 16 + c + 1

          def gs_xqf(tc, c):
              return gs_b + tc * 16 + 8 + c + 1

          qm_prev = [None]
          for tc in range(16):
              @blk.tensor
              def _(t):
                  if tc == 0:
                      t.wait_ge(S.h["VS"], vs_col)
                  for c in range(NCH):
                      gidx = tc * NCH + c
                      if gidx >= 2:
                          pi = gidx - 2
                          t.wait_ge(S.h["GS"], gs_xct(pi // NCH, pi % NCH))
                      i = t.transpose(pxt[c % 2][:],
                                      X[:, c * T + tc * P:c * T + (tc + 1) * P],
                                      idf[:])
                      S.inc(i, "TS")
                      xtr[tc * NCH + c] = S.n["TS"]

              @blk.vector
              def _(v):
                  v = AV.bind(v)
                  if TAP2 == "xct" and tc >= 1:
                      v.wait_ge(S.h["LDG"], xct_dumped_prev[0])
                  for c in range(NCH):
                      v.wait_ge(S.h["TS"], xtr[tc * NCH + c])
                      v.tensor_scalar(XCt[:, c * P:(c + 1) * P], pxt[c % 2][:],
                                      MUCOL[:, tc:tc + 1], None, Alu.subtract,
                                      inc="GS")
              xct_done = ("GS", S.n["GS"])

              if TAP2 == "xct":
                  @blk.gpsimd
                  def _(g):
                      g.wait_ge(S.h["LDG"], S.n["LDG"])
                      g.wait_ge(S.h["GS"], xct_done[1])
                      i = g.dma_start(out=dbg_out[tc * P:(tc + 1) * P, :],
                                      in_=XCt[:])
                      S.inc(i, "LDG", 16)
                  xct_dumped_prev[0] = S.n["LDG"]

              ew = ()
              if tc >= 1:
                  ew = (("TS", xtr[200 + (tc - 1) * NCH + NCH - 1]),)
              if TAP2 == "xct":
                  ew = ew + (("LDG", xct_dumped_prev[0]),)
              qm = quant_tile(qs1, XCt, XQt, D, xct_done,
                              scale_col=RCOL[:, tc:tc + 1], extra_wait=ew)

              @blk.tensor
              def _(t):
                  t.wait_ge(S.h["VS"], qm)
                  for c in range(NCH):
                      gidx = tc * NCH + c
                      if gidx >= 2:
                          pi = gidx - 2
                          t.wait_ge(S.h["GS"], gs_xqf(pi // NCH, pi % NCH))
                      i = t.transpose(pqx[c % 2][:],
                                      XQt[:, c * P:(c + 1) * P], idb[:])
                      S.inc(i, "TS")
                      xtr[200 + tc * NCH + c] = S.n["TS"]

              @blk.vector
              def _(v):
                  v = AV.bind(v)
                  if tc >= 1 and "noxqst" not in DBG:
                      v.wait_ge(S.h["LDG"], xqst_mark[tc - 1])
                  for c in range(NCH):
                      v.wait_ge(S.h["TS"], xtr[200 + tc * NCH + c])
                      v.tensor_copy(XQF_st[:, c * P:(c + 1) * P],
                                    pqx[c % 2][:], inc="GS")

              @blk.gpsimd
              def _(g):
                  if "noxqst" in DBG:
                      return
                  g.wait_ge(S.h["LDG"], S.n["LDG"])
                  g.wait_ge(S.h["GS"], gs_xqf(tc, NCH - 1))
                  i = g.dma_start(
                      out=xq_dram[:, tc].rearrange("c p j -> p c j"),
                      in_=XQF_st[:].rearrange("p (c j) -> p c j", j=P))
                  S.inc(i, "LDG", 16)
              xqst_mark[tc] = S.n["LDG"]
          xq_st_done = S.n["LDG"]
          p2b.close()
          open_stacks.remove(p2b)
          p2.close()
          open_stacks.remove(p2)

        g_p0b = {k: S.n[k] for k in ("TS", "VS", "SS")}
        gs_p1v = S.n["GS"]

        try:
          if PHASES >= 1:
            p1 = contextlib.ExitStack()
            open_stacks.append(p1)
            k_ext = sb("k_ext", [65, SEQ], BF16, p1)
            vexE = sb("vexE", [P, 32 * 65], BF16, p1)
            vexO = sb("vexO", [P, 32 * 128], BF16, p1)
            AO = sb("AO", [P, 4 * T], BF16, p1)
            KMX = sb("KMX", [65, 4], F32, p1)
            CT = sb("CT", [65, 512], F32, p1)
            E65 = sb("E65", [P, 65], BF16, p1)
            E65h = sb("E65h", [P, 65], BF16, p1)
            ones64 = sb("ones64", [65, P], F32, p1)
            PM, PR, PS1, PS2, PS3 = R[4], R[5], R[1], R[2], R[3]

            @blk.vector
            def _(v):
                v = AV.bind(v)
                v.wait_ge(S.h["TS"], g_p0b["TS"])
                v.wait_ge(S.h["SS"], g_p0b["SS"])
                v.memset(k_ext[64:65, :], 1.0)
                v.memset(vexE[:], 0.0)
                v.memset(vexO[:], 0.0)
                for kc in range(32):
                    v.memset(vexE[:, kc * 65 + 64:kc * 65 + 65], 1.0)
                    v.memset(vexO[:, kc * 128:kc * 128 + 1], 1.0)
                v.memset(E65[:], 0.0)
                v.memset(E65h[:], 0.0)
                v.memset(E65[0:64, 64:65], 1.0)
                v.memset(E65h[64:128, 64:65], 1.0)
                i = v.memset(ones64[64:65, :], 1.0)

            a1 = contextlib.ExitStack()
            open_stacks.append(a1)
            pbf = sb("pbf", [P, NCH * 512], BF16, a1)
            pxn2 = sb("pxn2", [P, NCH * 512], BF16, a1)
            psq2 = pxn2
            VTb = sb("VTb", [64, 512], BF16, a1)
            ksqb = sb("ksqb", [64, 512], BF16, a1)
            wkv0b = sb("wkv0b", [P, NCH * 128], BF16, a1)

            @blk.gpsimd
            def _(g):
                g.wait_ge(S.h["LDG"], S.n["LDG"])
                g.wait_ge(S.h["TS"], g_p0b["TS"])
                g.wait_ge(S.h["VS"], g_p0b["VS"])
                g.wait_ge(S.h["SS"], g_p0b["SS"])
                for c in range(NCH):
                    i = g.dma_start(out=wkv0b[:, c * 128:(c + 1) * 128],
                                    in_=wkv0[c * P:(c + 1) * P, :])
                    S.inc(i, "LDG", 16)
            ldg_wkv = S.n["LDG"]
            pst = [psum(f"pst{i}", [1, 512], F32, a1) for i in range(2)]
            prp = [psum(f"prp{i}", [P, 512], F32, a1) for i in range(2)]
            pkv = [psum(f"pkv{i}", [P, 512], F32, a1) for i in range(2)]
            pkn = psum("pkn", [65, 512], F32, a1)
            pvt = psum("pvt", [P, 64], BF16, a1)

            kvmm = {}
            knmm = {}
            vtmm = {}
            kev = {}
            vcp = {}
            kmaxv = {}
            pbuffree = {}
            affdone = {}
            sqd = {}

            for sl in range(8):
                own = sl < 4
                if not own:
                    sp = sl - 4

                    @blk.gpsimd
                    def _(g):
                        g.wait_ge(S.h["LDG"], S.n["LDG"])
                        if sl > 4:
                            S.wait(g, "VS", affdone[sl - 1])
                        for c in range(NCH):
                            i = g.dma_start(
                                out=pbf[:, c * 512:(c + 1) * 512],
                                in_=x_part[c * P:(c + 1) * P,
                                           sp * 512:(sp + 1) * 512])
                            S.inc(i, "LDG", 16)
                    ld_pb = S.n["LDG"]
                    vs_pbf = S.n["VS"]

                    @blk.scalar
                    def _(s):
                        s = ASS.bind(s)
                        if sl == 4:
                            s.wait_ge(S.h["TS"], g_p0b["TS"])
                            s.wait_ge(S.h["VS"], g_p0b["VS"])
                        S.wait(s, "LDG", ld_pb)
                        if sl > 4:
                            s.wait_ge(S.h["TS"], kvmm[sl - 1])
                        for c in range(NCH):
                            i = s.activation(psq2[:, c * 512:(c + 1) * 512],
                                             pbf[:, c * 512:(c + 1) * 512],
                                             Act.Square)
                    ss_psq = S.n["SS"]

                    @blk.tensor
                    def _(t):
                        t.wait_ge(S.h["LDG"], ld_pb)
                        t.wait_ge(S.h["SS"], ss_psq)
                        for c in range(NCH):
                            i = t.matmul(pst[0][:], ones_st[:],
                                         pbf[:, c * 512:(c + 1) * 512],
                                         start=(c == 0), stop=(c == NCH - 1))
                            i = t.matmul(pst[1][:], ones_st[:],
                                         psq2[:, c * 512:(c + 1) * 512],
                                         start=(c == 0), stop=(c == NCH - 1))
                        S.inc(i, "TS")
                    ts_pstat = S.n["TS"]

                    @blk.vector
                    def _(v):
                        v = AV.bind(v)
                        v.wait_ge(S.h["TS"], ts_pstat)
                        v.tensor_scalar(PM[:], pst[0][:], 1.0 / D, None, Alu.mult)
                        v.tensor_scalar(PS1[:], pst[1][:], 1.0 / D, None, Alu.mult)
                        v.tensor_tensor(PS2[:], PM[:], PM[:], Alu.mult)
                        v.tensor_tensor(PS1[:], PS1[:], PS2[:], Alu.subtract)
                        i = v.tensor_scalar(PS1[:], PS1[:], EPS0, None, Alu.add)

                    row_rsqrt(PS1, PR, PS2, n=512)

                    @blk.vector
                    def _(v):
                        v = AV.bind(v)
                        v.tensor_scalar(PS1[:], PS1[:], -EPS0, None, Alu.add)
                        v.tensor_tensor(PS2[:], PR[:], PR[:], Alu.mult)
                        v.tensor_tensor(PS1[:], PS1[:], PS2[:], Alu.mult)
                        i = v.tensor_scalar(PS1[:], PS1[:], EPS0, None, Alu.add)

                    row_rsqrt(PS1, PS2, PS3, n=512)

                    @blk.vector
                    def _(v):
                        v = AV.bind(v)
                        i = v.tensor_tensor(PR[:], PR[:], PS2[:], Alu.mult)
                    vs_prho = S.n["VS"]

                    @blk.tensor
                    def _(t):
                        t.wait_ge(S.h["VS"], vs_prho)
                        i = t.matmul(prp[0][:], ones_rep[:], PM[:],
                                     start=True, stop=True)
                        i = t.matmul(prp[1][:], ones_rep[:], PR[:],
                                     start=True, stop=True)
                        S.inc(i, "TS")
                    ts_prep = S.n["TS"]

                    @blk.vector
                    def _(v):
                        v = AV.bind(v)
                        v.wait_ge(S.h["TS"], ts_prep)
                        v.wait_ge(S.h["TS"], ts_pstat)
                        for c in range(NCH):
                            o = c * 512
                            v.tensor_tensor(TMP0[:], pbf[:, o:o + 512],
                                            prp[0][:], Alu.subtract)
                            v.tensor_tensor(TMP0[:], TMP0[:], prp[1][:],
                                            Alu.mult)
                            i = v.scalar_tensor_tensor(
                                pxn2[:, o:o + 512], TMP0[:], nv0[:, c:c + 1],
                                nv0[:, NCH + c:NCH + c + 1].broadcast_to(
                                    (P, 512)), Alu.mult, Alu.add)
                    affdone[sl] = S.n["VS"]

                @blk.tensor
                def _(t):
                    if own:
                        t.wait_ge(S.h["VS"], S.n["VS"])
                        t.wait_ge(S.h["LDG"], ldg_wkv)
                    else:
                        t.wait_ge(S.h["VS"], affdone[sl])
                    if sl >= 2:
                        t.wait_ge(S.h["VS"], kev[sl - 2])
                        t.wait_ge(S.h["SS"], sqd[sl - 2])
                    for c in range(NCH):
                        if own:
                            mv = XN2[:, c * T + sl * 512:c * T + sl * 512 + 512]
                        else:
                            mv = pxn2[:, c * 512:(c + 1) * 512]
                        i = t.matmul(pkv[sl % 2][:],
                                     wkv0b[:, c * 128:(c + 1) * 128], mv,
                                     start=(c == 0), stop=(c == NCH - 1))
                    S.inc(i, "TS")
                kvmm[sl] = S.n["TS"]

                @blk.vector
                def _(v):
                    v = AV.bind(v)
                    v.wait_ge(S.h["TS"], kvmm[sl])
                    if sl >= 1:
                        v.wait_ge(S.h["TS"], vtmm[(sl - 1, 3)])
                    i = v.tensor_copy(k_ext[0:64, sl * 512:(sl + 1) * 512],
                                      pkv[sl % 2][0:64, :])
                    i = v.tensor_copy(VTb[:], pkv[sl % 2][64:128, :])
                kev[sl] = S.n["VS"]

                @blk.scalar
                def _(s):
                    s = ASS.bind(s)
                    s.wait_ge(S.h["VS"], kev[sl])
                    if sl >= 1:
                        s.wait_ge(S.h["TS"], knmm[sl - 1])
                    i = s.activation(ksqb[:],
                                     k_ext[0:64, sl * 512:(sl + 1) * 512],
                                     Act.Square)
                sqd[sl] = S.n["SS"]


                @blk.tensor
                def _(t):
                    t.wait_ge(S.h["SS"], sqd[sl])
                    if sl >= 1:
                        t.wait_ge(S.h["VS"], kmaxv[sl - 1])
                    i = t.matmul(pkn[:], E65[0:64, :], ksqb[:],
                                 start=True, stop=True)
                    S.inc(i, "TS")
                    knmm[sl] = S.n["TS"]
                    t.wait_ge(S.h["VS"], kev[sl])
                    for j in range(4):
                        if (sl, j) != (0, 0):
                            t.wait_ge(S.h["GS"], gs_p1v + 4 * sl + j)
                        i = t.transpose(pvt[:], VTb[:, j * 128:(j + 1) * 128],
                                        idb[0:64, 0:64])
                        S.inc(i, "TS")
                        vtmm[(sl, j)] = S.n["TS"]

                @blk.vector
                def _(v):
                    v = AV.bind(v)
                    v.wait_ge(S.h["TS"], knmm[sl])
                    v.tensor_reduce(KMX[64:65, 1:2], pkn[64:65, :], AX.X,
                                    Alu.max)
                    if sl == 0:
                        i = v.tensor_copy(KMX[64:65, 0:1], KMX[64:65, 1:2])
                    else:
                        i = v.tensor_tensor(KMX[64:65, 0:1], KMX[64:65, 0:1],
                                            KMX[64:65, 1:2], Alu.max)
                    kmaxv[sl] = S.n["VS"]
                    for j in range(4):
                        kc = sl * 4 + j
                        v.wait_ge(S.h["TS"], vtmm[(sl, j)])
                        v.tensor_copy(vexE[:, kc * 65:kc * 65 + 64], pvt[:])
                        i = v.tensor_copy(
                            vexO[:, kc * 128 + 64:(kc + 1) * 128], pvt[:],
                            inc="GS")

            @blk.scalar
            def _(s):
                s = ASS.bind(s)
                S.wait(s, "VS")
                i = s.activation(KMX[64:65, 2:3], KMX[64:65, 0:1], Act.Sqrt)
            ss_kmax = S.n["SS"]
            a1.close()
            open_stacks.remove(a1)
            g_a1 = {k: S.n[k] for k in ("TS", "VS", "SS")}

            if P1STAGE < 1.3:
                raise StopEmission
            a2 = contextlib.ExitStack()
            open_stacks.append(a2)
            wq0b = sb("wq0b", [P, NCH * 512], BF16, a2)
            qex = [sb(f"qex{i}", [65, NH0 * 512], BF16, a2) for i in range(2)]

            @blk.gpsimd
            def _(g):
                g.wait_ge(S.h["LDG"], S.n["LDG"])
                g.wait_ge(S.h["TS"], g_a1["TS"])
                g.wait_ge(S.h["VS"], g_a1["VS"])
                g.wait_ge(S.h["SS"], g_a1["SS"])
                for c in range(NCH):
                    i = g.dma_start(out=wq0b[:, c * 512:(c + 1) * 512],
                                    in_=wq0[c * P:(c + 1) * P, :])
                    S.inc(i, "LDG", 16)
            ldg_wq = S.n["LDG"]
            qsqb = sb("qsqb", [P, 512], BF16, a2)
            expb = [sb(f"expb{i}", [P, 512], BF16, a2) for i in range(2)]
            REPT = TMP0
            AR0 = PS3
            AR64 = sb("AR64", [65, 512], F32, a2)
            pq = [psum(f"pq{i}", [P, 512], F32, a2) for i in range(2)]
            pqn = psum("pqn", [65, 512], F32, a2)
            psim = [psum(f"psim{i}", [P, 512], F32, a2) for i in range(2)]
            pav = [psum(f"pav{i}", [P, 512], F32, a2) for i in range(2)]
            prep1 = psum("prep1", [P, 512], F32, a2)

            qmmv = {}
            qevv = {}
            qsqv = {}
            qnv = {}
            crowv = {}
            ctsq = {}
            simv = {}
            expv = {}
            avv = {}
            repv = {}
            arv = {}
            reptv = {}
            nrmv = {}

            def sh_prev(slab, h, k):
                idx = slab * NH0 + h - k
                if idx < 0:
                    return None
                return (idx // NH0, idx % NH0)

            os_q0 = S.n["OS"]
            os_q0b = S.n["OS2"]
            for slab in (() if "noqmm" in DBG else range(4)):
                for oc in range(4):
                    h_lo, h_hi = 2 * oc, 2 * oc + 1

                    @blk.tensor
                    def _(t):
                        if (slab, oc) == (0, 0):
                            t.wait_ge(S.h["LDG"], ldg_wq)
                            t.wait_ge(S.h["VS"], g_a1["VS"])
                            t.wait_ge(S.h["SS"], g_a1["SS"])
                        if oc == 0 and slab >= 2 and P1STAGE >= 2:
                            t.wait_ge(S.h["TS"], simv[(slab - 2, NH0 - 1, 31)])
                        if (slab, oc) not in ((0, 0), (0, 1)):
                            po = slab * 4 + oc - 2
                            t.wait_ge(S.h["OS"], os_q0 + po + 1)
                        for c in range(NCH):
                            i = t.matmul(
                                pq[oc % 2][:],
                                wq0b[:, c * 512 + oc * 128:
                                     c * 512 + (oc + 1) * 128],
                                XN2[:, c * T + slab * 512:
                                    c * T + (slab + 1) * 512],
                                start=(c == 0), stop=(c == NCH - 1))
                        S.inc(i, "TS")
                        qmmv[(slab, oc)] = S.n["TS"]

                    @blk.vector
                    def _(v):
                        if "noqev" in DBG:
                            return
                        v = AV.bind(v)
                        if (slab, oc) == (0, 0):
                            v.wait_ge(S.h["SS"], g_a1["SS"])
                        v.wait_ge(S.h["TS"], qmmv[(slab, oc)])
                        v.tensor_scalar(
                            qex[slab % 2][0:64, h_lo * 512:(h_lo + 1) * 512],
                            pq[oc % 2][0:64, :], 0.125, None, Alu.mult)
                        i = v.tensor_scalar(
                            qex[slab % 2][0:64, h_hi * 512:(h_hi + 1) * 512],
                            pq[oc % 2][64:128, :], 0.125, None, Alu.mult,
                            inc="OS")

                    for par, h in (((0, h_lo), (1, h_hi)) if P1STAGE >= 1.6 else ()):
                        @blk.scalar
                        def _(s):
                            s = ASS.bind(s)
                            s.wait_ge(S.h["OS"], os_q0 + slab * 4 + oc + 1)
                            pv = sh_prev(slab, h, 1)
                            if pv is not None:
                                s.wait_ge(S.h["TS"], qnv[pv])
                            i = s.activation(qsqb[0:64, :],
                                             qex[slab % 2][0:64,
                                                           h * 512:(h + 1) * 512],
                                             Act.Square, inc="OS2")

                        @blk.tensor
                        def _(t):
                            t.wait_ge(S.h["OS2"],
                                      os_q0b + (slab * NH0 + h) + 1)
                            pv = sh_prev(slab, h, 1)
                            if pv is not None:
                                t.wait_ge(S.h["SS"], ctsq[pv])
                            i = t.matmul(pqn[:], E65[0:64, :], qsqb[0:64, :],
                                         start=True, stop=True)
                            S.inc(i, "TS")
                            qnv[(slab, h)] = S.n["TS"]

                        @blk.scalar
                        def _(s):
                            s = ASS.bind(s)
                            s.wait_ge(S.h["TS"], qnv[(slab, h)])
                            s.wait_ge(S.h["SS"], ss_kmax)
                            pv = sh_prev(slab, h, 1)
                            if pv is not None:
                                s.wait_ge(S.h["VS"], crowv[pv])
                            i = s.activation(CT[64:65, :], pqn[64:65, :],
                                             Act.Sqrt)
                            ctsq[(slab, h)] = S.n["SS"]

                        @blk.vector
                        def _(v):
                            v = AV.bind(v)
                            v.wait_ge(S.h["SS"], ctsq[(slab, h)])
                            v.tensor_scalar(CT[64:65, :], CT[64:65, :],
                                            KMX[64:65, 2:3], -30.0,
                                            Alu.mult, Alu.add)
                            i = v.tensor_scalar(
                                qex[slab % 2][64:65, h * 512:(h + 1) * 512],
                                CT[64:65, :], 0.0, -1.0, Alu.max, Alu.mult)
                            crowv[(slab, h)] = S.n["VS"]

                for h in (range(NH0) if P1STAGE >= 2 else ()):
                    par = h % 2
                    vext = vexE if par == 0 else vexO
                    vw = 65 if par == 0 else 128
                    bav = (slab * NH0 + h) % 2

                    @blk.tensor
                    def _(t):
                        t.wait_ge(S.h["VS"], crowv[(slab, h)])
                        t.wait_ge(S.h["OS"],
                                  os_q0 + slab * 4 + h // 2 + 1)
                        pv = sh_prev(slab, h, 2)
                        if pv is not None:
                            t.wait_ge(S.h["VS"], nrmv[pv])
                        for kc in range(32):
                            eidx = (slab * NH0 + h) * 32 + kc
                            if eidx >= 2:
                                t.wait_ge(S.h["ES"], eidx - 1)
                            i = t.matmul(
                                psim[kc % 2][:],
                                k_ext[0:65, kc * 128:(kc + 1) * 128],
                                qex[slab % 2][0:65, h * 512:(h + 1) * 512],
                                start=True, stop=True)
                            S.inc(i, "TS")
                            simv[(slab, h, kc)] = S.n["TS"]
                            if kc >= 1:
                                t.wait_ge(S.h["ES"],
                                          (slab * NH0 + h) * 32 + kc)
                                i = t.matmul(pav[bav][0:vw, :],
                                             vext[:, (kc - 1) * vw:kc * vw],
                                             expb[(kc - 1) % 2][:],
                                             start=(kc - 1 == 0), stop=False)
                                S.inc(i, "TS")
                                avv[(slab, h, kc - 1)] = S.n["TS"]
                        t.wait_ge(S.h["ES"], (slab * NH0 + h) * 32 + 32)
                        i = t.matmul(pav[bav][0:vw, :], vext[:, 31 * vw:32 * vw],
                                     expb[31 % 2][:], start=False, stop=True)
                        S.inc(i, "TS")
                        avv[(slab, h, 31)] = S.n["TS"]

                    @blk.scalar
                    def _(s):
                        s = ASS.bind(s)
                        for kc in range(32):
                            s.wait_ge(S.h["TS"], simv[(slab, h, kc)])
                            if kc >= 2:
                                s.wait_ge(S.h["TS"], avv[(slab, h, kc - 2)])
                            elif sh_prev(slab, h, 1) is not None:
                                pp = sh_prev(slab, h, 1)
                                s.wait_ge(S.h["TS"], avv[(pp[0], pp[1],
                                                          30 + kc)])
                            i = s.activation(expb[kc % 2][:], psim[kc % 2][:],
                                             Act.Exp, inc="ES")

                    sumrow = 64 if par == 0 else 0
                    arr = AR64[64:65, :] if par == 0 else AR0[:]

                    @blk.vector
                    def _(v):
                        v = AV.bind(v)
                        v.wait_ge(S.h["TS"], avv[(slab, h, 31)])
                        i = v.reciprocal(arr, pav[bav][sumrow:sumrow + 1, :])
                        arv[(slab, h)] = S.n["VS"]

                    @blk.tensor
                    def _(t):
                        t.wait_ge(S.h["VS"], arv[(slab, h)])
                        pv = sh_prev(slab, h, 1)
                        if pv is not None:
                            t.wait_ge(S.h["VS"], reptv[pv])
                        st_ = ones64[64:65, :] if par == 0 else ones_rep[:]
                        i = t.matmul(prep1[:], st_, arr, start=True, stop=True)
                        S.inc(i, "TS")
                        repv[(slab, h)] = S.n["TS"]

                    @blk.vector
                    def _(v):
                        v = AV.bind(v)
                        v.wait_ge(S.h["TS"], repv[(slab, h)])
                        i = v.tensor_copy(REPT[:], prep1[:])
                        reptv[(slab, h)] = S.n["VS"]
                        rows = slice(0, 64) if par == 0 else slice(64, 128)
                        dst = AO[64 * par:64 * par + 64,
                                 (h // 2) * T + slab * 512:
                                 (h // 2) * T + (slab + 1) * 512]
                        i = v.tensor_tensor(dst, pav[bav][rows, :],
                                            REPT[rows, :], Alu.mult)
                        nrmv[(slab, h)] = S.n["VS"]

            a2.close()
            open_stacks.remove(a2)
            g_a2 = {k: S.n[k] for k in ("TS", "VS", "SS")}

            if P1STAGE < 3:
                raise StopEmission
            a4 = contextlib.ExitStack()
            open_stacks.append(a4)
            wo0b = sb("wo0b", [P, 4 * D], BF16, a4)
            y0sl = sb("y0sl", [P, NCH * 512], BF16, a4)

            @blk.gpsimd
            def _(g):
                g.wait_ge(S.h["LDG"], S.n["LDG"])
                g.wait_ge(S.h["TS"], g_a2["TS"])
                g.wait_ge(S.h["VS"], g_a2["VS"])
                g.wait_ge(S.h["SS"], g_a2["SS"])
                for ic in range(4):
                    i = g.dma_start(out=wo0b[:, ic * D:(ic + 1) * D],
                                    in_=wo0[ic * P:(ic + 1) * P, :])
                    S.inc(i, "LDG", 16)
            ldg_wo = S.n["LDG"]
            y0sq = sb("y0sq", [P, NCH * 512], BF16, a4)
            po = [psum(f"po{i}", [P, 512], F32, a4) for i in range(2)]
            pst4 = [psum(f"pst4{i}", [1, 512], F32, a4) for i in range(2)]
            prp4 = [psum(f"prp4{i}", [P, 512], F32, a4) for i in range(2)]

            oev = {}
            osq = {}
            omm = {}
            ost = {}
            orow = {}
            orep = {}
            oadd = {}

            os_y0 = S.n["OS"]
            os_y0b = S.n["OS2"]
            for slab in range(4):
                for oc in range(NCH):
                    @blk.tensor
                    def _(t):
                        if (slab, oc) == (0, 0):
                            t.wait_ge(S.h["LDG"], ldg_wo)
                            t.wait_ge(S.h["VS"], g_a2["VS"])
                            t.wait_ge(S.h["SS"], g_a2["SS"])
                        if (slab, oc) not in ((0, 0), (0, 1)):
                            pidx = slab * NCH + oc - 2
                            t.wait_ge(S.h["OS"], os_y0 + pidx + 1)
                            t.wait_ge(S.h["OS2"], os_y0b + pidx + 1)
                        for ic in range(4):
                            stat = wo0b[:, ic * D + oc * P:
                                        ic * D + (oc + 1) * P]
                            mov = AO[:, ic * T + slab * 512:
                                     ic * T + (slab + 1) * 512]
                            i = t.matmul(po[oc % 2][:], stat, mov,
                                         start=(ic == 0), stop=(ic == 3))
                        S.inc(i, "TS")
                        omm[(slab, oc)] = S.n["TS"]

                    @blk.vector
                    def _(v):
                        v = AV.bind(v)
                        if (slab, oc) == (0, 0):
                            v.wait_ge(S.h["SS"], g_a2["SS"])
                        v.wait_ge(S.h["TS"], omm[(slab, oc)])
                        i = v.tensor_copy(y0sl[:, oc * 512:(oc + 1) * 512],
                                          po[oc % 2][:], inc="OS")

                    @blk.scalar
                    def _(s):
                        s = ASS.bind(s)
                        if (slab, oc) == (0, 0):
                            s.wait_ge(S.h["VS"], g_a2["VS"])
                        s.wait_ge(S.h["OS"], os_y0 + slab * NCH + oc + 1)
                        i = s.activation(y0sq[:, oc * 512:(oc + 1) * 512],
                                         y0sl[:, oc * 512:(oc + 1) * 512],
                                         Act.Square, inc="OS2")

                @blk.tensor
                def _(t):
                    t.wait_ge(S.h["OS"], os_y0 + slab * NCH + NCH)
                    t.wait_ge(S.h["OS2"], os_y0b + slab * NCH + NCH)
                    if slab >= 1:
                        t.wait_ge(S.h["VS"], orow[slab - 1])
                    for c in range(NCH):
                        i = t.matmul(pst4[0][:], ones_st[:],
                                     y0sl[:, c * 512:(c + 1) * 512],
                                     start=(c == 0), stop=(c == NCH - 1))
                        i = t.matmul(pst4[1][:], ones_st[:],
                                     y0sq[:, c * 512:(c + 1) * 512],
                                     start=(c == 0), stop=(c == NCH - 1))
                    S.inc(i, "TS")
                    ost[slab] = S.n["TS"]

                @blk.vector
                def _(v):
                    v = AV.bind(v)
                    v.wait_ge(S.h["TS"], ost[slab])
                    v.tensor_scalar(PM[:], pst4[0][:], 1.0 / D, None, Alu.mult)
                    v.tensor_scalar(PS1[:], pst4[1][:], 1.0 / D, None, Alu.mult)
                    v.tensor_tensor(PS2[:], PM[:], PM[:], Alu.mult)
                    v.tensor_tensor(PS1[:], PS1[:], PS2[:], Alu.subtract)
                    i = v.tensor_scalar(PS1[:], PS1[:], EPS0, None, Alu.add)

                row_rsqrt(PS1, PR, PS2, n=512)

                orow[slab] = S.n["VS"]

                @blk.tensor
                def _(t):
                    t.wait_ge(S.h["VS"], orow[slab])
                    if slab >= 1:
                        t.wait_ge(S.h["VS"], oadd[slab - 1])
                    i = t.matmul(prp4[0][:], ones_rep[:], PM[:],
                                 start=True, stop=True)
                    i = t.matmul(prp4[1][:], ones_rep[:], PR[:],
                                 start=True, stop=True)
                    S.inc(i, "TS")
                    orep[slab] = S.n["TS"]

                @blk.vector
                def _(v):
                    v = AV.bind(v)
                    v.wait_ge(S.h["TS"], orep[slab])
                    for c in range(NCH):
                        xo = c * T + slab * 512
                        yo = c * 512
                        v.tensor_tensor(TMP0[:], y0sl[:, yo:yo + 512],
                                        prp4[0][:], Alu.subtract)
                        v.tensor_tensor(TMP0[:], TMP0[:], prp4[1][:], Alu.mult)
                        v.scalar_tensor_tensor(X[:, xo:xo + 512], TMP0[:],
                                               nv0[:, 2 * NCH + c:2 * NCH + c + 1],
                                               X[:, xo:xo + 512],
                                               Alu.mult, Alu.add)
                        i = v.tensor_scalar(X[:, xo:xo + 512], X[:, xo:xo + 512],
                                            nv0[:, 3 * NCH + c:3 * NCH + c + 1],
                                            None, Alu.add)
                    oadd[slab] = S.n["VS"]

            a4.close()
            open_stacks.remove(a4)
            p1.close()
            open_stacks.remove(p1)

          if PHASES < 2 or P2STAGE < 1:
              raise StopEmission
          g_p1 = {k: S.n[k] for k in ("TS", "VS", "SS")}

          @blk.gpsimd
          def _(g):
              g.wait_ge(S.h["LDG"], S.n["LDG"])
              g.wait_ge(S.h["CC"], S.n["CC"])
              for tt in range(22):
                  r, sl_ = tt // 3, tt % 3
                  i = g.dma_start(out=wgq_c[tt * P:(tt + 1) * P, :],
                                  in_=wq_all[r, sl_ * P * D:(sl_ + 1) * P * D
                                             ].rearrange("(a b) -> a b", a=P))
                  S.inc(i, "LDG", 16)
                  i = g.dma_start(
                      out=wuq_c[tt * P:(tt + 1) * P, :],
                      in_=wq_all[r, (3 + sl_) * P * D:(4 + sl_) * P * D
                                 ].rearrange("(a b) -> a b", a=P))
                  S.inc(i, "LDG", 16)
              for r in range(8):
                  i = g.dma_start(
                      out=wdq_c[r * P:(r + 1) * P, :],
                      in_=wq_all[r, 6 * P * D:].rearrange("(a b) -> a b", a=P))
                  S.inc(i, "LDG", 16)
              g.wait_ge(S.h["LDG"], S.n["LDG"])
              g.wait_ge(S.h["TS"], g_p1["TS"])
              g.wait_ge(S.h["VS"], g_p1["VS"])
              g.wait_ge(S.h["SS"], g_p1["SS"])
              for c in range(NCH):
                  i = g.dma_start(
                      out=XN2[:, c * T:(c + 1) * T].rearrange(
                          "p (tc j) -> p tc j", j=P),
                      in_=xq_dram[c].rearrange("tc p j -> p tc j"))
                  S.inc(i, "LDG", 16)
          ldg_wqc = S.n["LDG"] - 16 * NCH
          xq_done_ldg = S.n["LDG"]
          xqf = XN2

          p2d = contextlib.ExitStack()
          open_stacks.append(p2d)
          WG_S = [sb(f"WG_S{i}", [P, NCH * 512], BF16, p2d) for i in range(2)]
          WU_S = [sb(f"WU_S{i}", [P, NCH * 512], BF16, p2d) for i in range(2)]
          SGb = sb("SGb", [P, 512], BF16, p2d)
          HT = sb("HT", [P, 512], F32, p2d)
          HQ = sb("HQ", [P, 512], BF16, p2d)
          pg = [psum(f"pg{i}", [P, 512], F32, p2d) for i in range(2)]
          pu = [psum(f"pu{i}", [P, 512], F32, p2d) for i in range(2)]
          qs2 = mk_qscratch(p2d, 512, "h")

          FSW = [512] * 5 + [256]
          ld_strip = {}
          gm = {}
          um = {}
          hq_mark = {}
          hst = {}
          sil = {}

          for fs in range(6):
              W = FSW[fs]
              semn = "LD" if fs % 2 == 0 else "LD2"

              @blk.sync
              def _(q):
                  q.wait_ge(S.h[semn], S.n[semn])
                  if fs < 2:
                      q.wait_ge(S.h["LDG"], ldg_wqc)
                      q.wait_ge(S.h["TS"], g_p1["TS"])
                      q.wait_ge(S.h["VS"], g_p1["VS"])
                      q.wait_ge(S.h["SS"], g_p1["SS"])
                  else:
                      q.wait_ge(S.h["TS"], um[(fs - 2, 15)])
                  for dc in range(NCH):
                      i = q.dma_start(
                          out=WG_S[fs % 2][:, dc * 512:dc * 512 + W],
                          in_=wgq_c[fs * 512:fs * 512 + W,
                                    dc * P:(dc + 1) * P],
                          transpose=True)
                      S.inc(i, semn, 16)
                      i = q.dma_start(
                          out=WU_S[fs % 2][:, dc * 512:dc * 512 + W],
                          in_=wuq_c[fs * 512:fs * 512 + W,
                                    dc * P:(dc + 1) * P],
                          transpose=True)
                      S.inc(i, semn, 16)
              ld_strip[fs] = (semn, S.n[semn])

              for tci in range(16):
                  @blk.tensor
                  def _(t):
                      if tci == 0:
                          t.wait_ge(S.h[ld_strip[fs][0]], ld_strip[fs][1])
                          if fs == 0:
                              t.wait_ge(S.h["LDG"], xq_done_ldg)
                      if (fs, tci) > (0, 1):
                          pi = fs * 16 + tci - 2
                          t.wait_ge(S.h["SS"], sil[(pi // 16, pi % 16)])
                          t.wait_ge(S.h["VS"], hq_mark[(pi // 16, pi % 16)])
                      for dc in range(NCH):
                          i = t.matmul(pg[tci % 2][:, 0:W],
                                       xqf[:, dc * T + tci * P:
                                           dc * T + (tci + 1) * P],
                                       WG_S[fs % 2][:, dc * 512:dc * 512 + W],
                                       start=(dc == 0), stop=(dc == NCH - 1))
                      S.inc(i, "TS")
                      gm[(fs, tci)] = S.n["TS"]
                      for dc in range(NCH):
                          i = t.matmul(pu[tci % 2][:, 0:W],
                                       xqf[:, dc * T + tci * P:
                                           dc * T + (tci + 1) * P],
                                       WU_S[fs % 2][:, dc * 512:dc * 512 + W],
                                       start=(dc == 0), stop=(dc == NCH - 1))
                      S.inc(i, "TS")
                      um[(fs, tci)] = S.n["TS"]

                  @blk.scalar
                  def _(s):
                      s = ASS.bind(s)
                      s.wait_ge(S.h["TS"], gm[(fs, tci)])
                      s.activation(SGb[:, 0:W], pg[tci % 2][:, 0:W], Act.Silu)
                  sil[(fs, tci)] = S.n["SS"]

                  @blk.vector
                  def _(v):
                      v = AV.bind(v)
                      v.wait_ge(S.h["SS"], sil[(fs, tci)])
                      v.wait_ge(S.h["TS"], um[(fs, tci)])
                      v.tensor_tensor(HT[:, 0:W], SGb[:, 0:W],
                                      pu[tci % 2][:, 0:W], Alu.mult)
                  ht_mark = ("VS", S.n["VS"])

                  gi = fs * 16 + tci
                  ew = ()
                  if gi >= 1:
                      pi = gi - 1
                      ew = (("LDG", hst[(pi // 16, pi % 16)]),)
                  hq_mark[(fs, tci)] = quant_tile(qs2, HT[:, 0:W], HQ[:, 0:W], W,
                                                  ht_mark, extra_wait=ew)

                  @blk.gpsimd
                  def _(g):
                      g.wait_ge(S.h["LDG"], S.n["LDG"])
                      g.wait_ge(S.h["VS"], hq_mark[(fs, tci)])
                      i = g.dma_start(
                          out=hq_dram[tci * P:(tci + 1) * P,
                                      fs * 512:fs * 512 + W],
                          in_=HQ[:, 0:W])
                      S.inc(i, "LDG", 16)
                  hst[(fs, tci)] = S.n["LDG"]

          ldg_hq = S.n["LDG"]
          p2d.close()
          open_stacks.remove(p2d)

          if TAP2 in ("hq", "wgq"):
              @blk.gpsimd
              def _(g):
                  g.wait_ge(S.h["LDG"], S.n["LDG"])
                  if TAP2 == "hq":
                      i = g.dma_start(out=dbg_out[:], in_=hq_dram[:, 0:D])
                  else:
                      i = g.dma_start(out=dbg_out[:],
                                      in_=wgq_c[0:T, 0:D])
                  S.inc(i, "LDG", 16)

          if P2STAGE < 2:
              raise StopEmission

          p2e = contextlib.ExitStack()
          open_stacks.append(p2e)
          WDT = sb("WDT", [P, 22 * D], BF16, p2e)
          HQT = [sb(f"HQT{i}", [P, 22 * 256], BF16, p2e) for i in range(2)]
          pd = [psum(f"pd{i}", [P, 256], F32, p2e) for i in range(2)]

          @blk.sync
          def _(q):
              q.wait_ge(S.h["LD"], S.n["LD"])
              q.wait_ge(S.h["LD2"], S.n["LD2"])
              q.wait_ge(S.h["TS"], S.n["TS"])
              q.wait_ge(S.h["VS"], S.n["VS"])
              q.wait_ge(S.h["SS"], S.n["SS"])
              q.wait_ge(S.h["LDG"], S.n["LDG"])
              for ffc in range(22):
                  i = q.dma_start(out=WDT[:, ffc * D:(ffc + 1) * D],
                                  in_=wdq_c[:, ffc * P:(ffc + 1) * P],
                                  transpose=True)
                  S.inc(i, "LD", 16)
          ld_wdt = S.n["LD"]

          hqt_ld = {}
          dmm = {}
          dadd = {}
          for tsh in range(8):
              semn = "LD" if tsh % 2 == 0 else "LD2"

              @blk.sync
              def _(q):
                  q.wait_ge(S.h[semn], S.n[semn])
                  if tsh == 0:
                      q.wait_ge(S.h["LDG"], ldg_hq)
                  if tsh >= 2:
                      q.wait_ge(S.h["TS"], dmm[(tsh - 2, NCH - 1)])
                  for ffc in range(22):
                      i = q.dma_start(
                          out=HQT[tsh % 2][:, ffc * 256:(ffc + 1) * 256],
                          in_=hq_dram[tsh * 256:(tsh + 1) * 256,
                                      ffc * P:(ffc + 1) * P],
                          transpose=True)
                      S.inc(i, semn, 16)
              hqt_ld[tsh] = (semn, S.n[semn])

              for dc in range(NCH):
                  @blk.tensor
                  def _(t):
                      if dc == 0:
                          t.wait_ge(S.h[hqt_ld[tsh][0]], hqt_ld[tsh][1])
                          if tsh == 0:
                              t.wait_ge(S.h["LD"], ld_wdt)
                      if (tsh, dc) > (0, 1):
                          pi = tsh * NCH + dc - 2
                          t.wait_ge(S.h["VS"], dadd[(pi // NCH, pi % NCH)])
                      for ffc in range(22):
                          i = t.matmul(pd[dc % 2][:],
                                       WDT[:, ffc * D + dc * P:
                                           ffc * D + (dc + 1) * P],
                                       HQT[tsh % 2][:,
                                                    ffc * 256:(ffc + 1) * 256],
                                       start=(ffc == 0), stop=(ffc == 21))
                      S.inc(i, "TS")
                      dmm[(tsh, dc)] = S.n["TS"]

                  @blk.vector
                  def _(v):
                      v = AV.bind(v)
                      v.wait_ge(S.h["TS"], dmm[(tsh, dc)])
                      xo = dc * T + tsh * 256
                      v.tensor_tensor(X[:, xo:xo + 256], X[:, xo:xo + 256],
                                      pd[dc % 2][:], Alu.add)
                      dadd[(tsh, dc)] = S.n["VS"]

          p2e.close()
          open_stacks.remove(p2e)


          g_p2 = {k: S.n[k] for k in ("TS", "VS", "SS")}
          p3 = contextlib.ExitStack()
          open_stacks.append(p3)
          x3b = [sb(f"x3b{i}", [P, NCH * 512], BF16, p3) for i in range(2)]
          x3q = [sb(f"x3q{i}", [P, NCH * 512], BF16, p3) for i in range(2)]
          p3s = [psum(f"p3s{i}", [1, 512], F32, p3) for i in range(2)]
          p3q = [psum(f"p3q{i}", [1, 512], F32, p3) for i in range(2)]
          p3r = [psum(f"p3r{i}", [P, 512], F32, p3) for i in range(4)]

          vc3 = []
          sq3 = []
          st3 = []
          fr3 = []
          xn1m = []

          for sl in range(4):
              @blk.vector
              def _(v):
                  v = AV.bind(v)
                  if sl == 0:
                      v.wait_ge(S.h["TS"], g_p2["TS"])
                      v.wait_ge(S.h["SS"], g_p2["SS"])
                      v.wait_ge(S.h["LDG"], S.n["LDG"])
                  if sl >= 2:
                      v.wait_ge(S.h["TS"], st3[sl - 2])
                  for c in range(NCH):
                      o = c * T + sl * 512
                      i = v.tensor_copy(x3b[sl % 2][:, c * 512:(c + 1) * 512],
                                        X[:, o:o + 512])
                  vc3.append(S.n["VS"])

              @blk.scalar
              def _(s):
                  s = ASS.bind(s)
                  if sl == 0:
                      s.wait_ge(S.h["TS"], g_p2["TS"])
                      s.wait_ge(S.h["VS"], g_p2["VS"])
                      s.wait_ge(S.h["LDG"], S.n["LDG"])
                  if sl >= 2:
                      s.wait_ge(S.h["TS"], st3[sl - 2])
                  for c in range(NCH):
                      o = c * T + sl * 512
                      i = s.activation(x3q[sl % 2][:, c * 512:(c + 1) * 512],
                                       X[:, o:o + 512], Act.Square)
                  sq3.append(S.n["SS"])

              @blk.tensor
              def _(t):
                  t.wait_ge(S.h["VS"], vc3[sl])
                  t.wait_ge(S.h["SS"], sq3[sl])
                  if sl == 0:
                      t.wait_ge(S.h["LDG"], S.n["LDG"])
                  if sl >= 2:
                      t.wait_ge(S.h["VS"], fr3[sl - 2])
                  for c in range(NCH):
                      i = t.matmul(p3s[sl % 2][:], ones_st[:],
                                   x3b[sl % 2][:, c * 512:(c + 1) * 512],
                                   start=(c == 0), stop=(c == NCH - 1))
                      i = t.matmul(p3q[sl % 2][:], ones_st[:],
                                   x3q[sl % 2][:, c * 512:(c + 1) * 512],
                                   start=(c == 0), stop=(c == NCH - 1))
                  S.inc(i, "TS")
                  st3.append(S.n["TS"])

              @blk.vector
              def _(v):
                  v = AV.bind(v)
                  v.wait_ge(S.h["TS"], st3[sl])
                  o = sl * 512
                  v.tensor_scalar(mu_r[:, o:o + 512], p3s[sl % 2][:], 1.0 / D,
                                  None, Alu.mult)
                  v.tensor_scalar(R[0][:], p3q[sl % 2][:], 1.0 / D,
                                  None, Alu.mult)
                  fr3.append(S.n["VS"])
                  v.tensor_tensor(R[1][:], mu_r[:, o:o + 512],
                                  mu_r[:, o:o + 512], Alu.mult)
                  v.tensor_tensor(R[0][:], R[0][:], R[1][:], Alu.subtract)
                  i = v.tensor_scalar(R[1][:], R[0][:], EPS0, None, Alu.add)

              row_rsqrt(R[1], r_r[:, sl * 512:sl * 512 + 512], R[2], n=512)

          ts_r3 = []
          xn1_done = []
          for sl in range(4):
              o = sl * 512
              b = (sl % 2) * 2

              @blk.tensor
              def _(t):
                  S.wait(t, "VS")
                  if sl >= 2:
                      t.wait_ge(S.h["VS"], xn1_done[sl - 2])
                  i = t.matmul(p3r[b][:], ones_rep[:], mu_r[:, o:o + 512],
                               start=True, stop=True)
                  i = t.matmul(p3r[b + 1][:], ones_rep[:], r_r[:, o:o + 512],
                               start=True, stop=True)
                  S.inc(i, "TS")
                  ts_r3.append(S.n["TS"])

              @blk.vector
              def _(v):
                  v = AV.bind(v)
                  v.wait_ge(S.h["TS"], ts_r3[sl])
                  for c in range(NCH):
                      xo = c * T + o
                      v.tensor_tensor(TMP0[:], X[:, xo:xo + 512], p3r[b][:],
                                      Alu.subtract)
                      i = v.tensor_tensor(XN2[:, xo:xo + 512], TMP0[:],
                                          p3r[b + 1][:], Alu.mult)
                  xn1_done.append(S.n["VS"])
          xn1_all = S.n["VS"]
          p3.close()
          open_stacks.remove(p3)
          g_p3 = {k: S.n[k] for k in ("TS", "VS", "SS")}

          rows_stk.close()
          if PHASES >= 4:
            p4 = contextlib.ExitStack()
            open_stacks.append(p4)
            CMB = sb("CMB", [P, 16 * 8], F32, p4)
            p4r = contextlib.ExitStack()
            open_stacks.append(p4r)
            LGF = sb("LGF", [8, T], F32, p4r)
            LGT = sb("LGT", [P, 16 * 8], F32, p4r)
            RW = sb("RW", [P, NCH * 8], BF16, p4r)
            RC1 = sb("RC1", [P, 1], F32, p4r)
            RC2 = sb("RC2", [P, 8], F32, p4r)
            RC3 = sb("RC3", [P, 8], F32, p4r)
            p4l = [psum(f"p4l{i}", [8, 512], F32, p4r) for i in range(2)]
            p4t = [psum(f"p4t{i}", [P, 8], F32, p4r) for i in range(2)]
            p4ct = [psum(f"p4ct{i}", [8, P], F32, p4r) for i in range(2)]
            RB128 = sb("RB128", [P, 8], F32, p4r)

            @blk.gpsimd
            def _(g):
                g.wait_ge(S.h["LDG"], S.n["LDG"])
                g.wait_ge(S.h["TS"], g_p3["TS"])
                g.wait_ge(S.h["VS"], g_p3["VS"])
                g.wait_ge(S.h["SS"], g_p3["SS"])
                for c in range(NCH):
                    i = g.dma_start(out=RW[:, c * 8:(c + 1) * 8],
                                    in_=rw[c * P:(c + 1) * P, :])
                    S.inc(i, "LDG", 16)
            ldg_rw = S.n["LDG"]

            rmm = []
            rev = []
            for sl in range(4):
                @blk.tensor
                def _(t):
                    if sl == 0:
                        t.wait_ge(S.h["LDG"], ldg_rw)
                    t.wait_ge(S.h["VS"], xn1_done[sl])
                    if sl >= 2:
                        t.wait_ge(S.h["VS"], rev[sl - 2])
                    for c in range(NCH):
                        i = t.matmul(p4l[sl % 2][:], RW[:, c * 8:(c + 1) * 8],
                                     XN2[:, c * T + sl * 512:
                                         c * T + (sl + 1) * 512],
                                     start=(c == 0), stop=(c == NCH - 1))
                    S.inc(i, "TS")
                    rmm.append(S.n["TS"])

                @blk.vector
                def _(v):
                    v = AV.bind(v)
                    v.wait_ge(S.h["TS"], rmm[sl])
                    i = v.tensor_copy(LGF[:, sl * 512:(sl + 1) * 512],
                                      p4l[sl % 2][:])
                    rev.append(S.n["VS"])

            @blk.tensor
            def _(t):
                S.wait(t, "VS")
                i = t.matmul(p4t[0][:], ones_rep[:], rbt[:], start=True,
                             stop=True)
                S.inc(i, "TS")
            ts_rb = S.n["TS"]

            @blk.vector
            def _(v):
                v = AV.bind(v)
                v.wait_ge(S.h["TS"], ts_rb)
                i = v.tensor_copy(RB128[:], p4t[0][:])
            vs_rb = S.n["VS"]

            ltr = []
            lev = []
            for tc in range(16):
                @blk.tensor
                def _(t):
                    if tc == 0:
                        t.wait_ge(S.h["VS"], vs_rb)
                    t.wait_ge(S.h["VS"], rev[tc // 4])
                    if tc >= 2:
                        t.wait_ge(S.h["VS"], lev[tc - 2])
                    i = t.transpose(p4t[tc % 2][:],
                                    LGF[0:8, tc * P:(tc + 1) * P],
                                    idf[0:8, 0:8])
                    S.inc(i, "TS")
                    ltr.append(S.n["TS"])

                @blk.vector
                def _(v):
                    v = AV.bind(v)
                    v.wait_ge(S.h["TS"], ltr[tc])
                    i = v.tensor_tensor(LGT[:, tc * 8:(tc + 1) * 8],
                                        p4t[tc % 2][:], RB128[:], Alu.add)
                    lev.append(S.n["VS"])

            cmb_done = []
            for tc in range(16):
                @blk.vector
                def _(v):
                    v = AV.bind(v)
                    lg = LGT[:, tc * 8:(tc + 1) * 8]
                    v.tensor_reduce(RC1[:], lg, AX.X, Alu.max)
                    i = v.tensor_scalar(RC1[:], RC1[:], -1.0, None, Alu.mult)
                vs_m = S.n["VS"]

                @blk.scalar
                def _(s):
                    s = ASS.bind(s)
                    s.wait_ge(S.h["VS"], vs_m)
                    i = s.activation(RC2[:],
                                     LGT[:, tc * 8:(tc + 1) * 8], Act.Exp,
                                     bias=RC1[:])
                ss_e = S.n["SS"]

                @blk.vector
                def _(v):
                    v = AV.bind(v)
                    v.wait_ge(S.h["SS"], ss_e)
                    cs = CMB[:, tc * 8:(tc + 1) * 8]
                    v.tensor_reduce(RC1[:], RC2[:], AX.X, Alu.add)
                    v.reciprocal(RC1[:], RC1[:])
                    v.tensor_scalar(RC2[:], RC2[:], RC1[:], None, Alu.mult)
                    v.tensor_reduce(RC1[:], RC2[:], AX.X, Alu.max)
                    v.tensor_tensor(cs, RC2[:],
                                    RC1[:].broadcast_to((P, 8)), Alu.is_ge)
                    v.scalar_tensor_tensor(RC3[:], cs, -3.0, RC2[:],
                                           Alu.mult, Alu.add)
                    v.tensor_reduce(RC1[:], RC3[:], AX.X, Alu.max)
                    v.tensor_tensor(RC3[:], RC3[:],
                                    RC1[:].broadcast_to((P, 8)), Alu.is_ge)
                    v.tensor_tensor(RC3[:], RC3[:], cs, Alu.add)
                    i = v.tensor_tensor(cs, RC2[:], RC3[:], Alu.mult)
                cmb_done.append(S.n["VS"])

            comb_all = S.n["VS"]


            p4r.close()
            open_stacks.remove(p4r)

            p4b = contextlib.ExitStack()
            open_stacks.append(p4b)
            HM = sb("HM", [P, 32 * 1024], BF16, p4b)
            UPS = [sb(f"UPS{i}", [P, NCH * P], BF16, p4b) for i in range(3)]
            DNS = [sb(f"DNS{i}", [P, 32 * P], BF16, p4b) for i in range(2)]
            REPC = [sb(f"REPC{i}", [P, 512], F32, p4b) for i in range(2)]
            TMPD = TMP0
            pup = [psum(f"pup{i}", [P, 512], F32, p4b) for i in range(4)]
            pdn = [psum(f"pdn{i}", [P, 512], F32, p4b) for i in range(2)]
            prc = psum("prc", [P, 512], F32, p4b)
            prow = psum("prow", [1, 512], F32, p4b)

            last_repc = [0]
            rp_rowv = [0]
            rpt = {}
            prev_dn = [None]
            upld = {}
            dnld = {}
            upmm = {}
            gel = {}
            dnmm = {}
            dnev = {}
            rpl = {}
            rpe = {}

            nld = [0]

            def strip_sem():
                nld[0] += 1
                return "LDG" if nld[0] % 2 == 0 else "LG2"

            for e in range(8):
              for th in range(2):
                for f in range(32):
                    semn = strip_sem()

                    @blk.gpsimd
                    def _(g):
                        g.wait_ge(S.h[semn], S.n[semn])
                        gi = (e * 2 + th) * 32 + f
                        if gi == 0:
                            g.wait_ge(S.h["TS"], g_p3["TS"])
                            g.wait_ge(S.h["SS"], g_p3["SS"])
                            g.wait_ge(S.h["VS"], comb_all)
                        if gi >= 3:
                            pi = gi - 3
                            t_ = upmm.get((pi // 64, (pi // 32) % 2,
                                           pi % 32))
                            if t_ is not None:
                                g.wait_ge(S.h["TS"], t_)
                        i = g.dma_start(
                            out=UPS[f % 3][:].rearrange(
                                "p (dc j) -> p dc j", j=P),
                            in_=upT[e, f].rearrange("(dc p) j -> p dc j",
                                                    p=P))
                        S.inc(i, semn, 16)
                    upld[(e, th, f)] = (semn, S.n[semn])

                    @blk.tensor
                    def _(t):
                        t.wait_ge(S.h[upld[(e, th, f)][0]],
                                  upld[(e, th, f)][1])
                        gi = (e * 2 + th) * 32 + f
                        if gi >= 2:
                            pi = gi - 2
                            t.wait_ge(S.h["SS"], gel[(pi // 64,
                                                      (pi // 32) % 2,
                                                      pi % 32)])
                        for hf in range(2):
                            for c in range(NCH):
                                i = t.matmul(
                                    pup[(f % 2) * 2 + hf][:],
                                    UPS[f % 3][:, c * P:(c + 1) * P],
                                    XN2[:, c * T + th * 1024 + hf * 512:
                                        c * T + th * 1024 + hf * 512 + 512],
                                    start=(c == 0), stop=(c == NCH - 1))
                            S.inc(i, "TS")
                        upmm[(e, th, f)] = S.n["TS"]

                    @blk.scalar
                    def _(s):
                        s = ASS.bind(s)
                        if f == 0 and (e, th) != (0, 0):
                            pe, pt = (e, th - 1) if th == 1 else (e - 1, 1)
                            s.wait_ge(S.h["TS"], dnmm[(pe, pt, NCH - 1, 1)])
                        s.wait_ge(S.h["TS"], upmm[(e, th, f)])
                        for hf in range(2):
                            i = s.activation(
                                HM[:, f * 1024 + hf * 512:
                                   f * 1024 + (hf + 1) * 512],
                                pup[(f % 2) * 2 + hf][:], Act.Gelu)
                        gel[(e, th, f)] = S.n["SS"]

                for hf in range(2):
                    tc0 = th * 8 + hf * 4

                    @blk.tensor
                    def _(t):
                        t.wait_ge(S.h["VS"], comb_all)
                        if (e, th, hf) != (0, 0, 0):
                            t.wait_ge(S.h["VS"], rp_rowv[0])
                        for j in range(4):
                            i = t.transpose(prow[0:1, j * P:(j + 1) * P],
                                            CMB[:, (tc0 + j) * 8 + e:
                                                (tc0 + j) * 8 + e + 1],
                                            idf[:])
                        S.inc(i, "TS")
                        rpt[(e, th, hf)] = S.n["TS"]

                    @blk.vector
                    def _(v):
                        v = AV.bind(v)
                        v.wait_ge(S.h["TS"], rpt[(e, th, hf)])
                        i = v.tensor_copy(R[3][:], prow[0:1, :])
                        rp_rowv[0] = S.n["VS"]
                    rp_row = S.n["VS"]

                    @blk.tensor
                    def _(t):
                        t.wait_ge(S.h["VS"], rp_row)
                        if (e, th, hf) != (0, 0, 0):
                            t.wait_ge(S.h["VS"], last_repc[0])
                        i = t.matmul(prc[:], ones_rep[:], R[3][:],
                                     start=True, stop=True)
                        S.inc(i, "TS")
                        rpl[(e, th, hf)] = S.n["TS"]

                    @blk.vector
                    def _(v):
                        v = AV.bind(v)
                        v.wait_ge(S.h["TS"], rpl[(e, th, hf)])
                        i = v.tensor_copy(REPC[hf][:], prc[:])
                        last_repc[0] = S.n["VS"]
                        rpe[(e, th, hf)] = S.n["VS"]

                for dc in range(NCH):
                    semn = strip_sem()

                    @blk.gpsimd
                    def _(g):
                        g.wait_ge(S.h[semn], S.n[semn])
                        if dc >= 2:
                            g.wait_ge(S.h["TS"], dnmm[(e, th, dc - 2, 1)])
                        elif (e, th) != (0, 0):
                            pe, pt = (e, 0) if th == 1 else (e - 1, 1)
                            g.wait_ge(S.h["TS"], dnmm[(pe, pt, dc + 6, 1)])
                        i = g.dma_start(
                            out=DNS[dc % 2][:].rearrange(
                                "p (fc j) -> p fc j", j=P),
                            in_=dnT[e, dc].rearrange(
                                "(fc p) j -> p fc j", p=P))
                        S.inc(i, semn, 16)
                    dnld[(e, th, dc)] = (semn, S.n[semn])

                    for hf in range(2):
                        @blk.tensor
                        def _(t):
                            t.wait_ge(S.h[dnld[(e, th, dc)][0]],
                                      dnld[(e, th, dc)][1])
                            t.wait_ge(S.h["SS"], gel[(e, th, 31)])
                            pk = (e, th, dc - 1, hf)
                            if dc >= 1 and pk in dnev:
                                t.wait_ge(S.h["VS"], dnev[pk])
                            elif dc == 0 and (e, th) != (0, 0):
                                pe, pt = (e, 0) if th == 1 else (e - 1, 1)
                                t.wait_ge(S.h["VS"],
                                          dnev[(pe, pt, NCH - 1, hf)])
                            for fc in range(32):
                                i = t.matmul(
                                    pdn[hf][:],
                                    DNS[dc % 2][:, fc * P:(fc + 1) * P],
                                    HM[:, fc * 1024 + hf * 512:
                                       fc * 1024 + (hf + 1) * 512],
                                    start=(fc == 0), stop=(fc == 31))
                            S.inc(i, "TS")
                            dnmm[(e, th, dc, hf)] = S.n["TS"]

                        @blk.vector
                        def _(v):
                            v = AV.bind(v)
                            v.wait_ge(S.h["TS"], dnmm[(e, th, dc, hf)])
                            v.wait_ge(S.h["VS"], rpe[(e, th, hf)])
                            xo = dc * T + th * 1024 + hf * 512
                            v.tensor_tensor(TMPD[:], pdn[hf][:], REPC[hf][:],
                                            Alu.mult)
                            i = v.tensor_tensor(X[:, xo:xo + 512],
                                                X[:, xo:xo + 512],
                                                TMPD[:], Alu.add)
                            dnev[(e, th, dc, hf)] = S.n["VS"]

            p4b.close()
            open_stacks.remove(p4b)
            p4.close()
            open_stacks.remove(p4)


          if PHASES >= 5:
            g_p4 = {k: S.n[k] for k in ("TS", "VS", "SS")}
            p5 = contextlib.ExitStack()
            open_stacks.append(p5)
            QG = sb("QG", [P, 4096], BF16, p5)
            WB5 = sb("WB5", [P, NCH * D], BF16, p5)
            WQ5 = sb("WQ5", [P, NCH * D], BF16, p5)
            WG5 = sb("WG5", [P, NCH * D], BF16, p5)
            SREC = sb("SREC", [P, 512], F32, p5)
            SGC = sb("SGC", [P, 1], F32, p5)
            GP5 = sb("GP5", [P, NCH * CH], BF16, p5)
            GD5 = sb("GD5", [P, NCH * CH], BF16, p5)
            pq5 = [psum(f"pq5{i}", [P, 512], F32, p5) for i in range(2)]
            p5a = contextlib.ExitStack()
            open_stacks.append(p5a)
            KG = sb("KG", [P, 4096], BF16, p5a)
            KT = sb("KT", [P, 4096], BF16, p5a)
            VT5 = sb("VT5", [P, 4096], BF16, p5a)
            M3 = sb("M3", [P, NH1 * CH], BF16, p5a)
            MSK2 = [sb(f"MSK{i}", [P, CH], BF16, p5a) for i in range(2)]
            RB52 = [sb(f"RB5{i}", [64, CH], F32, p5a) for i in range(2)]
            SMAT = sb("SMAT", [P, 512], F32, p5a)
            SBF = sb("SBF", [P, 512], BF16, p5a)
            psc2 = [psum(f"psc{i}", [P, 2 * CH], F32, p5a) for i in range(2)]
            PB = [psum(f"pb5{i}", [64, CH + 64], F32, p5a)
                  for i in range(2)]
            pkt2 = [psum(f"pkt{i}", [P, 64], BF16, p5a) for i in range(2)]

            @blk.gpsimd
            def _(q):
                q.wait_ge(S.h["LDG"], S.n["LDG"])
                q.wait_ge(S.h["TS"], g_p4["TS"])
                q.wait_ge(S.h["VS"], g_p4["VS"])
                q.wait_ge(S.h["SS"], g_p4["SS"])
                i = q.dma_start(out=M3[:].rearrange("p (h i) -> p h i", i=CH),
                                in_=mask3[:].rearrange("h j i -> j h i"))
                S.inc(i, "LDG", 16)
                i = q.dma_start(out=SGC[:], in_=sgate[:])
                S.inc(i, "LDG", 16)
                i = q.dma_start(out=GP5[:].rearrange("p (c i) -> p c i", i=CH),
                                in_=gpos[:].rearrange("(c p) i -> p c i", p=P))
                S.inc(i, "LDG", 16)
                i = q.dma_start(out=GD5[:].rearrange("p (c i) -> p c i", i=CH),
                                in_=gdecC[:].rearrange("(c p) i -> p c i", p=P))
                S.inc(i, "LDG", 16)
            ld_m3 = S.n["LDG"]

            @blk.vector
            def _(v):
                v = AV.bind(v)
                v.wait_ge(S.h["TS"], g_p4["TS"])
                v.wait_ge(S.h["SS"], g_p4["SS"])
                v.wait_ge(S.h["LDG"], S.n["LDG"])
                i = v.memset(SMAT[:], 0.0)
                i = v.memset(SBF[:], 0.0)
            vs_s0 = S.n["VS"]

            gs5_b = S.n["GS"]
            rec_list = []
            fix_list = []
            retst = {}
            srbv = {}
            fixt = {}
            rsq5 = {}
            swv = {}
            oproj5 = {}
            pq5_hist = []
            qev5 = {}
            kev5 = {}
            vev5 = {}
            ktv = {}
            mskv = {}
            rbv = {}
            retv = {}
            stv = {}
            sbv = {}
            ldw5 = {}
            pmm5 = {}
            ktr = {}
            scv = {}
            suv = {}
            riv = {}
            rjv = {}

            for g5 in range(4):
                gt0 = g5 * 512
                for wi, (wt, dst, pat) in enumerate(
                        ((wq1, QG, GP5), (wk1, KG, GD5), (wv1, VT5, None))):
                    semn = "LDG" if (g5 * 3 + wi) % 2 == 0 else "LG2"

                    @blk.gpsimd
                    def _(q):
                        q.wait_ge(S.h[semn], S.n[semn])
                        if g5 == 0 and wi == 0:
                            q.wait_ge(S.h["LDG"], ld_m3)
                            q.wait_ge(S.h["TS"], g_p4["TS"])
                            q.wait_ge(S.h["VS"], g_p4["VS"])
                            q.wait_ge(S.h["SS"], g_p4["SS"])
                        if (g5, wi) != (0, 0):
                            t_ = pmm5[(g5 * 3 + wi) - 1]
                            q.wait_ge(S.h["TS"], t_)
                        for c in range(NCH):
                            i = q.dma_start(out=WB5[:, c * D:(c + 1) * D],
                                            in_=wt[c * P:(c + 1) * P, :])
                            S.inc(i, semn, 16)
                    ldw5[(g5, wi)] = (semn, S.n[semn])

                    if pat is not None:
                        for oc in range(NCH):
                            @blk.tensor
                            def _(t):
                                if oc == 0:
                                    t.wait_ge(S.h[ldw5[(g5, wi)][0]],
                                              ldw5[(g5, wi)][1])
                                    if g5 == 0 and wi == 0:
                                        t.wait_ge(S.h["VS"], vs_s0)
                                    if g5 >= 1 and wi == 0:
                                        t.wait_ge(S.h["VS"], retv[
                                            (g5 - 1, 3, NH1 - 1)])
                                        t.wait_ge(S.h["VS"], stv[
                                            (g5 - 1, 3, NH1 - 1)])
                                        t.wait_ge(S.h["LDG"], retst[
                                            (g5 - 1, 3, NH1 - 1)])
                                if len(pq5_hist) >= 2:
                                    t.wait_ge(S.h[pq5_hist[-2][0]],
                                              pq5_hist[-2][1])
                                for c in range(NCH):
                                    i = t.matmul(
                                        pq5[oc % 2][:],
                                        WB5[:, c * D + oc * P:
                                            c * D + (oc + 1) * P],
                                        XN2[:, c * T + gt0:c * T + gt0 + 512],
                                        start=(c == 0), stop=(c == NCH - 1))
                                S.inc(i, "TS")
                                pmm5[g5 * 3 + wi] = S.n["TS"]
                                pmm5[(g5, wi, oc)] = S.n["TS"]

                            @blk.vector
                            def _(v):
                                v = AV.bind(v)
                                v.wait_ge(S.h["TS"], pmm5[(g5, wi, oc)])
                                pp = pat[:, oc * CH:(oc + 1) * CH]
                                bc = pp.unsqueeze(1).broadcast_to((P, 4, CH))
                                ps3 = pq5[oc % 2][:].rearrange(
                                    "p (c i) -> p c i", i=CH)
                                d3 = dst[:, oc * 512:(oc + 1) * 512].rearrange(
                                    "p (c i) -> p c i", i=CH)
                                i = v.tensor_tensor(d3, ps3, bc, Alu.mult)
                                qev5[(g5, wi, oc)] = S.n["VS"]
                                pq5_hist.append(("VS", S.n["VS"]))
                    else:
                        for tc4 in range(4):
                            for sl2 in range(2):
                                @blk.tensor
                                def _(t):
                                    if tc4 == 0 and sl2 == 0:
                                        t.wait_ge(S.h[ldw5[(g5, wi)][0]],
                                                  ldw5[(g5, wi)][1])
                                    if len(pq5_hist) >= 2:
                                        t.wait_ge(S.h[pq5_hist[-2][0]],
                                                  pq5_hist[-2][1])
                                    for c in range(NCH):
                                        i = t.matmul(
                                            pq5[(tc4 * 2 + sl2) % 2][:],
                                            XN2[:, c * T + gt0 + tc4 * P:
                                                c * T + gt0 + (tc4 + 1) * P],
                                            WB5[:, c * D + sl2 * 512:
                                                c * D + (sl2 + 1) * 512],
                                            start=(c == 0),
                                            stop=(c == NCH - 1))
                                    S.inc(i, "TS")
                                    pmm5[g5 * 3 + wi] = S.n["TS"]
                                    pmm5[(g5, wi, tc4, sl2)] = S.n["TS"]

                                @blk.vector
                                def _(v):
                                    v = AV.bind(v)
                                    v.wait_ge(S.h["TS"],
                                              pmm5[(g5, wi, tc4, sl2)])
                                    i = v.tensor_copy(
                                        VT5[:, tc4 * D + sl2 * 512:
                                            tc4 * D + (sl2 + 1) * 512],
                                        pq5[(tc4 * 2 + sl2) % 2][:])
                                    vev5[(g5, tc4, sl2)] = S.n["VS"]
                                    pq5_hist.append(("VS", S.n["VS"]))

                if g5 == 3:
                    @blk.gpsimd
                    def _(g):
                        g.wait_ge(S.h["LG2"], S.n["LG2"])
                        for wt, dst in ((wq1, WQ5), (wg1, WG5)):
                            for c in range(NCH):
                                i = g.dma_start(
                                    out=dst[:, c * D:(c + 1) * D],
                                    in_=wt[c * P:(c + 1) * P, :])
                                S.inc(i, "LG2", 16)
                    ldg_wqg = S.n["LG2"]

                for cl in range(4):
                    for h in range(NH1):
                        idx5 = cl * NH1 + h

                        @blk.tensor
                        def _(t):
                            t.wait_ge(S.h["VS"], qev5[(g5, 1, h // 2)])
                            if g5 * 64 + idx5 >= 2:
                                t.wait_ge(S.h["GS"],
                                          gs5_b + g5 * 64 + idx5 - 1)
                            b5 = 64 * (h % 2)
                            i = t.transpose(
                                pkt2[idx5 % 2][:, 0:64],
                                KG[b5:b5 + 64,
                                   (h // 2) * 512 + cl * CH:
                                   (h // 2) * 512 + (cl + 1) * CH],
                                idb[b5:b5 + 64, b5:b5 + 64])
                            S.inc(i, "TS")
                            ktr[(g5, cl, h)] = S.n["TS"]

                        @blk.vector
                        def _(v):
                            v = AV.bind(v)
                            v.wait_ge(S.h["TS"], ktr[(g5, cl, h)])
                            i = v.tensor_copy(
                                KT[:, cl * D + h * 64:cl * D + (h + 1) * 64],
                                pkt2[idx5 % 2][:, 0:64], inc="GS")

                for cl in range(4):
                    for h in range(NH1):
                        b5 = 64 * (h % 2)
                        hc = (h // 2) * 512 + cl * CH
                        ridx = len(rec_list)
                        pcb = psc2[ridx % 2]
                        pbb = PB[ridx % 2]
                        rbb = RB52[ridx % 2]
                        msb = MSK2[ridx % 2]
                        key = (g5, cl, h)
                        prev_state = ((g5, cl - 1, h) if cl > 0 else
                                      ((g5 - 1, 3, h) if g5 > 0 else None))

                        @blk.tensor
                        def _(t):
                            t.wait_ge(S.h["VS"], qev5[(g5, 0, h // 2)])
                            t.wait_ge(S.h["VS"], qev5[(g5, 1, h // 2)])
                            if ridx == 0:
                                t.wait_ge(S.h["VS"], vs_s0)
                            if ridx >= 2:
                                t.wait_ge(S.h["VS"],
                                          mskv[rec_list[ridx - 2]])
                            i = t.matmul(pcb[:, 0:CH],
                                         KG[b5:b5 + 64, hc:hc + CH],
                                         QG[b5:b5 + 64, hc:hc + CH],
                                         start=True, stop=True)
                            S.inc(i, "TS")
                            scv[key] = S.n["TS"]

                        @blk.vector
                        def _(v):
                            v = AV.bind(v)
                            v.wait_ge(S.h["TS"], scv[key])
                            i = v.tensor_tensor(msb[:], pcb[:, 0:CH],
                                                M3[:, h * CH:(h + 1) * CH],
                                                Alu.mult)
                            mskv[key] = S.n["VS"]

                        @blk.tensor
                        def _(t):
                            t.wait_ge(S.h["VS"], mskv[key])
                            t.wait_ge(S.h["VS"], vev5[(g5, cl, 0)])
                            t.wait_ge(S.h["VS"], vev5[(g5, cl, 1)])
                            if ridx >= 2:
                                t.wait_ge(S.h["VS"], stv[rec_list[ridx - 2]])
                            if prev_state is not None:
                                t.wait_ge(S.h["VS"], stv[prev_state])
                            i = t.matmul(pbb[0:64, 0:CH],
                                         VT5[:, cl * D + h * 64:
                                             cl * D + (h + 1) * 64],
                                         msb[:], start=True, stop=True)
                            S.inc(i, "TS")
                            riv[key] = S.n["TS"]
                            i = t.matmul(pbb[0:64, CH:CH + 64],
                                         KT[:, cl * D + h * 64:
                                            cl * D + (h + 1) * 64],
                                         VT5[:, cl * D + h * 64:
                                             cl * D + (h + 1) * 64],
                                         start=True, stop=True)
                            S.inc(i, "TS")
                            suv[key] = S.n["TS"]
                            i = t.matmul(pcb[0:64, CH:2 * CH],
                                         SBF[b5:b5 + 64,
                                             (h // 2) * 64:(h // 2) * 64 + 64],
                                         QG[b5:b5 + 64, hc:hc + CH],
                                         start=True, stop=True)
                            S.inc(i, "TS")
                            rjv[key] = S.n["TS"]

                        @blk.vector
                        def _(v):
                            v = AV.bind(v)
                            if ridx >= 2:
                                v.wait_ge(S.h["LDG"],
                                          retst[rec_list[ridx - 2]])
                            v.wait_ge(S.h["TS"], riv[key])
                            i = v.tensor_copy(rbb[:], pbb[0:64, 0:CH])
                            rbv[key] = S.n["VS"]
                            v.wait_ge(S.h["TS"], rjv[key])
                            i = v.tensor_tensor(rbb[:], rbb[:],
                                                pcb[0:64, CH:2 * CH],
                                                Alu.add)
                            retv[key] = S.n["VS"]
                            gC = float(GAMMAS[h] ** CH)
                            v.scalar_tensor_tensor(
                                SMAT[b5:b5 + 64,
                                     (h // 2) * 64:(h // 2) * 64 + 64],
                                SMAT[b5:b5 + 64,
                                     (h // 2) * 64:(h // 2) * 64 + 64],
                                gC, pbb[0:64, CH:CH + 64],
                                Alu.mult, Alu.add)
                            i = v.tensor_copy(
                                SBF[b5:b5 + 64,
                                    (h // 2) * 64:(h // 2) * 64 + 64],
                                SMAT[b5:b5 + 64,
                                     (h // 2) * 64:(h // 2) * 64 + 64])
                            stv[key] = S.n["VS"]

                        @blk.gpsimd
                        def _(g):
                            g.wait_ge(S.h["LDG"], S.n["LDG"])
                            g.wait_ge(S.h["VS"], retv[key])
                            i = g.dma_start(
                                out=ret_dram[64 * h:64 * (h + 1),
                                             gt0 + cl * CH:
                                             gt0 + (cl + 1) * CH],
                                in_=rbb[:])
                            S.inc(i, "LDG", 16)
                        retst[key] = S.n["LDG"]
                        rec_list.append(key)

            ldg_ret = S.n["LDG"]

            @blk.gpsimd
            def _(g):
                g.wait_ge(S.h["LG2"], S.n["LG2"])
                g.wait_ge(S.h["TS"], pmm5[11])
                for c in range(NCH):
                    i = g.dma_start(out=WB5[:, c * D:(c + 1) * D],
                                    in_=wo1[c * P:(c + 1) * P, :])
                    S.inc(i, "LG2", 16)
            ldg_wb = S.n["LG2"]

            st3o = st_contrib[:].rearrange("(h p j) -> h p j", p=64, j=64)
            sta3 = st_all[0].rearrange("(h p j) -> h p j", p=64, j=64)

            @blk.gpsimd
            def _(g):
                g.wait_ge(S.h["LDG"], S.n["LDG"])
                g.wait_ge(S.h["VS"], stv[(3, 3, NH1 - 1)])
                i = g.dma_start(
                    out=st3o[0::2].transpose([1, 0, 2]),
                    in_=SMAT[0:64, :].rearrange("p (q j) -> p q j", j=64))
                S.inc(i, "LDG", 16)
                i = g.dma_start(
                    out=st3o[1::2].transpose([1, 0, 2]),
                    in_=SMAT[64:128, :].rearrange("p (q j) -> p q j", j=64))
                S.inc(i, "LDG", 16)
                g.wait_ge(S.h["LDG"], S.n["LDG"])
                i = g.collective_compute(
                    "AllGather", Alu.bypass,
                    replica_groups=[[0, 1], [2, 3], [4, 5], [6, 7]],
                    ins=[st_contrib[:]], outs=[st_all[:]])
                S.inc(i, "CC", 1)
                g.wait_ge(S.h["CC"], S.n["CC"])
                i = g.dma_start(
                    out=SREC[0:64, :].rearrange("p (q j) -> p q j", j=64),
                    in_=sta3[0::2].transpose([1, 0, 2]))
                S.inc(i, "LDG", 16)
                i = g.dma_start(
                    out=SREC[64:128, :].rearrange("p (q j) -> p q j", j=64),
                    in_=sta3[1::2].transpose([1, 0, 2]))
                S.inc(i, "LDG", 16)
            ldg_srec = S.n["LDG"]

            @blk.vector
            def _(v):
                v = AV.bind(v)
                v.wait_ge(S.h["LDG"], ldg_srec)
                i = v.tensor_scalar(SREC[:], SREC[:], SGC[:, 0:1], None,
                                    Alu.mult)
            vs_srec = S.n["VS"]

            g_p5a = {k: S.n[k] for k in ("TS", "VS", "SS")}
            p5a.close()
            open_stacks.remove(p5a)

            p5b = contextlib.ExitStack()
            open_stacks.append(p5b)
            RSL = sb("RSL", [P, NCH * 512], BF16, p5b)
            RSQ = sb("RSQ", [P, 2 * 512], BF16, p5b)
            SW = sb("SW", [P, NCH * 512], BF16, p5b)
            LNG = sb("LNG", [P, NCH * 512], BF16, p5b)
            SRB2 = [sb(f"SRB{i}", [P, 64], BF16, p5b) for i in range(2)]
            pst5 = [psum(f"pst5{i}", [1, 512], F32, p5b) for i in range(2)]
            prep5 = [psum(f"prep5{i}", [P, 512], F32, p5b) for i in range(2)]
            pr22 = [psum(f"pr2{i}", [64, CH], F32, p5b) for i in range(2)]

            fixv = {}
            st5d = {}
            lng5d = {}
            rsqc = {}
            stc = {}
            for ts5 in range(4):
                gt0 = ts5 * 512

                semn = "LD" if ts5 % 2 == 0 else "LD2"

                @blk.sync
                def _(q):
                    q.wait_ge(S.h[semn], S.n[semn])
                    if ts5 == 0:
                        q.wait_ge(S.h["TS"], g_p5a["TS"])
                        q.wait_ge(S.h["VS"], g_p5a["VS"])
                        q.wait_ge(S.h["SS"], g_p5a["SS"])
                    q.wait_ge(S.h["LDG"], retst[(ts5, 3, NH1 - 1)])
                    if ts5 >= 1:
                        q.wait_ge(S.h["VS"], lng5d[ts5 - 1])
                        q.wait_ge(S.h["TS"], st5d[ts5 - 1])
                        q.wait_ge(S.h["SS"], rsq5[ts5 - 1])
                    for c in range(NCH):
                        i = q.dma_start(
                            out=RSL[:, c * 512:(c + 1) * 512],
                            in_=ret_dram[c * P:(c + 1) * P, gt0:gt0 + 512])
                        S.inc(i, semn, 16)
                rsl_ld = (semn, S.n[semn])

                for oc in range(NCH):
                    @blk.tensor
                    def _(t):
                        if ts5 == 0 and oc == 0:
                            t.wait_ge(S.h["LG2"], ldg_wqg)
                        if len(pq5_hist) >= 2:
                            t.wait_ge(S.h[pq5_hist[-2][0]],
                                      pq5_hist[-2][1])
                        for c in range(NCH):
                            i = t.matmul(
                                pq5[oc % 2][:],
                                WQ5[:, c * D + oc * P:c * D + (oc + 1) * P],
                                XN2[:, c * T + gt0:c * T + gt0 + 512],
                                start=(c == 0), stop=(c == NCH - 1))
                        S.inc(i, "TS")
                        pmm5[(9, ts5, oc)] = S.n["TS"]

                    @blk.vector
                    def _(v):
                        v = AV.bind(v)
                        v.wait_ge(S.h["TS"], pmm5[(9, ts5, oc)])
                        pp = GP5[:, oc * CH:(oc + 1) * CH]
                        bc = pp.unsqueeze(1).broadcast_to((P, 4, CH))
                        ps3 = pq5[oc % 2][:].rearrange("p (c i) -> p c i",
                                                       i=CH)
                        d3 = QG[:, oc * 512:(oc + 1) * 512].rearrange(
                            "p (c i) -> p c i", i=CH)
                        i = v.tensor_tensor(d3, ps3, bc, Alu.mult)
                        pq5_hist.append(("VS", S.n["VS"]))
                qg_done = S.n["VS"]

                for h in range(NH1):
                    b5 = 64 * (h % 2)
                    for cl in range(4):
                        gc = ts5 * 4 + cl
                        hc = (h // 2) * 512 + cl * CH
                        fidx = len(fix_list)
                        srb_b = SRB2[fidx % 2]
                        pr_b = pr22[fidx % 2]
                        fkey = (ts5, h, cl)

                        @blk.vector
                        def _(v):
                            v = AV.bind(v)
                            if fidx == 0:
                                v.wait_ge(S.h["TS"], g_p5a["TS"])
                            if fidx >= 2:
                                v.wait_ge(S.h["TS"],
                                          fixt[fix_list[fidx - 2]])
                            gcv = float(GAMMAS[h] ** (CH * gc))
                            i = v.tensor_scalar(
                                srb_b[b5:b5 + 64, :],
                                SREC[b5:b5 + 64,
                                     (h // 2) * 64:(h // 2) * 64 + 64],
                                gcv, None, Alu.mult)
                            srbv[fkey] = S.n["VS"]

                        @blk.tensor
                        def _(t):
                            t.wait_ge(S.h["VS"], srbv[fkey])
                            if h == 0 and cl == 0:
                                t.wait_ge(S.h["VS"], qg_done)
                            if fidx >= 2:
                                t.wait_ge(S.h["VS"],
                                          fixv[fix_list[fidx - 2]])
                            i = t.matmul(pr_b[:], srb_b[b5:b5 + 64, :],
                                         QG[b5:b5 + 64, hc:hc + CH],
                                         start=True, stop=True)
                            S.inc(i, "TS")
                            fixt[fkey] = S.n["TS"]

                        @blk.vector
                        def _(v):
                            v = AV.bind(v)
                            v.wait_ge(S.h["TS"], fixt[fkey])
                            if h == 0 and cl == 0:
                                v.wait_ge(S.h[rsl_ld[0]], rsl_ld[1])
                            rp = RSL[
                                b5:b5 + 64,
                                (h // 2) * 512 + cl * CH:
                                (h // 2) * 512 + (cl + 1) * CH]
                            i = v.tensor_tensor(rp, rp, pr_b[:], Alu.add)
                            fixv[fkey] = S.n["VS"]
                        fix_list.append(fkey)
                fix_all = S.n["VS"]

                for c in range(NCH):
                    gi = ts5 * NCH + c

                    @blk.scalar
                    def _(s):
                        s = ASS.bind(s)
                        if c == 0:
                            s.wait_ge(S.h["VS"], fix_all)
                            if ts5 == 0:
                                s.wait_ge(S.h["TS"], g_p5a["TS"])
                        if gi >= 2:
                            s.wait_ge(S.h["TS"], stc[gi - 2])
                        i = s.activation(RSQ[:, (c % 2) * 512:
                                              (c % 2) * 512 + 512],
                                         RSL[:, c * 512:(c + 1) * 512],
                                         Act.Square)
                        rsqc[gi] = S.n["SS"]

                    @blk.tensor
                    def _(t):
                        if c == 0:
                            t.wait_ge(S.h["VS"], fix_all)
                        t.wait_ge(S.h["SS"], rsqc[gi])
                        i = t.matmul(pst5[0][:], ones_st[:],
                                     RSL[:, c * 512:(c + 1) * 512],
                                     start=(c == 0), stop=(c == NCH - 1))
                        i = t.matmul(pst5[1][:], ones_st[:],
                                     RSQ[:, (c % 2) * 512:(c % 2) * 512 + 512],
                                     start=(c == 0), stop=(c == NCH - 1))
                        S.inc(i, "TS")
                        stc[gi] = S.n["TS"]
                rsq5[ts5] = S.n["SS"]
                st5d[ts5] = S.n["TS"]

                @blk.vector
                def _(v):
                    v = AV.bind(v)
                    v.wait_ge(S.h["TS"], st5d[ts5])
                    v.tensor_scalar(R[4][:], pst5[0][:], 1.0 / D, None,
                                    Alu.mult)
                    v.tensor_scalar(R[0][:], pst5[1][:], 1.0 / D, None,
                                    Alu.mult)
                    v.tensor_tensor(R[1][:], R[4][:], R[4][:], Alu.mult)
                    v.tensor_tensor(R[0][:], R[0][:], R[1][:], Alu.subtract)
                    i = v.tensor_scalar(R[1][:], R[0][:], EPS1, None, Alu.add)

                row_rsqrt(R[1], R[5], R[2], n=512)

                for oc in range(NCH):
                    @blk.tensor
                    def _(t):
                        if len(pq5_hist) >= 2:
                            t.wait_ge(S.h[pq5_hist[-2][0]],
                                      pq5_hist[-2][1])
                        for c in range(NCH):
                            i = t.matmul(
                                pq5[oc % 2][:],
                                WG5[:, c * D + oc * P:c * D + (oc + 1) * P],
                                XN2[:, c * T + gt0:c * T + gt0 + 512],
                                start=(c == 0), stop=(c == NCH - 1))
                        S.inc(i, "TS")
                        pmm5[(8, ts5, oc)] = S.n["TS"]

                    @blk.scalar
                    def _(s):
                        s = ASS.bind(s)
                        s.wait_ge(S.h["TS"], pmm5[(8, ts5, oc)])
                        if oc == 0 and ts5 >= 1:
                            s.wait_ge(S.h["VS"], lng5d[ts5 - 1])
                        i = s.activation(SW[:, oc * 512:(oc + 1) * 512],
                                         pq5[oc % 2][:], Act.Silu,
                                         bias=nv1[:, oc:oc + 1])
                        swv[(ts5, oc)] = S.n["SS"]
                        pq5_hist.append(("SS", S.n["SS"]))

                @blk.tensor
                def _(t):
                    S.wait(t, "VS")
                    i = t.matmul(prep5[0][:], ones_rep[:], R[4][:],
                                 start=True, stop=True)
                    i = t.matmul(prep5[1][:], ones_rep[:], R[5][:],
                                 start=True, stop=True)
                    S.inc(i, "TS")
                rep5 = S.n["TS"]

                @blk.vector
                def _(v):
                    v = AV.bind(v)
                    v.wait_ge(S.h["TS"], rep5)
                    v.wait_ge(S.h["SS"], swv[(ts5, NCH - 1)])
                    if ts5 >= 1:
                        v.wait_ge(S.h["TS"],
                                  oproj5[(ts5 - 1, NCH - 1)])
                    for c in range(NCH):
                        co = c * 512
                        v.tensor_tensor(TMP0[:],
                                        RSL[:, co:co + 512],
                                        prep5[0][:], Alu.subtract)
                        v.tensor_tensor(TMP0[:], TMP0[:], prep5[1][:],
                                        Alu.mult)
                        v.scalar_tensor_tensor(
                            LNG[:, co:co + 512], TMP0[:],
                            nv1[:, 2 * NCH + c:2 * NCH + c + 1],
                            nv1[:, 3 * NCH + c:3 * NCH + c + 1
                                ].broadcast_to((P, 512)),
                            Alu.mult, Alu.add)
                        i = v.tensor_tensor(LNG[:, co:co + 512],
                                            LNG[:, co:co + 512],
                                            SW[:, co:co + 512], Alu.mult)
                lng5d[ts5] = S.n["VS"]

                for oc in range(NCH):
                    @blk.tensor
                    def _(t):
                        if oc == 0:
                            if ts5 == 0:
                                t.wait_ge(S.h["LG2"], ldg_wb)
                            t.wait_ge(S.h["VS"], lng5d[ts5])
                        if len(pq5_hist) >= 2:
                            t.wait_ge(S.h[pq5_hist[-2][0]],
                                      pq5_hist[-2][1])
                        for c in range(NCH):
                            i = t.matmul(
                                pq5[oc % 2][:],
                                WB5[:, c * D + oc * P:c * D + (oc + 1) * P],
                                LNG[:, c * 512:(c + 1) * 512],
                                start=(c == 0), stop=(c == NCH - 1))
                        S.inc(i, "TS")
                        oproj5[(ts5, oc)] = S.n["TS"]

                    @blk.vector
                    def _(v):
                        v = AV.bind(v)
                        v.wait_ge(S.h["TS"], oproj5[(ts5, oc)])
                        xo = oc * T + gt0
                        v.tensor_tensor(X[:, xo:xo + 512], X[:, xo:xo + 512],
                                        pq5[oc % 2][:], Alu.add)
                        i = v.tensor_scalar(X[:, xo:xo + 512],
                                            X[:, xo:xo + 512],
                                            nv1[:, NCH + oc:NCH + oc + 1],
                                            None, Alu.add)
                        pq5_hist.append(("VS", S.n["VS"]))

            p5b.close()
            open_stacks.remove(p5b)
            p5.close()
            open_stacks.remove(p5)
        except StopEmission:
            for stk in reversed(open_stacks):
                stk.close()


        if TAP in ("xn2", "xq"):
            tap_marks = []
            for c in range(NCH):
                for sl4 in range(4):
                    @blk.vector
                    def _(v):
                        v = AV.bind(v)
                        S.wait(v, "TS")
                        S.wait(v, "SS")
                        if tap_marks:
                            v.wait_ge(S.h["LDG"], tap_marks[-1])
                        v.tensor_copy(
                            TMP0[:],
                            XN2[:, c * T + sl4 * 512:c * T + (sl4 + 1) * 512])
                    m = S.n["VS"]

                    @blk.gpsimd
                    def _(g):
                        g.wait_ge(S.h["LDG"], S.n["LDG"])
                        g.wait_ge(S.h["VS"], m)
                        i = g.dma_start(
                            out=y_out[c * P:(c + 1) * P,
                                      sl4 * 512:(sl4 + 1) * 512],
                            in_=TMP0[:])
                        S.inc(i, "LDG", 16)
                    tap_marks.append(S.n["LDG"])

            @blk.gpsimd
            def _(g):
                g.wait_ge(S.h["LDG"], S.n["LDG"])
        else:
            @blk.gpsimd
            def _(g):
                g.wait_ge(S.h["LDG"], S.n["LDG"])
                S.wait(g, "LD", ld_x)
                S.wait(g, "VS")
                S.wait(g, "TS")
                S.wait(g, "SS")
                for c in range(NCH):
                    i = g.dma_start(out=y_out[c * P:(c + 1) * P, :],
                                    in_=X[:, c * T:(c + 1) * T])
                    S.inc(i, "LDG", 16)
                g.wait_ge(S.h["LDG"], S.n["LDG"])

    return nc


def _prep_inputs(inputs):
    f32 = np.float32
    x = np.ascontiguousarray(inputs["L0_attn_x"], dtype=f32)

    def col8(vname):
        return np.ascontiguousarray(
            inputs[vname].astype(f32).reshape(NCH, P).T)

    nvec0 = np.concatenate([col8("L0_attn_norm_w"), col8("L0_attn_norm_b"),
                            col8("L0_attn_out_norm_w"),
                            col8("L0_attn_out_norm_b")], axis=1)
    nvec1 = np.concatenate([col8("L1_attn_bg"), col8("L1_attn_bo"),
                            col8("L1_attn_ln_gamma"),
                            col8("L1_attn_ln_beta")], axis=1)

    wg = inputs["L0_ffn_w_gate"].astype(f32)
    wu = inputs["L0_ffn_w_up"].astype(f32)
    wd = inputs["L0_ffn_w_down"].astype(f32)
    wg_p = np.zeros((8 * 384, D), f32)
    wg_p[:FF0] = wg
    wu_p = np.zeros((8 * 384, D), f32)
    wu_p[:FF0] = wu

    up = inputs["L1_ffn_expert_up"].astype(f32)
    dn = inputs["L1_ffn_expert_down"].astype(f32)
    upT = np.ascontiguousarray(
        up.transpose(0, 2, 1).reshape(8, D, 32, P).transpose(0, 2, 1, 3))
    dnT = np.ascontiguousarray(
        dn.transpose(0, 2, 1).reshape(8, FFE, NCH, P).transpose(0, 2, 1, 3))

    identb = np.eye(P, dtype=f32).astype(BF)
    identf = np.eye(P, dtype=f32)

    gam = np.array(GAMMAS, dtype=np.float64)
    i_idx = np.arange(CH, dtype=np.float64)
    gpos = np.zeros((D, CH), f32)
    gdecC = np.zeros((D, CH), f32)
    for h in range(NH1):
        gpos[h * DH:(h + 1) * DH, :] = (gam[h] ** i_idx / 8.0).astype(f32)
        gdecC[h * DH:(h + 1) * DH, :] = (gam[h] ** (CH - i_idx)).astype(f32)
    ii = i_idx[None, :]
    jj = i_idx[:, None]
    mask3 = np.zeros((NH1, CH, CH), f32)
    for h in range(NH1):
        mask3[h] = ((ii >= jj) * gam[h] ** (-float(CH))).astype(f32)
    gchunk = np.zeros((NH1, 16), f32)
    for h in range(NH1):
        gchunk[h] = (gam[h] ** (CH * np.arange(16, dtype=np.float64))).astype(f32)

    shared = {
        "wq0": inputs["L0_attn_Wq"].astype(f32),
        "wkv0": inputs["L0_attn_Wkv"].astype(f32),
        "wo0": inputs["L0_attn_Wo"].astype(f32),
        "nvec0": nvec0, "nvec1": nvec1,
        "wq1": inputs["L1_attn_Wq"].astype(f32),
        "wk1": inputs["L1_attn_Wk"].astype(f32),
        "wv1": inputs["L1_attn_Wv"].astype(f32),
        "wg1": inputs["L1_attn_Wg"].astype(f32),
        "wo1": inputs["L1_attn_Wo"].astype(f32),
        "rw": inputs["L1_ffn_router_w"].astype(f32),
        "rbias": inputs["L1_ffn_router_b"].astype(f32).reshape(1, 8),
        "upT": upT, "dnT": dnT,
        "identb": identb, "identf": identf,
        "gpos": gpos, "gdecC": gdecC, "mask3": mask3, "gchunk": gchunk,
    }

    in_maps = []
    for c in range(8):
        s, h = c // 2, c % 2
        own = np.ascontiguousarray(x[s, h * T:(h + 1) * T, :].T)
        part = np.ascontiguousarray(x[s, (1 - h) * T:(2 - h) * T, :].T)
        m = dict(shared)
        m["x_own"] = own
        m["x_part"] = part
        m["sgate"] = np.full((P, 1), float(h), f32)
        m["wg_sl"] = np.ascontiguousarray(wg_p[c * 384:(c + 1) * 384])
        m["wu_sl"] = np.ascontiguousarray(wu_p[c * 384:(c + 1) * 384])
        m["wd_sl"] = np.ascontiguousarray(wd[c * P:(c + 1) * P])
        in_maps.append(m)
    return in_maps


_NC_CACHE = {}


def kernel(**inputs):
    in_maps = _prep_inputs(inputs)
    if PHASES not in _NC_CACHE:
        _NC_CACHE[PHASES] = build_program()
    nc = _NC_CACHE[PHASES]
    res = run_bass_kernel_spmd(nc, in_maps, core_ids=list(range(8)))
    out = np.empty((4, SEQ, D), np.float32)
    for c in range(8):
        s, h = c // 2, c % 2
        out[s, h * T:(h + 1) * T, :] = res.results[c]["y"].T
    return out


# version 05 (final)
# speedup vs baseline: 1.0509x; 1.0241x over previous
import contextlib
import numpy as np
import ml_dtypes
import concourse.bass as bass
import concourse.mybir as mybir
from concourse.bass_utils import run_bass_kernel_spmd

F32 = mybir.dt.float32
BF16 = mybir.dt.bfloat16
Alu = mybir.AluOpType
Act = mybir.ActivationFunctionType
AX = mybir.AxisListType
BF = ml_dtypes.bfloat16

P = 128
T = 2048
D = 1024
NCH = D // P
SEQ = 4096
CH = 128
NCHK = T // CH
NH0 = 8
NH1 = 16
DH = 64
FF0 = 2816
FFE = 4096
EPS0 = 1e-5
EPS1 = 1e-3
WQC = 6 * P * D + P * FF0

PHASES = 5
P2STAGE = 2
TAP2 = None
import os
DBG = os.environ.get("KDBG", "")
P1STAGE = 3
TAP = None
GAMMAS = [1.0 - 2.0 ** (-5.0 - h) for h in range(NH1)]


class Sems:

    def __init__(self, nc, stack, names):
        self.h = {}
        self.n = {}
        self.byhandle = {}
        for nm in names:
            self.h[nm] = stack.enter_context(nc.semaphore(nm))
            self.n[nm] = 0
            self.byhandle[id(self.h[nm])] = nm

    def inc(self, instr, nm, k=1):
        instr.then_inc(self.h[nm], k)
        self.n[nm] += k
        return self.n[nm]

    def wait(self, eng, nm, val=None):
        eng.wait_ge(self.h[nm], self.n[nm] if val is None else val)




def _apname(x):
    try:
        return x.tensor.name
    except AttributeError:
        return None


class AutoEng:

    def __init__(self, S, nm):
        self.S = S
        self.nm = nm
        self.w = {}
        self.r = {}
        self.waited = {}
        self.raw = None

    def bind(self, raw):
        self.raw = raw
        return self

    def wait_ge(self, handle, val):
        nm = self.S.byhandle.get(id(handle))
        if nm is not None:
            if self.waited.get(nm, -1) >= val:
                return
            self.waited[nm] = max(self.waited.get(nm, -1), val)
        self.raw.wait_ge(handle, val)

    def _emit(self, ins_n, outs_n, call, inc):
        need = {}

        def add(e):
            if e is not None:
                need[e[0]] = max(need.get(e[0], 0), e[1])

        for t in ins_n:
            if t:
                add(self.w.get(t))
        for t in outs_n:
            if t:
                add(self.w.get(t))
                add(self.r.get(t))
        for semnm, val in need.items():
            if self.waited.get(semnm, -1) < val:
                self.raw.wait_ge(self.S.h[semnm], val)
                self.waited[semnm] = val
        i = call()
        semnm = inc or self.nm
        i.then_inc(self.S.h[semnm], 1)
        self.S.n[semnm] += 1
        v = self.S.n[semnm]
        for t in outs_n:
            if t:
                self.w[t] = (semnm, v)
        for t in ins_n:
            if t:
                self.r[t] = (semnm, v)
        return i

    def memset(self, ap, const, inc=None):
        return self._emit([], [_apname(ap)],
                          lambda: self.raw.memset(ap, const), inc)

    def tensor_copy(self, out, in_, inc=None):
        return self._emit([_apname(in_)], [_apname(out)],
                          lambda: self.raw.tensor_copy(out, in_), inc)

    def tensor_tensor(self, out, in0, in1, op, inc=None):
        return self._emit([_apname(in0), _apname(in1)], [_apname(out)],
                          lambda: self.raw.tensor_tensor(out, in0, in1, op),
                          inc)

    def tensor_scalar(self, out, in0, s1, s2, op0, op1=None, inc=None):
        ins = [_apname(in0), _apname(s1), _apname(s2)]
        if op1 is None:
            call = lambda: self.raw.tensor_scalar(out, in0, s1, s2, op0)
        else:
            call = lambda: self.raw.tensor_scalar(out, in0, s1, s2, op0, op1)
        return self._emit(ins, [_apname(out)], call, inc)

    def scalar_tensor_tensor(self, out, in0, scalar, in1, op0, op1, inc=None):
        return self._emit(
            [_apname(in0), _apname(scalar), _apname(in1)], [_apname(out)],
            lambda: self.raw.scalar_tensor_tensor(out, in0, scalar, in1,
                                                  op0, op1), inc)

    def tensor_reduce(self, out, in_, axis, op, apply_absolute_value=None,
                      inc=None):
        return self._emit(
            [_apname(in_)], [_apname(out)],
            lambda: self.raw.tensor_reduce(
                out, in_, axis, op,
                apply_absolute_value=apply_absolute_value), inc)

    def reciprocal(self, out, in_, inc=None):
        return self._emit([_apname(in_)], [_apname(out)],
                          lambda: self.raw.reciprocal(out, in_), inc)

    def activation(self, out, in_, func, bias=0.0, scale=1.0, accum_out=None,
                   inc=None):
        outs = [_apname(out), _apname(accum_out)]
        kw = {"bias": bias, "scale": scale}
        if accum_out is not None:
            kw["accum_out"] = accum_out
        return self._emit([_apname(in_), _apname(bias), _apname(scale)], outs,
                          lambda: self.raw.activation(out, in_, func, **kw),
                          inc)

    def copy(self, out, in_, inc=None):
        return self._emit([_apname(in_)], [_apname(out)],
                          lambda: self.raw.copy(out, in_), inc)


def build_program():
    nc = bass.Bass("TRN2", target_bir_lowering=False)
    st = contextlib.ExitStack()

    def din(name, shape, dtype=F32):
        return nc.dram_tensor(name, shape, dtype, kind="ExternalInput").ap()

    x_own = din("x_own", [D, T])
    x_part = din("x_part", [D, T])
    sgate = din("sgate", [P, 1])
    wq0 = din("wq0", [D, 512])
    wkv0 = din("wkv0", [D, 128])
    wo0 = din("wo0", [512, D])
    nvec0 = din("nvec0", [P, 4 * NCH])
    wg_sl = din("wg_sl", [384, D])
    wu_sl = din("wu_sl", [384, D])
    wd_sl = din("wd_sl", [P, FF0])
    wq1 = din("wq1", [D, D])
    wk1 = din("wk1", [D, D])
    wv1 = din("wv1", [D, D])
    wg1 = din("wg1", [D, D])
    wo1 = din("wo1", [D, D])
    nvec1 = din("nvec1", [P, 4 * NCH])
    rw = din("rw", [D, 8])
    rbias = din("rbias", [1, 8])
    upT = din("upT", [8, 32, D, P])
    dnT = din("dnT", [8, 8, FFE, P])
    identb = din("identb", [P, P], BF16)
    identf = din("identf", [P, P])
    gpos = din("gpos", [D, CH])
    gdecC = din("gdecC", [D, CH])
    mask3 = din("mask3", [NH1, CH, CH])
    gchunk = din("gchunk", [NH1, 16])
    gfix = din("gfix", [P, 16 * 8])

    y_out = nc.dram_tensor("y", [D, T], F32, kind="ExternalOutput").ap()
    dbg_out = (nc.dram_tensor("dbg", [T, D], F32, kind="ExternalOutput").ap()
               if TAP2 else None)

    wq_contrib = nc.dram_tensor("wq_contrib", [WQC], BF16, kind="Internal").ap()
    wq_all = nc.dram_tensor("wq_all", [8, WQC], BF16, kind="Internal").ap()
    st_contrib = nc.dram_tensor("st_contrib", [NH1 * DH * DH], F32,
                                kind="Internal").ap()
    wgq_c = nc.dram_tensor("wgq_c", [FF0, D], BF16, kind="Internal").ap()
    wuq_c = nc.dram_tensor("wuq_c", [FF0, D], BF16, kind="Internal").ap()
    wdq_c = nc.dram_tensor("wdq_c", [D, FF0], BF16, kind="Internal").ap()
    hq_dram = nc.dram_tensor("hq_dram", [T, FF0], BF16, kind="Internal").ap()
    xq_dram = nc.dram_tensor("xq_dram", [NCH, T // P, P, P], BF16,
                             kind="Internal").ap()
    ret_dram = nc.dram_tensor("ret_dram", [D, T], BF16, kind="Internal").ap()
    st_all = nc.dram_tensor("st_all", [2, NH1 * DH * DH], F32,
                            kind="Internal").ap()

    with st:
        def sb(name, shape, dt=F32, stk=st):
            return stk.enter_context(nc.sbuf_tensor(name, shape, dt))

        def psum(name, shape, dt=F32, stk=st):
            return stk.enter_context(nc.psum_tensor(name, shape, dt))

        X = sb("X", [P, NCH * T])
        XN2 = sb("XN2", [P, NCH * T], BF16)
        nv0 = sb("nv0", [P, 4 * NCH])
        nv1 = sb("nv1", [P, 4 * NCH])
        rbt = sb("rbt", [1, 8])
        sgt = sb("sgt", [P, 1])
        idb = sb("idb", [P, P], BF16)
        idf = sb("idf", [P, P])
        ones_st = sb("ones_st", [P, 1], BF16)
        ones_rep = sb("ones_rep", [1, P])
        R = [sb(f"R{i}", [1, 512]) for i in range(6)]
        TMP0 = sb("TMP0", [P, 512], F32)
        rows_stk = contextlib.ExitStack()
        mu_r = sb("mu_r", [1, T], F32, rows_stk)
        r_r = sb("r_r", [1, T], F32, rows_stk)

        S = Sems(nc, st, ["LD", "LDG", "TS", "VS", "SS", "GS", "ES", "OS", "OS2", "CC", "LD2", "LG2", "PH"])
        AV = AutoEng(S, "VS")
        ASS = AutoEng(S, "SS")
        blk = st.enter_context(nc.Block())

        def row_rsqrt(a_row, out_row, t_row, n=512):
            def ap(x):
                return x[:, 0:n] if hasattr(x, "ap") is False or True else x
            a, o, tr = a_row[:, 0:n], out_row[:, 0:n], t_row[:, 0:n]

            @blk.scalar
            def _(s):
                s = ASS.bind(s)
                S.wait(s, "VS")
                i = s.activation(tr, a, Act.Sqrt)

            @blk.vector
            def _(v):
                v = AV.bind(v)
                S.wait(v, "SS")
                v.reciprocal(o, tr)
                v.tensor_tensor(tr, o, o, Alu.mult)
                v.tensor_tensor(tr, tr, a, Alu.mult)
                v.tensor_scalar(tr, tr, -0.5, 1.5, Alu.mult, Alu.add)
                i = v.tensor_tensor(o, o, tr, Alu.mult)

        p0 = contextlib.ExitStack()
        xbfb = [sb(f"xbfb{i}", [P, NCH * 512], BF16, p0) for i in range(2)]
        rho_r = sb("rho_r", [1, T], F32, p0)
        xsqb = [sb(f"xsqb{i}", [P, NCH * 512], BF16, p0) for i in range(2)]
        psums = [psum(f"p0sum{i}", [1, 512], F32, p0) for i in range(2)]
        psumq = [psum(f"p0sq{i}", [1, 512], F32, p0) for i in range(2)]
        prep = [psum(f"p0rep{i}", [P, 512], F32, p0) for i in range(4)]

        @blk.sync
        def _(q):
            for c in range(NCH):
                i = q.dma_start(out=X[:, c * T:(c + 1) * T],
                                in_=x_own[c * P:(c + 1) * P, :])
                S.inc(i, "LD", 16)
            for dst, src in ((nv0, nvec0), (nv1, nvec1), (rbt, rbias),
                             (sgt, sgate), (idb, identb), (idf, identf)):
                i = q.dma_start(out=dst[:], in_=src[:])
                S.inc(i, "LD", 16)
        ld_x = S.n["LD"]

        vs_copy = []
        ss_sq = []
        ts_stat = []
        ts0_base = S.n["TS"]

        def ts_stat_pred(sl):
            return ts0_base + sl + 1

        @blk.vector
        def _(v):
            v = AV.bind(v)
            v.memset(ones_st[:], 1.0)
            i = v.memset(ones_rep[:], 1.0)
            S.wait(v, "LD", ld_x)
            for sl in range(4):
                if sl >= 2:
                    v.wait_ge(S.h["TS"], ts_stat_pred(sl - 2))
                for c in range(NCH):
                    o = c * T + sl * 512
                    i = v.tensor_copy(xbfb[sl % 2][:, c * 512:(c + 1) * 512],
                                      X[:, o:o + 512])
                vs_copy.append(S.n["VS"])

        @blk.scalar
        def _(s):
            s = ASS.bind(s)
            S.wait(s, "LD", ld_x)
            for sl in range(4):
                if sl >= 2:
                    s.wait_ge(S.h["TS"], ts_stat_pred(sl - 2))
                for c in range(NCH):
                    o = c * T + sl * 512
                    i = s.activation(xsqb[sl % 2][:, c * 512:(c + 1) * 512],
                                     X[:, o:o + 512], Act.Square)
                ss_sq.append(S.n["SS"])

        p0_freed = []
        for sl in range(4):
            o = sl * 512

            @blk.tensor
            def _(t):
                t.wait_ge(S.h["VS"], vs_copy[sl])
                t.wait_ge(S.h["SS"], ss_sq[sl])
                if sl >= 2:
                    t.wait_ge(S.h["VS"], p0_freed[sl - 2])
                for c in range(NCH):
                    i = t.matmul(psums[sl % 2][:], ones_st[:],
                                 xbfb[sl % 2][:, c * 512:(c + 1) * 512],
                                 start=(c == 0), stop=(c == NCH - 1))
                    i = t.matmul(psumq[sl % 2][:], ones_st[:],
                                 xsqb[sl % 2][:, c * 512:(c + 1) * 512],
                                 start=(c == 0), stop=(c == NCH - 1))
                S.inc(i, "TS")
                ts_stat.append(S.n["TS"])

            @blk.vector
            def _(v):
                v = AV.bind(v)
                v.wait_ge(S.h["TS"], ts_stat[sl])
                v.tensor_scalar(mu_r[:, o:o + 512], psums[sl % 2][:], 1.0 / D,
                                None, Alu.mult)
                v.tensor_scalar(R[0][:], psumq[sl % 2][:], 1.0 / D,
                                None, Alu.mult)
                p0_freed.append(S.n["VS"])
                v.tensor_tensor(R[1][:], mu_r[:, o:o + 512],
                                mu_r[:, o:o + 512], Alu.mult)
                v.tensor_tensor(R[0][:], R[0][:], R[1][:], Alu.subtract)
                i = v.tensor_scalar(R[1][:], R[0][:], EPS0, None, Alu.add)

            row_rsqrt(R[1], r_r[:, o:o + 512], R[2], n=512)

            @blk.vector
            def _(v):
                v = AV.bind(v)
                rr = r_r[:, o:o + 512]
                v.tensor_tensor(R[2][:], rr, rr, Alu.mult)
                v.tensor_tensor(R[1][:], R[0][:], R[2][:], Alu.mult)
                i = v.tensor_scalar(R[1][:], R[1][:], EPS0, None, Alu.add)

            row_rsqrt(R[1], R[3], R[2], n=512)

            @blk.vector
            def _(v):
                v = AV.bind(v)
                i = v.tensor_tensor(rho_r[:, o:o + 512], R[3][:],
                                    r_r[:, o:o + 512], Alu.mult)
        vs_rho = S.n["VS"]

        ts_rep = []
        xn2_freed = []
        for sl in range(4):
            o = sl * 512
            b = (sl % 2) * 2

            @blk.tensor
            def _(t):
                S.wait(t, "VS")
                if sl >= 2:
                    t.wait_ge(S.h["VS"], xn2_freed[sl - 2])
                i = t.matmul(prep[b][:], ones_rep[:], mu_r[:, o:o + 512],
                             start=True, stop=True)
                i = t.matmul(prep[b + 1][:], ones_rep[:], rho_r[:, o:o + 512],
                             start=True, stop=True)
                S.inc(i, "TS")
                ts_rep.append(S.n["TS"])

            @blk.vector
            def _(v):
                v = AV.bind(v)
                v.wait_ge(S.h["TS"], ts_rep[sl])
                for c in range(NCH):
                    xo = c * T + o
                    v.tensor_tensor(TMP0[:], X[:, xo:xo + 512], prep[b][:],
                                    Alu.subtract)
                    v.tensor_tensor(TMP0[:], TMP0[:], prep[b + 1][:], Alu.mult)
                    i = v.scalar_tensor_tensor(
                        XN2[:, xo:xo + 512], TMP0[:], nv0[:, c:c + 1],
                        nv0[:, NCH + c:NCH + c + 1].broadcast_to((P, 512)),
                        Alu.mult, Alu.add)
                xn2_freed.append(S.n["VS"])

                p0.close()
        g_p0 = {k: S.n[k] for k in ("TS", "VS", "SS")}

        class StopEmission(Exception):
            pass

        open_stacks = []

        if PHASES >= 2:
          p2 = contextlib.ExitStack()
          open_stacks.append(p2)
          def mk_qscratch(stk, qmax, tag):
              return {
                  "QA": sb(f"QA{tag}", [P, qmax], F32, stk),
                  "VVq": sb(f"VVq{tag}", [P, qmax], F32, stk),
                  "MBq": sb(f"MBq{tag}", [P, qmax // 16], F32, stk),
                  "MB2q": sb(f"MB2q{tag}", [P, qmax // 16], F32, stk),
                  "QB1": sb(f"QB1{tag}", [P, qmax], BF16, stk),
                  "QB2": sb(f"QB2{tag}", [P, qmax], BF16, stk),
                  "QB3": sb(f"QB3{tag}", [P, qmax], BF16, stk),
                  "prev": [None],
              }

          def quant_tile(qs, src, dst, W, src_wait, scale_col=None,
                         extra_wait=()):
              QA, VVq, MBq, MB2q, QB1, QB2, QB3, _qt_prev = (
                  qs["QA"], qs["VVq"], qs["MBq"], qs["MB2q"], qs["QB1"],
                  qs["QB2"], qs["QB3"], qs["prev"])
              """fp4 quant-dequant: src f32 [128 x W] -> dst bf16 [128 x W].
              src_wait: (semname, val) marking src ready (for S and V).
              extra_wait: additional (semname, val) waits on V (e.g. dst WAR)."""
              if not hasattr(src, "tensor"):
                  src = src[:]
              if not hasattr(dst, "tensor"):
                  dst = dst[:]
              nb = W // 16

              @blk.scalar
              def _(s):
                  s = ASS.bind(s)
                  if src_wait is not None:
                      s.wait_ge(S.h[src_wait[0]], src_wait[1])
                  if _qt_prev[0] is not None:
                      s.wait_ge(S.h["VS"], _qt_prev[0])
                  s.activation(QA[:, 0:W], src, Act.Abs)
              ss_abs = S.n["SS"]

              @blk.vector
              def _(v):
                  v = AV.bind(v)
                  if src_wait is not None:
                      v.wait_ge(S.h[src_wait[0]], src_wait[1])
                  for nm, val in extra_wait:
                      v.wait_ge(S.h[nm], val)
                  v.wait_ge(S.h["SS"], ss_abs)
                  qa3 = QA[:, 0:W].rearrange("p (b s) -> p b s", s=16)
                  v.tensor_reduce(MBq[:, 0:nb], qa3, AX.X, Alu.max)
                  v.tensor_scalar(MBq[:, 0:nb], MBq[:, 0:nb], 6e-12, None,
                                  Alu.max)
                  v.reciprocal(MB2q[:, 0:nb], MBq[:, 0:nb])
                  v.tensor_scalar(MB2q[:, 0:nb], MB2q[:, 0:nb], 6.0, None,
                                  Alu.mult)
                  bc = MB2q[:, 0:nb].unsqueeze(2).broadcast_to((P, nb, 16))
                  vv3 = VVq[:, 0:W].rearrange("p (b s) -> p b s", s=16)
                  v.tensor_tensor(vv3, qa3, bc, Alu.mult)
                  VV = VVq[:, 0:W]
                  b1, b2, b3 = QB1[:, 0:W], QB2[:, 0:W], QB3[:, 0:W]
                  v.tensor_scalar(b1, VV, 0.25, None, Alu.is_gt)
                  for th in (0.75, 1.25, 1.75):
                      v.tensor_scalar(b2, VV, th, None, Alu.is_gt)
                      v.tensor_tensor(b1, b1, b2, Alu.add)
                  v.tensor_scalar(b2, VV, 2.5, None, Alu.is_gt)
                  v.tensor_scalar(b3, VV, 3.5, None, Alu.is_gt)
                  v.tensor_tensor(b2, b2, b3, Alu.add)
                  v.scalar_tensor_tensor(b3, b1, 0.5, b2, Alu.mult, Alu.add)
                  v.tensor_scalar(b1, VV, 5.0, None, Alu.is_gt)
                  v.scalar_tensor_tensor(b2, b1, 2.0, b3, Alu.mult, Alu.add)
                  if scale_col is None:
                      v.tensor_scalar(MB2q[:, 0:nb], MBq[:, 0:nb], 1.0 / 6.0,
                                      None, Alu.mult)
                  else:
                      v.tensor_scalar(MB2q[:, 0:nb], MBq[:, 0:nb], scale_col,
                                      1.0 / 6.0, Alu.mult, Alu.mult)
                  bc2 = MB2q[:, 0:nb].unsqueeze(2).broadcast_to((P, nb, 16))
                  b23 = QB2[:, 0:W].rearrange("p (b s) -> p b s", s=16)
                  v.tensor_tensor(vv3, b23, bc2, Alu.mult)
                  v.tensor_scalar(b1, src, 0.0, None, Alu.is_ge)
                  v.tensor_tensor(QA[:, 0:W], b1, VV, Alu.mult)
                  v.scalar_tensor_tensor(dst, QA[:, 0:W], 2.0, VV,
                                         Alu.mult, Alu.subtract)
              _qt_prev[0] = S.n["VS"]
              return S.n["VS"]

          qs1 = mk_qscratch(p2, FF0 // 2, "a")

          NW = 6 * 1024 + FF0
          p2a = contextlib.ExitStack()
          open_stacks.append(p2a)
          WSL = sb("WSL", [P, 4864], F32, p2a)
          WQB = sb("WQB", [P, NW], BF16, p2a)

          @blk.sync
          def _(q):
              q.wait_ge(S.h["LD"], S.n["LD"])
              q.wait_ge(S.h["TS"], g_p0["TS"])
              q.wait_ge(S.h["VS"], g_p0["VS"])
              q.wait_ge(S.h["SS"], g_p0["SS"])
              for j in range(3):
                  i = q.dma_start(out=WSL[:, j * 1024:(j + 1) * 1024],
                                  in_=wg_sl[j * P:(j + 1) * P, :])
                  S.inc(i, "LD", 16)
              i = q.dma_start(out=WSL[:, 3 * 1024:4 * 1024],
                              in_=wu_sl[0:P, :])
              S.inc(i, "LD", 16)
          ld_wsl = S.n["LD"]

          wq_marks = []
          for j in range(4):
              wq_marks.append(quant_tile(qs1, WSL[:, j * 1024:(j + 1) * 1024],
                                         WQB[:, j * 1024:(j + 1) * 1024],
                                         1024, ("LD", ld_wsl)))

          @blk.sync
          def _(q):
              q.wait_ge(S.h["LD"], S.n["LD"])
              q.wait_ge(S.h["VS"], wq_marks[-1])
              for j in range(2):
                  i = q.dma_start(out=WSL[:, j * 1024:(j + 1) * 1024],
                                  in_=wu_sl[(j + 1) * P:(j + 2) * P, :])
                  S.inc(i, "LD", 16)
              i = q.dma_start(out=WSL[:, 2 * 1024:2 * 1024 + FF0],
                              in_=wd_sl[:])
              S.inc(i, "LD", 16)
          ld_wsl2 = S.n["LD"]

          for j in range(2):
              wq_marks.append(quant_tile(qs1, WSL[:, j * 1024:(j + 1) * 1024],
                                         WQB[:, (4 + j) * 1024:(5 + j) * 1024],
                                         1024, ("LD", ld_wsl2)))
          for hh in range(2):
              wq_marks.append(quant_tile(
                  qs1,
                  WSL[:, 2 * 1024 + hh * 1408:2 * 1024 + (hh + 1) * 1408],
                  WQB[:, 6 * 1024 + hh * 1408:6 * 1024 + (hh + 1) * 1408],
                  1408, ("LD", ld_wsl2)))

          ldg_contrib = []

          @blk.gpsimd
          def _(g):
              g.wait_ge(S.h["LDG"], S.n["LDG"])
              g.wait_ge(S.h["VS"], wq_marks[-1])
              ofs = 0
              for j in range(7):
                  w = 1024 if j < 6 else FF0
                  i = g.dma_start(
                      out=wq_contrib[ofs * P:(ofs + w) * P].rearrange(
                          "(a b) -> a b", a=P),
                      in_=WQB[:, ofs:ofs + w])
                  S.inc(i, "LDG", 16)
                  ofs += w
              g.wait_ge(S.h["LDG"], S.n["LDG"])
              ldg_contrib.append(S.n["LDG"])
              i = g.collective_compute(
                  "AllGather", Alu.bypass,
                  replica_groups=[[0, 1, 2, 3, 4, 5, 6, 7]],
                  ins=[wq_contrib[:]], outs=[wq_all[:]])
              S.inc(i, "CC", 1)
          p2a.close()
          open_stacks.remove(p2a)

          MUCOL = sb("MUCOL", [P, 16], F32, p2)
          RCOL = sb("RCOL", [P, 16], F32, p2)
          XQF_st = sb("XQF_st", [P, D], BF16, p2)
          p2b = contextlib.ExitStack()
          open_stacks.append(p2b)
          pcolm = psum("pcolm", [P, 16], F32, p2b)
          pcolr = psum("pcolr", [P, 16], F32, p2b)

          @blk.tensor
          def _(t):
              t.wait_ge(S.h["VS"], g_p0["VS"])
              t.wait_ge(S.h["SS"], g_p0["SS"])
              t.wait_ge(S.h["LDG"], ldg_contrib[0])
              for tc in range(16):
                  i = t.transpose(pcolm[:, tc:tc + 1],
                                  mu_r[0:1, tc * P:(tc + 1) * P],
                                  idf[0:1, 0:1])
                  i = t.transpose(pcolr[:, tc:tc + 1],
                                  r_r[0:1, tc * P:(tc + 1) * P],
                                  idf[0:1, 0:1])
              S.inc(i, "TS")
          ts_col = S.n["TS"]

          @blk.vector
          def _(v):
              v = AV.bind(v)
              v.wait_ge(S.h["TS"], ts_col)
              v.tensor_copy(MUCOL[:], pcolm[:])
              v.tensor_copy(RCOL[:], pcolr[:])
          vs_col = S.n["VS"]

          XCt = sb("XCt", [P, D], F32, p2b)
          XQt = sb("XQt", [P, D], BF16, p2b)
          pxt = [psum(f"pxt{i}", [P, P], F32, p2b) for i in range(2)]
          pqx = [psum(f"pqx{i}", [P, P], BF16, p2b) for i in range(2)]

          xtr = {}
          xct_dumped_prev = [0]
          xqst_mark = {}
          gs_b = S.n["GS"]

          def gs_xct(tc, c):
              return gs_b + tc * 16 + c + 1

          def gs_xqf(tc, c):
              return gs_b + tc * 16 + 8 + c + 1

          qm_prev = [None]
          for tc in range(16):
              @blk.tensor
              def _(t):
                  if tc == 0:
                      t.wait_ge(S.h["VS"], vs_col)
                  for c in range(NCH):
                      gidx = tc * NCH + c
                      if gidx >= 2:
                          pi = gidx - 2
                          t.wait_ge(S.h["GS"], gs_xct(pi // NCH, pi % NCH))
                      i = t.transpose(pxt[c % 2][:],
                                      X[:, c * T + tc * P:c * T + (tc + 1) * P],
                                      idf[:])
                      S.inc(i, "TS")
                      xtr[tc * NCH + c] = S.n["TS"]

              @blk.vector
              def _(v):
                  v = AV.bind(v)
                  if TAP2 == "xct" and tc >= 1:
                      v.wait_ge(S.h["LDG"], xct_dumped_prev[0])
                  for c in range(NCH):
                      v.wait_ge(S.h["TS"], xtr[tc * NCH + c])
                      v.tensor_scalar(XCt[:, c * P:(c + 1) * P], pxt[c % 2][:],
                                      MUCOL[:, tc:tc + 1], None, Alu.subtract,
                                      inc="GS")
              xct_done = ("GS", S.n["GS"])

              if TAP2 == "xct":
                  @blk.gpsimd
                  def _(g):
                      g.wait_ge(S.h["LDG"], S.n["LDG"])
                      g.wait_ge(S.h["GS"], xct_done[1])
                      i = g.dma_start(out=dbg_out[tc * P:(tc + 1) * P, :],
                                      in_=XCt[:])
                      S.inc(i, "LDG", 16)
                  xct_dumped_prev[0] = S.n["LDG"]

              ew = ()
              if tc >= 1:
                  ew = (("TS", xtr[200 + (tc - 1) * NCH + NCH - 1]),)
              if TAP2 == "xct":
                  ew = ew + (("LDG", xct_dumped_prev[0]),)
              qm = quant_tile(qs1, XCt, XQt, D, xct_done,
                              scale_col=RCOL[:, tc:tc + 1], extra_wait=ew)

              @blk.tensor
              def _(t):
                  t.wait_ge(S.h["VS"], qm)
                  for c in range(NCH):
                      gidx = tc * NCH + c
                      if gidx >= 2:
                          pi = gidx - 2
                          t.wait_ge(S.h["GS"], gs_xqf(pi // NCH, pi % NCH))
                      i = t.transpose(pqx[c % 2][:],
                                      XQt[:, c * P:(c + 1) * P], idb[:])
                      S.inc(i, "TS")
                      xtr[200 + tc * NCH + c] = S.n["TS"]

              @blk.vector
              def _(v):
                  v = AV.bind(v)
                  if tc >= 1 and "noxqst" not in DBG:
                      v.wait_ge(S.h["LDG"], xqst_mark[tc - 1])
                  for c in range(NCH):
                      v.wait_ge(S.h["TS"], xtr[200 + tc * NCH + c])
                      v.tensor_copy(XQF_st[:, c * P:(c + 1) * P],
                                    pqx[c % 2][:], inc="GS")

              @blk.gpsimd
              def _(g):
                  if "noxqst" in DBG:
                      return
                  g.wait_ge(S.h["LDG"], S.n["LDG"])
                  g.wait_ge(S.h["GS"], gs_xqf(tc, NCH - 1))
                  i = g.dma_start(
                      out=xq_dram[:, tc].rearrange("c p j -> p c j"),
                      in_=XQF_st[:].rearrange("p (c j) -> p c j", j=P))
                  S.inc(i, "LDG", 16)
              xqst_mark[tc] = S.n["LDG"]
          xq_st_done = S.n["LDG"]
          p2b.close()
          open_stacks.remove(p2b)
          p2.close()
          open_stacks.remove(p2)

        g_p0b = {k: S.n[k] for k in ("TS", "VS", "SS")}
        gs_p1v = S.n["GS"]

        try:
          if PHASES >= 1:
            p1 = contextlib.ExitStack()
            open_stacks.append(p1)
            k_ext = sb("k_ext", [65, SEQ], BF16, p1)
            vexE = sb("vexE", [P, 32 * 65], BF16, p1)
            vexO = sb("vexO", [P, 32 * 128], BF16, p1)
            AO = sb("AO", [P, 4 * T], BF16, p1)
            KMX = sb("KMX", [65, 4], F32, p1)
            CT = sb("CT", [65, 512], F32, p1)
            E65 = sb("E65", [P, 65], BF16, p1)
            E65h = sb("E65h", [P, 65], BF16, p1)
            ones64 = sb("ones64", [65, P], F32, p1)
            PM, PR, PS1, PS2, PS3 = R[4], R[5], R[1], R[2], R[3]

            @blk.vector
            def _(v):
                v = AV.bind(v)
                v.wait_ge(S.h["TS"], g_p0b["TS"])
                v.wait_ge(S.h["SS"], g_p0b["SS"])
                v.memset(k_ext[64:65, :], 1.0)
                v.memset(vexE[:], 0.0)
                v.memset(vexO[:], 0.0)
                for kc in range(32):
                    v.memset(vexE[:, kc * 65 + 64:kc * 65 + 65], 1.0)
                    v.memset(vexO[:, kc * 128:kc * 128 + 1], 1.0)
                v.memset(E65[:], 0.0)
                v.memset(E65h[:], 0.0)
                v.memset(E65[0:64, 64:65], 1.0)
                v.memset(E65h[64:128, 64:65], 1.0)
                i = v.memset(ones64[64:65, :], 1.0)

            a1 = contextlib.ExitStack()
            open_stacks.append(a1)
            pbf = sb("pbf", [P, NCH * 512], BF16, a1)
            pxn2 = sb("pxn2", [P, NCH * 512], BF16, a1)
            psq2 = pxn2
            VTb = sb("VTb", [64, 512], BF16, a1)
            ksqb = sb("ksqb", [64, 512], BF16, a1)
            wkv0b = sb("wkv0b", [P, NCH * 128], BF16, a1)

            @blk.gpsimd
            def _(g):
                g.wait_ge(S.h["LDG"], S.n["LDG"])
                g.wait_ge(S.h["TS"], g_p0b["TS"])
                g.wait_ge(S.h["VS"], g_p0b["VS"])
                g.wait_ge(S.h["SS"], g_p0b["SS"])
                for c in range(NCH):
                    i = g.dma_start(out=wkv0b[:, c * 128:(c + 1) * 128],
                                    in_=wkv0[c * P:(c + 1) * P, :])
                    S.inc(i, "LDG", 16)
            ldg_wkv = S.n["LDG"]
            pst = [psum(f"pst{i}", [1, 512], F32, a1) for i in range(2)]
            prp = [psum(f"prp{i}", [P, 512], F32, a1) for i in range(2)]
            pkv = [psum(f"pkv{i}", [P, 512], F32, a1) for i in range(2)]
            pkn = psum("pkn", [65, 512], F32, a1)
            pvt = psum("pvt", [P, 64], BF16, a1)

            kvmm = {}
            knmm = {}
            vtmm = {}
            kev = {}
            vcp = {}
            kmaxv = {}
            pbuffree = {}
            affdone = {}
            sqd = {}

            for sl in range(8):
                own = sl < 4
                if not own:
                    sp = sl - 4

                    @blk.gpsimd
                    def _(g):
                        g.wait_ge(S.h["LDG"], S.n["LDG"])
                        if sl > 4:
                            S.wait(g, "VS", affdone[sl - 1])
                        for c in range(NCH):
                            i = g.dma_start(
                                out=pbf[:, c * 512:(c + 1) * 512],
                                in_=x_part[c * P:(c + 1) * P,
                                           sp * 512:(sp + 1) * 512])
                            S.inc(i, "LDG", 16)
                    ld_pb = S.n["LDG"]
                    vs_pbf = S.n["VS"]

                    @blk.scalar
                    def _(s):
                        s = ASS.bind(s)
                        if sl == 4:
                            s.wait_ge(S.h["TS"], g_p0b["TS"])
                            s.wait_ge(S.h["VS"], g_p0b["VS"])
                        S.wait(s, "LDG", ld_pb)
                        if sl > 4:
                            s.wait_ge(S.h["TS"], kvmm[sl - 1])
                        for c in range(NCH):
                            i = s.activation(psq2[:, c * 512:(c + 1) * 512],
                                             pbf[:, c * 512:(c + 1) * 512],
                                             Act.Square)
                    ss_psq = S.n["SS"]

                    @blk.tensor
                    def _(t):
                        t.wait_ge(S.h["LDG"], ld_pb)
                        t.wait_ge(S.h["SS"], ss_psq)
                        for c in range(NCH):
                            i = t.matmul(pst[0][:], ones_st[:],
                                         pbf[:, c * 512:(c + 1) * 512],
                                         start=(c == 0), stop=(c == NCH - 1))
                            i = t.matmul(pst[1][:], ones_st[:],
                                         psq2[:, c * 512:(c + 1) * 512],
                                         start=(c == 0), stop=(c == NCH - 1))
                        S.inc(i, "TS")
                    ts_pstat = S.n["TS"]

                    @blk.vector
                    def _(v):
                        v = AV.bind(v)
                        v.wait_ge(S.h["TS"], ts_pstat)
                        v.tensor_scalar(PM[:], pst[0][:], 1.0 / D, None, Alu.mult)
                        v.tensor_scalar(PS1[:], pst[1][:], 1.0 / D, None, Alu.mult)
                        v.tensor_tensor(PS2[:], PM[:], PM[:], Alu.mult)
                        v.tensor_tensor(PS1[:], PS1[:], PS2[:], Alu.subtract)
                        i = v.tensor_scalar(PS1[:], PS1[:], EPS0, None, Alu.add)

                    row_rsqrt(PS1, PR, PS2, n=512)

                    @blk.vector
                    def _(v):
                        v = AV.bind(v)
                        v.tensor_scalar(PS1[:], PS1[:], -EPS0, None, Alu.add)
                        v.tensor_tensor(PS2[:], PR[:], PR[:], Alu.mult)
                        v.tensor_tensor(PS1[:], PS1[:], PS2[:], Alu.mult)
                        i = v.tensor_scalar(PS1[:], PS1[:], EPS0, None, Alu.add)

                    row_rsqrt(PS1, PS2, PS3, n=512)

                    @blk.vector
                    def _(v):
                        v = AV.bind(v)
                        i = v.tensor_tensor(PR[:], PR[:], PS2[:], Alu.mult)
                    vs_prho = S.n["VS"]

                    @blk.tensor
                    def _(t):
                        t.wait_ge(S.h["VS"], vs_prho)
                        i = t.matmul(prp[0][:], ones_rep[:], PM[:],
                                     start=True, stop=True)
                        i = t.matmul(prp[1][:], ones_rep[:], PR[:],
                                     start=True, stop=True)
                        S.inc(i, "TS")
                    ts_prep = S.n["TS"]

                    @blk.vector
                    def _(v):
                        v = AV.bind(v)
                        v.wait_ge(S.h["TS"], ts_prep)
                        v.wait_ge(S.h["TS"], ts_pstat)
                        for c in range(NCH):
                            o = c * 512
                            v.tensor_tensor(TMP0[:], pbf[:, o:o + 512],
                                            prp[0][:], Alu.subtract)
                            v.tensor_tensor(TMP0[:], TMP0[:], prp[1][:],
                                            Alu.mult)
                            i = v.scalar_tensor_tensor(
                                pxn2[:, o:o + 512], TMP0[:], nv0[:, c:c + 1],
                                nv0[:, NCH + c:NCH + c + 1].broadcast_to(
                                    (P, 512)), Alu.mult, Alu.add)
                    affdone[sl] = S.n["VS"]

                @blk.tensor
                def _(t):
                    if own:
                        t.wait_ge(S.h["VS"], S.n["VS"])
                        t.wait_ge(S.h["LDG"], ldg_wkv)
                    else:
                        t.wait_ge(S.h["VS"], affdone[sl])
                    if sl >= 2:
                        t.wait_ge(S.h["VS"], kev[sl - 2])
                        t.wait_ge(S.h["SS"], sqd[sl - 2])
                    for c in range(NCH):
                        if own:
                            mv = XN2[:, c * T + sl * 512:c * T + sl * 512 + 512]
                        else:
                            mv = pxn2[:, c * 512:(c + 1) * 512]
                        i = t.matmul(pkv[sl % 2][:],
                                     wkv0b[:, c * 128:(c + 1) * 128], mv,
                                     start=(c == 0), stop=(c == NCH - 1))
                    S.inc(i, "TS")
                kvmm[sl] = S.n["TS"]

                @blk.vector
                def _(v):
                    v = AV.bind(v)
                    v.wait_ge(S.h["TS"], kvmm[sl])
                    if sl >= 1:
                        v.wait_ge(S.h["TS"], vtmm[(sl - 1, 3)])
                    i = v.tensor_copy(k_ext[0:64, sl * 512:(sl + 1) * 512],
                                      pkv[sl % 2][0:64, :])
                    i = v.tensor_copy(VTb[:], pkv[sl % 2][64:128, :])
                kev[sl] = S.n["VS"]

                @blk.scalar
                def _(s):
                    s = ASS.bind(s)
                    s.wait_ge(S.h["VS"], kev[sl])
                    if sl >= 1:
                        s.wait_ge(S.h["TS"], knmm[sl - 1])
                    i = s.activation(ksqb[:],
                                     k_ext[0:64, sl * 512:(sl + 1) * 512],
                                     Act.Square)
                sqd[sl] = S.n["SS"]


                @blk.tensor
                def _(t):
                    t.wait_ge(S.h["SS"], sqd[sl])
                    if sl >= 1:
                        t.wait_ge(S.h["VS"], kmaxv[sl - 1])
                    i = t.matmul(pkn[:], E65[0:64, :], ksqb[:],
                                 start=True, stop=True)
                    S.inc(i, "TS")
                    knmm[sl] = S.n["TS"]
                    t.wait_ge(S.h["VS"], kev[sl])
                    for j in range(4):
                        if (sl, j) != (0, 0):
                            t.wait_ge(S.h["GS"], gs_p1v + 4 * sl + j)
                        i = t.transpose(pvt[:], VTb[:, j * 128:(j + 1) * 128],
                                        idb[0:64, 0:64])
                        S.inc(i, "TS")
                        vtmm[(sl, j)] = S.n["TS"]

                @blk.vector
                def _(v):
                    v = AV.bind(v)
                    v.wait_ge(S.h["TS"], knmm[sl])
                    v.tensor_reduce(KMX[64:65, 1:2], pkn[64:65, :], AX.X,
                                    Alu.max)
                    if sl == 0:
                        i = v.tensor_copy(KMX[64:65, 0:1], KMX[64:65, 1:2])
                    else:
                        i = v.tensor_tensor(KMX[64:65, 0:1], KMX[64:65, 0:1],
                                            KMX[64:65, 1:2], Alu.max)
                    kmaxv[sl] = S.n["VS"]
                    for j in range(4):
                        kc = sl * 4 + j
                        v.wait_ge(S.h["TS"], vtmm[(sl, j)])
                        v.tensor_copy(vexE[:, kc * 65:kc * 65 + 64], pvt[:])
                        i = v.tensor_copy(
                            vexO[:, kc * 128 + 64:(kc + 1) * 128], pvt[:],
                            inc="GS")

            @blk.scalar
            def _(s):
                s = ASS.bind(s)
                S.wait(s, "VS")
                i = s.activation(KMX[64:65, 2:3], KMX[64:65, 0:1], Act.Sqrt)
            ss_kmax = S.n["SS"]
            a1.close()
            open_stacks.remove(a1)
            g_a1 = {k: S.n[k] for k in ("TS", "VS", "SS")}

            if P1STAGE < 1.3:
                raise StopEmission
            a2 = contextlib.ExitStack()
            open_stacks.append(a2)
            wq0b = sb("wq0b", [P, NCH * 512], BF16, a2)
            qex = [sb(f"qex{i}", [65, NH0 * 512], BF16, a2) for i in range(2)]

            @blk.gpsimd
            def _(g):
                g.wait_ge(S.h["LDG"], S.n["LDG"])
                g.wait_ge(S.h["TS"], g_a1["TS"])
                g.wait_ge(S.h["VS"], g_a1["VS"])
                g.wait_ge(S.h["SS"], g_a1["SS"])
                for c in range(NCH):
                    i = g.dma_start(out=wq0b[:, c * 512:(c + 1) * 512],
                                    in_=wq0[c * P:(c + 1) * P, :])
                    S.inc(i, "LDG", 16)
            ldg_wq = S.n["LDG"]
            qsqb = sb("qsqb", [P, 512], BF16, a2)
            expb = [sb(f"expb{i}", [P, 512], BF16, a2) for i in range(2)]
            REPT = TMP0
            AR0 = PS3
            AR64 = sb("AR64", [65, 512], F32, a2)
            pq = [psum(f"pq{i}", [P, 512], F32, a2) for i in range(2)]
            pqn = psum("pqn", [65, 512], F32, a2)
            psim = [psum(f"psim{i}", [P, 512], F32, a2) for i in range(2)]
            pav = [psum(f"pav{i}", [P, 512], F32, a2) for i in range(2)]
            prep1 = psum("prep1", [P, 512], F32, a2)

            qmmv = {}
            qevv = {}
            qsqv = {}
            qnv = {}
            crowv = {}
            ctsq = {}
            simv = {}
            expv = {}
            avv = {}
            repv = {}
            arv = {}
            reptv = {}
            nrmv = {}

            def sh_prev(slab, h, k):
                idx = slab * NH0 + h - k
                if idx < 0:
                    return None
                return (idx // NH0, idx % NH0)

            os_q0 = S.n["OS"]
            os_q0b = S.n["OS2"]
            for slab in (() if "noqmm" in DBG else range(4)):
                for oc in range(4):
                    h_lo, h_hi = 2 * oc, 2 * oc + 1

                    @blk.tensor
                    def _(t):
                        if (slab, oc) == (0, 0):
                            t.wait_ge(S.h["LDG"], ldg_wq)
                            t.wait_ge(S.h["VS"], g_a1["VS"])
                            t.wait_ge(S.h["SS"], g_a1["SS"])
                        if oc == 0 and slab >= 2 and P1STAGE >= 2:
                            t.wait_ge(S.h["TS"], simv[(slab - 2, NH0 - 1, 31)])
                        if (slab, oc) not in ((0, 0), (0, 1)):
                            po = slab * 4 + oc - 2
                            t.wait_ge(S.h["OS"], os_q0 + po + 1)
                        for c in range(NCH):
                            i = t.matmul(
                                pq[oc % 2][:],
                                wq0b[:, c * 512 + oc * 128:
                                     c * 512 + (oc + 1) * 128],
                                XN2[:, c * T + slab * 512:
                                    c * T + (slab + 1) * 512],
                                start=(c == 0), stop=(c == NCH - 1))
                        S.inc(i, "TS")
                        qmmv[(slab, oc)] = S.n["TS"]

                    @blk.vector
                    def _(v):
                        if "noqev" in DBG:
                            return
                        v = AV.bind(v)
                        if (slab, oc) == (0, 0):
                            v.wait_ge(S.h["SS"], g_a1["SS"])
                        v.wait_ge(S.h["TS"], qmmv[(slab, oc)])
                        v.tensor_scalar(
                            qex[slab % 2][0:64, h_lo * 512:(h_lo + 1) * 512],
                            pq[oc % 2][0:64, :], 0.125, None, Alu.mult)
                        i = v.tensor_scalar(
                            qex[slab % 2][0:64, h_hi * 512:(h_hi + 1) * 512],
                            pq[oc % 2][64:128, :], 0.125, None, Alu.mult,
                            inc="OS")

                    for par, h in (((0, h_lo), (1, h_hi)) if P1STAGE >= 1.6 else ()):
                        @blk.scalar
                        def _(s):
                            s = ASS.bind(s)
                            s.wait_ge(S.h["OS"], os_q0 + slab * 4 + oc + 1)
                            pv = sh_prev(slab, h, 1)
                            if pv is not None:
                                s.wait_ge(S.h["TS"], qnv[pv])
                            i = s.activation(qsqb[0:64, :],
                                             qex[slab % 2][0:64,
                                                           h * 512:(h + 1) * 512],
                                             Act.Square, inc="OS2")

                        @blk.tensor
                        def _(t):
                            t.wait_ge(S.h["OS2"],
                                      os_q0b + (slab * NH0 + h) + 1)
                            pv = sh_prev(slab, h, 1)
                            if pv is not None:
                                t.wait_ge(S.h["SS"], ctsq[pv])
                            i = t.matmul(pqn[:], E65[0:64, :], qsqb[0:64, :],
                                         start=True, stop=True)
                            S.inc(i, "TS")
                            qnv[(slab, h)] = S.n["TS"]

                        @blk.scalar
                        def _(s):
                            s = ASS.bind(s)
                            s.wait_ge(S.h["TS"], qnv[(slab, h)])
                            s.wait_ge(S.h["SS"], ss_kmax)
                            pv = sh_prev(slab, h, 1)
                            if pv is not None:
                                s.wait_ge(S.h["VS"], crowv[pv])
                            i = s.activation(CT[64:65, :], pqn[64:65, :],
                                             Act.Sqrt)
                            ctsq[(slab, h)] = S.n["SS"]

                        @blk.vector
                        def _(v):
                            v = AV.bind(v)
                            v.wait_ge(S.h["SS"], ctsq[(slab, h)])
                            v.tensor_scalar(CT[64:65, :], CT[64:65, :],
                                            KMX[64:65, 2:3], -30.0,
                                            Alu.mult, Alu.add)
                            i = v.tensor_scalar(
                                qex[slab % 2][64:65, h * 512:(h + 1) * 512],
                                CT[64:65, :], 0.0, -1.0, Alu.max, Alu.mult)
                            crowv[(slab, h)] = S.n["VS"]

                for h in (range(NH0) if P1STAGE >= 2 else ()):
                    par = h % 2
                    vext = vexE if par == 0 else vexO
                    vw = 65 if par == 0 else 128
                    bav = (slab * NH0 + h) % 2

                    @blk.tensor
                    def _(t):
                        t.wait_ge(S.h["VS"], crowv[(slab, h)])
                        t.wait_ge(S.h["OS"],
                                  os_q0 + slab * 4 + h // 2 + 1)
                        pv = sh_prev(slab, h, 2)
                        if pv is not None:
                            t.wait_ge(S.h["VS"], nrmv[pv])
                        for kc in range(32):
                            eidx = (slab * NH0 + h) * 32 + kc
                            if eidx >= 2:
                                t.wait_ge(S.h["ES"], eidx - 1)
                            i = t.matmul(
                                psim[kc % 2][:],
                                k_ext[0:65, kc * 128:(kc + 1) * 128],
                                qex[slab % 2][0:65, h * 512:(h + 1) * 512],
                                start=True, stop=True)
                            S.inc(i, "TS")
                            simv[(slab, h, kc)] = S.n["TS"]
                            if kc >= 1:
                                t.wait_ge(S.h["ES"],
                                          (slab * NH0 + h) * 32 + kc)
                                i = t.matmul(pav[bav][0:vw, :],
                                             vext[:, (kc - 1) * vw:kc * vw],
                                             expb[(kc - 1) % 2][:],
                                             start=(kc - 1 == 0), stop=False)
                                S.inc(i, "TS")
                                avv[(slab, h, kc - 1)] = S.n["TS"]
                        t.wait_ge(S.h["ES"], (slab * NH0 + h) * 32 + 32)
                        i = t.matmul(pav[bav][0:vw, :], vext[:, 31 * vw:32 * vw],
                                     expb[31 % 2][:], start=False, stop=True)
                        S.inc(i, "TS")
                        avv[(slab, h, 31)] = S.n["TS"]

                    @blk.scalar
                    def _(s):
                        s = ASS.bind(s)
                        for kc in range(32):
                            s.wait_ge(S.h["TS"], simv[(slab, h, kc)])
                            if kc >= 2:
                                s.wait_ge(S.h["TS"], avv[(slab, h, kc - 2)])
                            elif sh_prev(slab, h, 1) is not None:
                                pp = sh_prev(slab, h, 1)
                                s.wait_ge(S.h["TS"], avv[(pp[0], pp[1],
                                                          30 + kc)])
                            i = s.activation(expb[kc % 2][:], psim[kc % 2][:],
                                             Act.Exp, inc="ES")

                    sumrow = 64 if par == 0 else 0
                    arr = AR64[64:65, :] if par == 0 else AR0[:]

                    @blk.vector
                    def _(v):
                        v = AV.bind(v)
                        v.wait_ge(S.h["TS"], avv[(slab, h, 31)])
                        i = v.reciprocal(arr, pav[bav][sumrow:sumrow + 1, :])
                        arv[(slab, h)] = S.n["VS"]

                    @blk.tensor
                    def _(t):
                        t.wait_ge(S.h["VS"], arv[(slab, h)])
                        pv = sh_prev(slab, h, 1)
                        if pv is not None:
                            t.wait_ge(S.h["VS"], reptv[pv])
                        st_ = ones64[64:65, :] if par == 0 else ones_rep[:]
                        i = t.matmul(prep1[:], st_, arr, start=True, stop=True)
                        S.inc(i, "TS")
                        repv[(slab, h)] = S.n["TS"]

                    @blk.vector
                    def _(v):
                        v = AV.bind(v)
                        v.wait_ge(S.h["TS"], repv[(slab, h)])
                        i = v.tensor_copy(REPT[:], prep1[:])
                        reptv[(slab, h)] = S.n["VS"]
                        rows = slice(0, 64) if par == 0 else slice(64, 128)
                        dst = AO[64 * par:64 * par + 64,
                                 (h // 2) * T + slab * 512:
                                 (h // 2) * T + (slab + 1) * 512]
                        i = v.tensor_tensor(dst, pav[bav][rows, :],
                                            REPT[rows, :], Alu.mult)
                        nrmv[(slab, h)] = S.n["VS"]

            a2.close()
            open_stacks.remove(a2)
            g_a2 = {k: S.n[k] for k in ("TS", "VS", "SS")}

            if P1STAGE < 3:
                raise StopEmission
            a4 = contextlib.ExitStack()
            open_stacks.append(a4)
            wo0b = sb("wo0b", [P, 4 * D], BF16, a4)
            y0sl = sb("y0sl", [P, NCH * 512], BF16, a4)

            @blk.gpsimd
            def _(g):
                g.wait_ge(S.h["LDG"], S.n["LDG"])
                g.wait_ge(S.h["TS"], g_a2["TS"])
                g.wait_ge(S.h["VS"], g_a2["VS"])
                g.wait_ge(S.h["SS"], g_a2["SS"])
                for ic in range(4):
                    i = g.dma_start(out=wo0b[:, ic * D:(ic + 1) * D],
                                    in_=wo0[ic * P:(ic + 1) * P, :])
                    S.inc(i, "LDG", 16)
            ldg_wo = S.n["LDG"]
            y0sq = sb("y0sq", [P, NCH * 512], BF16, a4)
            po = [psum(f"po{i}", [P, 512], F32, a4) for i in range(2)]
            pst4 = [psum(f"pst4{i}", [1, 512], F32, a4) for i in range(2)]
            prp4 = [psum(f"prp4{i}", [P, 512], F32, a4) for i in range(2)]

            oev = {}
            osq = {}
            omm = {}
            ost = {}
            orow = {}
            orep = {}
            oadd = {}

            os_y0 = S.n["OS"]
            os_y0b = S.n["OS2"]
            for slab in range(4):
                for oc in range(NCH):
                    @blk.tensor
                    def _(t):
                        if (slab, oc) == (0, 0):
                            t.wait_ge(S.h["LDG"], ldg_wo)
                            t.wait_ge(S.h["VS"], g_a2["VS"])
                            t.wait_ge(S.h["SS"], g_a2["SS"])
                        if (slab, oc) not in ((0, 0), (0, 1)):
                            pidx = slab * NCH + oc - 2
                            t.wait_ge(S.h["OS"], os_y0 + pidx + 1)
                            t.wait_ge(S.h["OS2"], os_y0b + pidx + 1)
                        for ic in range(4):
                            stat = wo0b[:, ic * D + oc * P:
                                        ic * D + (oc + 1) * P]
                            mov = AO[:, ic * T + slab * 512:
                                     ic * T + (slab + 1) * 512]
                            i = t.matmul(po[oc % 2][:], stat, mov,
                                         start=(ic == 0), stop=(ic == 3))
                        S.inc(i, "TS")
                        omm[(slab, oc)] = S.n["TS"]

                    @blk.vector
                    def _(v):
                        v = AV.bind(v)
                        if (slab, oc) == (0, 0):
                            v.wait_ge(S.h["SS"], g_a2["SS"])
                        v.wait_ge(S.h["TS"], omm[(slab, oc)])
                        i = v.tensor_copy(y0sl[:, oc * 512:(oc + 1) * 512],
                                          po[oc % 2][:], inc="OS")

                    @blk.scalar
                    def _(s):
                        s = ASS.bind(s)
                        if (slab, oc) == (0, 0):
                            s.wait_ge(S.h["VS"], g_a2["VS"])
                        s.wait_ge(S.h["OS"], os_y0 + slab * NCH + oc + 1)
                        i = s.activation(y0sq[:, oc * 512:(oc + 1) * 512],
                                         y0sl[:, oc * 512:(oc + 1) * 512],
                                         Act.Square, inc="OS2")

                @blk.tensor
                def _(t):
                    t.wait_ge(S.h["OS"], os_y0 + slab * NCH + NCH)
                    t.wait_ge(S.h["OS2"], os_y0b + slab * NCH + NCH)
                    if slab >= 1:
                        t.wait_ge(S.h["VS"], orow[slab - 1])
                    for c in range(NCH):
                        i = t.matmul(pst4[0][:], ones_st[:],
                                     y0sl[:, c * 512:(c + 1) * 512],
                                     start=(c == 0), stop=(c == NCH - 1))
                        i = t.matmul(pst4[1][:], ones_st[:],
                                     y0sq[:, c * 512:(c + 1) * 512],
                                     start=(c == 0), stop=(c == NCH - 1))
                    S.inc(i, "TS")
                    ost[slab] = S.n["TS"]

                @blk.vector
                def _(v):
                    v = AV.bind(v)
                    v.wait_ge(S.h["TS"], ost[slab])
                    v.tensor_scalar(PM[:], pst4[0][:], 1.0 / D, None, Alu.mult)
                    v.tensor_scalar(PS1[:], pst4[1][:], 1.0 / D, None, Alu.mult)
                    v.tensor_tensor(PS2[:], PM[:], PM[:], Alu.mult)
                    v.tensor_tensor(PS1[:], PS1[:], PS2[:], Alu.subtract)
                    i = v.tensor_scalar(PS1[:], PS1[:], EPS0, None, Alu.add)

                row_rsqrt(PS1, PR, PS2, n=512)

                orow[slab] = S.n["VS"]

                @blk.tensor
                def _(t):
                    t.wait_ge(S.h["VS"], orow[slab])
                    if slab >= 1:
                        t.wait_ge(S.h["VS"], oadd[slab - 1])
                    i = t.matmul(prp4[0][:], ones_rep[:], PM[:],
                                 start=True, stop=True)
                    i = t.matmul(prp4[1][:], ones_rep[:], PR[:],
                                 start=True, stop=True)
                    S.inc(i, "TS")
                    orep[slab] = S.n["TS"]

                @blk.vector
                def _(v):
                    v = AV.bind(v)
                    v.wait_ge(S.h["TS"], orep[slab])
                    for c in range(NCH):
                        xo = c * T + slab * 512
                        yo = c * 512
                        v.tensor_tensor(TMP0[:], y0sl[:, yo:yo + 512],
                                        prp4[0][:], Alu.subtract)
                        v.tensor_tensor(TMP0[:], TMP0[:], prp4[1][:], Alu.mult)
                        v.scalar_tensor_tensor(X[:, xo:xo + 512], TMP0[:],
                                               nv0[:, 2 * NCH + c:2 * NCH + c + 1],
                                               X[:, xo:xo + 512],
                                               Alu.mult, Alu.add)
                        i = v.tensor_scalar(X[:, xo:xo + 512], X[:, xo:xo + 512],
                                            nv0[:, 3 * NCH + c:3 * NCH + c + 1],
                                            None, Alu.add)
                    oadd[slab] = S.n["VS"]

            a4.close()
            open_stacks.remove(a4)
            p1.close()
            open_stacks.remove(p1)

          if PHASES < 2 or P2STAGE < 1:
              raise StopEmission
          g_p1 = {k: S.n[k] for k in ("TS", "VS", "SS")}

          @blk.gpsimd
          def _(g):
              g.wait_ge(S.h["LDG"], S.n["LDG"])
              g.wait_ge(S.h["CC"], S.n["CC"])
              for tt in range(22):
                  r, sl_ = tt // 3, tt % 3
                  i = g.dma_start(out=wgq_c[tt * P:(tt + 1) * P, :],
                                  in_=wq_all[r, sl_ * P * D:(sl_ + 1) * P * D
                                             ].rearrange("(a b) -> a b", a=P))
                  S.inc(i, "LDG", 16)
                  i = g.dma_start(
                      out=wuq_c[tt * P:(tt + 1) * P, :],
                      in_=wq_all[r, (3 + sl_) * P * D:(4 + sl_) * P * D
                                 ].rearrange("(a b) -> a b", a=P))
                  S.inc(i, "LDG", 16)
              for r in range(8):
                  i = g.dma_start(
                      out=wdq_c[r * P:(r + 1) * P, :],
                      in_=wq_all[r, 6 * P * D:].rearrange("(a b) -> a b", a=P))
                  S.inc(i, "LDG", 16)
              g.wait_ge(S.h["LDG"], S.n["LDG"])
              g.wait_ge(S.h["TS"], g_p1["TS"])
              g.wait_ge(S.h["VS"], g_p1["VS"])
              g.wait_ge(S.h["SS"], g_p1["SS"])
              for c in range(NCH):
                  i = g.dma_start(
                      out=XN2[:, c * T:(c + 1) * T].rearrange(
                          "p (tc j) -> p tc j", j=P),
                      in_=xq_dram[c].rearrange("tc p j -> p tc j"))
                  S.inc(i, "LDG", 16)
          ldg_wqc = S.n["LDG"] - 16 * NCH
          xq_done_ldg = S.n["LDG"]
          xqf = XN2

          p2d = contextlib.ExitStack()
          open_stacks.append(p2d)
          WG_S = [sb(f"WG_S{i}", [P, NCH * 512], BF16, p2d) for i in range(2)]
          WU_S = [sb(f"WU_S{i}", [P, NCH * 512], BF16, p2d) for i in range(2)]
          SGb = sb("SGb", [P, 512], BF16, p2d)
          HT = sb("HT", [P, 512], F32, p2d)
          HQ = sb("HQ", [P, 512], BF16, p2d)
          pg = [psum(f"pg{i}", [P, 512], F32, p2d) for i in range(2)]
          pu = [psum(f"pu{i}", [P, 512], F32, p2d) for i in range(2)]
          qs2 = mk_qscratch(p2d, 512, "h")

          FSW = [512] * 5 + [256]
          ld_strip = {}
          gm = {}
          um = {}
          hq_mark = {}
          hst = {}
          sil = {}

          for fs in range(6):
              W = FSW[fs]
              semn = "LD" if fs % 2 == 0 else "LD2"

              @blk.sync
              def _(q):
                  q.wait_ge(S.h[semn], S.n[semn])
                  if fs < 2:
                      q.wait_ge(S.h["LDG"], ldg_wqc)
                      q.wait_ge(S.h["TS"], g_p1["TS"])
                      q.wait_ge(S.h["VS"], g_p1["VS"])
                      q.wait_ge(S.h["SS"], g_p1["SS"])
                  else:
                      q.wait_ge(S.h["TS"], um[(fs - 2, 15)])
                  for dc in range(NCH):
                      i = q.dma_start(
                          out=WG_S[fs % 2][:, dc * 512:dc * 512 + W],
                          in_=wgq_c[fs * 512:fs * 512 + W,
                                    dc * P:(dc + 1) * P],
                          transpose=True)
                      S.inc(i, semn, 16)
                      i = q.dma_start(
                          out=WU_S[fs % 2][:, dc * 512:dc * 512 + W],
                          in_=wuq_c[fs * 512:fs * 512 + W,
                                    dc * P:(dc + 1) * P],
                          transpose=True)
                      S.inc(i, semn, 16)
              ld_strip[fs] = (semn, S.n[semn])

              for tci in range(16):
                  @blk.tensor
                  def _(t):
                      if tci == 0:
                          t.wait_ge(S.h[ld_strip[fs][0]], ld_strip[fs][1])
                          if fs == 0:
                              t.wait_ge(S.h["LDG"], xq_done_ldg)
                      if (fs, tci) > (0, 1):
                          pi = fs * 16 + tci - 2
                          t.wait_ge(S.h["SS"], sil[(pi // 16, pi % 16)])
                          t.wait_ge(S.h["VS"], hq_mark[(pi // 16, pi % 16)])
                      for dc in range(NCH):
                          i = t.matmul(pg[tci % 2][:, 0:W],
                                       xqf[:, dc * T + tci * P:
                                           dc * T + (tci + 1) * P],
                                       WG_S[fs % 2][:, dc * 512:dc * 512 + W],
                                       start=(dc == 0), stop=(dc == NCH - 1))
                      S.inc(i, "TS")
                      gm[(fs, tci)] = S.n["TS"]
                      for dc in range(NCH):
                          i = t.matmul(pu[tci % 2][:, 0:W],
                                       xqf[:, dc * T + tci * P:
                                           dc * T + (tci + 1) * P],
                                       WU_S[fs % 2][:, dc * 512:dc * 512 + W],
                                       start=(dc == 0), stop=(dc == NCH - 1))
                      S.inc(i, "TS")
                      um[(fs, tci)] = S.n["TS"]

                  @blk.scalar
                  def _(s):
                      s = ASS.bind(s)
                      s.wait_ge(S.h["TS"], gm[(fs, tci)])
                      s.activation(SGb[:, 0:W], pg[tci % 2][:, 0:W], Act.Silu)
                  sil[(fs, tci)] = S.n["SS"]

                  @blk.vector
                  def _(v):
                      v = AV.bind(v)
                      v.wait_ge(S.h["SS"], sil[(fs, tci)])
                      v.wait_ge(S.h["TS"], um[(fs, tci)])
                      v.tensor_tensor(HT[:, 0:W], SGb[:, 0:W],
                                      pu[tci % 2][:, 0:W], Alu.mult)
                  ht_mark = ("VS", S.n["VS"])

                  gi = fs * 16 + tci
                  ew = ()
                  if gi >= 1:
                      pi = gi - 1
                      ew = (("LDG", hst[(pi // 16, pi % 16)]),)
                  hq_mark[(fs, tci)] = quant_tile(qs2, HT[:, 0:W], HQ[:, 0:W], W,
                                                  ht_mark, extra_wait=ew)

                  @blk.gpsimd
                  def _(g):
                      g.wait_ge(S.h["LDG"], S.n["LDG"])
                      g.wait_ge(S.h["VS"], hq_mark[(fs, tci)])
                      i = g.dma_start(
                          out=hq_dram[tci * P:(tci + 1) * P,
                                      fs * 512:fs * 512 + W],
                          in_=HQ[:, 0:W])
                      S.inc(i, "LDG", 16)
                  hst[(fs, tci)] = S.n["LDG"]

          ldg_hq = S.n["LDG"]
          p2d.close()
          open_stacks.remove(p2d)

          if TAP2 in ("hq", "wgq"):
              @blk.gpsimd
              def _(g):
                  g.wait_ge(S.h["LDG"], S.n["LDG"])
                  if TAP2 == "hq":
                      i = g.dma_start(out=dbg_out[:], in_=hq_dram[:, 0:D])
                  else:
                      i = g.dma_start(out=dbg_out[:],
                                      in_=wgq_c[0:T, 0:D])
                  S.inc(i, "LDG", 16)

          if P2STAGE < 2:
              raise StopEmission

          p2e = contextlib.ExitStack()
          open_stacks.append(p2e)
          WDT = sb("WDT", [P, 22 * D], BF16, p2e)
          HQT = [sb(f"HQT{i}", [P, 22 * 256], BF16, p2e) for i in range(2)]
          pd = [psum(f"pd{i}", [P, 256], F32, p2e) for i in range(2)]

          @blk.sync
          def _(q):
              q.wait_ge(S.h["LD"], S.n["LD"])
              q.wait_ge(S.h["LD2"], S.n["LD2"])
              q.wait_ge(S.h["TS"], S.n["TS"])
              q.wait_ge(S.h["VS"], S.n["VS"])
              q.wait_ge(S.h["SS"], S.n["SS"])
              q.wait_ge(S.h["LDG"], S.n["LDG"])
              for ffc in range(22):
                  i = q.dma_start(out=WDT[:, ffc * D:(ffc + 1) * D],
                                  in_=wdq_c[:, ffc * P:(ffc + 1) * P],
                                  transpose=True)
                  S.inc(i, "LD", 16)
          ld_wdt = S.n["LD"]

          hqt_ld = {}
          dmm = {}
          dadd = {}
          for tsh in range(8):
              semn = "LD" if tsh % 2 == 0 else "LD2"

              @blk.sync
              def _(q):
                  q.wait_ge(S.h[semn], S.n[semn])
                  if tsh == 0:
                      q.wait_ge(S.h["LDG"], ldg_hq)
                  if tsh >= 2:
                      q.wait_ge(S.h["TS"], dmm[(tsh - 2, NCH - 1)])
                  for ffc in range(22):
                      i = q.dma_start(
                          out=HQT[tsh % 2][:, ffc * 256:(ffc + 1) * 256],
                          in_=hq_dram[tsh * 256:(tsh + 1) * 256,
                                      ffc * P:(ffc + 1) * P],
                          transpose=True)
                      S.inc(i, semn, 16)
              hqt_ld[tsh] = (semn, S.n[semn])

              for dc in range(NCH):
                  @blk.tensor
                  def _(t):
                      if dc == 0:
                          t.wait_ge(S.h[hqt_ld[tsh][0]], hqt_ld[tsh][1])
                          if tsh == 0:
                              t.wait_ge(S.h["LD"], ld_wdt)
                      if (tsh, dc) > (0, 1):
                          pi = tsh * NCH + dc - 2
                          t.wait_ge(S.h["VS"], dadd[(pi // NCH, pi % NCH)])
                      for ffc in range(22):
                          i = t.matmul(pd[dc % 2][:],
                                       WDT[:, ffc * D + dc * P:
                                           ffc * D + (dc + 1) * P],
                                       HQT[tsh % 2][:,
                                                    ffc * 256:(ffc + 1) * 256],
                                       start=(ffc == 0), stop=(ffc == 21))
                      S.inc(i, "TS")
                      dmm[(tsh, dc)] = S.n["TS"]

                  @blk.vector
                  def _(v):
                      v = AV.bind(v)
                      v.wait_ge(S.h["TS"], dmm[(tsh, dc)])
                      xo = dc * T + tsh * 256
                      v.tensor_tensor(X[:, xo:xo + 256], X[:, xo:xo + 256],
                                      pd[dc % 2][:], Alu.add)
                      dadd[(tsh, dc)] = S.n["VS"]

          p2e.close()
          open_stacks.remove(p2e)


          g_p2 = {k: S.n[k] for k in ("TS", "VS", "SS")}
          p3 = contextlib.ExitStack()
          open_stacks.append(p3)
          x3b = [sb(f"x3b{i}", [P, NCH * 512], BF16, p3) for i in range(2)]
          x3q = [sb(f"x3q{i}", [P, NCH * 512], BF16, p3) for i in range(2)]
          p3s = [psum(f"p3s{i}", [1, 512], F32, p3) for i in range(2)]
          p3q = [psum(f"p3q{i}", [1, 512], F32, p3) for i in range(2)]
          p3r = [psum(f"p3r{i}", [P, 512], F32, p3) for i in range(4)]

          vc3 = []
          sq3 = []
          st3 = []
          fr3 = []
          xn1m = []

          for sl in range(4):
              @blk.vector
              def _(v):
                  v = AV.bind(v)
                  if sl == 0:
                      v.wait_ge(S.h["TS"], g_p2["TS"])
                      v.wait_ge(S.h["SS"], g_p2["SS"])
                      v.wait_ge(S.h["LDG"], S.n["LDG"])
                  if sl >= 2:
                      v.wait_ge(S.h["TS"], st3[sl - 2])
                  for c in range(NCH):
                      o = c * T + sl * 512
                      i = v.tensor_copy(x3b[sl % 2][:, c * 512:(c + 1) * 512],
                                        X[:, o:o + 512])
                  vc3.append(S.n["VS"])

              @blk.scalar
              def _(s):
                  s = ASS.bind(s)
                  if sl == 0:
                      s.wait_ge(S.h["TS"], g_p2["TS"])
                      s.wait_ge(S.h["VS"], g_p2["VS"])
                      s.wait_ge(S.h["LDG"], S.n["LDG"])
                  if sl >= 2:
                      s.wait_ge(S.h["TS"], st3[sl - 2])
                  for c in range(NCH):
                      o = c * T + sl * 512
                      i = s.activation(x3q[sl % 2][:, c * 512:(c + 1) * 512],
                                       X[:, o:o + 512], Act.Square)
                  sq3.append(S.n["SS"])

              @blk.tensor
              def _(t):
                  t.wait_ge(S.h["VS"], vc3[sl])
                  t.wait_ge(S.h["SS"], sq3[sl])
                  if sl == 0:
                      t.wait_ge(S.h["LDG"], S.n["LDG"])
                  if sl >= 2:
                      t.wait_ge(S.h["VS"], fr3[sl - 2])
                  for c in range(NCH):
                      i = t.matmul(p3s[sl % 2][:], ones_st[:],
                                   x3b[sl % 2][:, c * 512:(c + 1) * 512],
                                   start=(c == 0), stop=(c == NCH - 1))
                      i = t.matmul(p3q[sl % 2][:], ones_st[:],
                                   x3q[sl % 2][:, c * 512:(c + 1) * 512],
                                   start=(c == 0), stop=(c == NCH - 1))
                  S.inc(i, "TS")
                  st3.append(S.n["TS"])

              @blk.vector
              def _(v):
                  v = AV.bind(v)
                  v.wait_ge(S.h["TS"], st3[sl])
                  o = sl * 512
                  v.tensor_scalar(mu_r[:, o:o + 512], p3s[sl % 2][:], 1.0 / D,
                                  None, Alu.mult)
                  v.tensor_scalar(R[0][:], p3q[sl % 2][:], 1.0 / D,
                                  None, Alu.mult)
                  fr3.append(S.n["VS"])
                  v.tensor_tensor(R[1][:], mu_r[:, o:o + 512],
                                  mu_r[:, o:o + 512], Alu.mult)
                  v.tensor_tensor(R[0][:], R[0][:], R[1][:], Alu.subtract)
                  i = v.tensor_scalar(R[1][:], R[0][:], EPS0, None, Alu.add)

              row_rsqrt(R[1], r_r[:, sl * 512:sl * 512 + 512], R[2], n=512)

          ts_r3 = []
          xn1_done = []
          for sl in range(4):
              o = sl * 512
              b = (sl % 2) * 2

              @blk.tensor
              def _(t):
                  S.wait(t, "VS")
                  if sl >= 2:
                      t.wait_ge(S.h["VS"], xn1_done[sl - 2])
                  i = t.matmul(p3r[b][:], ones_rep[:], mu_r[:, o:o + 512],
                               start=True, stop=True)
                  i = t.matmul(p3r[b + 1][:], ones_rep[:], r_r[:, o:o + 512],
                               start=True, stop=True)
                  S.inc(i, "TS")
                  ts_r3.append(S.n["TS"])

              @blk.vector
              def _(v):
                  v = AV.bind(v)
                  v.wait_ge(S.h["TS"], ts_r3[sl])
                  for c in range(NCH):
                      xo = c * T + o
                      v.tensor_tensor(TMP0[:], X[:, xo:xo + 512], p3r[b][:],
                                      Alu.subtract)
                      i = v.tensor_tensor(XN2[:, xo:xo + 512], TMP0[:],
                                          p3r[b + 1][:], Alu.mult)
                  xn1_done.append(S.n["VS"])
          xn1_all = S.n["VS"]
          p3.close()
          open_stacks.remove(p3)
          g_p3 = {k: S.n[k] for k in ("TS", "VS", "SS")}

          rows_stk.close()
          if PHASES >= 4:
            p4 = contextlib.ExitStack()
            open_stacks.append(p4)
            CMB = sb("CMB", [P, 16 * 8], F32, p4)
            p4r = contextlib.ExitStack()
            open_stacks.append(p4r)
            LGF = sb("LGF", [8, T], F32, p4r)
            LGT = sb("LGT", [P, 16 * 8], F32, p4r)
            RW = sb("RW", [P, NCH * 8], BF16, p4r)
            RC1 = sb("RC1", [P, 1], F32, p4r)
            RC2 = sb("RC2", [P, 8], F32, p4r)
            RC3 = sb("RC3", [P, 8], F32, p4r)
            p4l = [psum(f"p4l{i}", [8, 512], F32, p4r) for i in range(2)]
            p4t = [psum(f"p4t{i}", [P, 8], F32, p4r) for i in range(2)]
            p4ct = [psum(f"p4ct{i}", [8, P], F32, p4r) for i in range(2)]
            RB128 = sb("RB128", [P, 8], F32, p4r)

            @blk.gpsimd
            def _(g):
                g.wait_ge(S.h["LDG"], S.n["LDG"])
                g.wait_ge(S.h["TS"], g_p3["TS"])
                g.wait_ge(S.h["VS"], g_p3["VS"])
                g.wait_ge(S.h["SS"], g_p3["SS"])
                for c in range(NCH):
                    i = g.dma_start(out=RW[:, c * 8:(c + 1) * 8],
                                    in_=rw[c * P:(c + 1) * P, :])
                    S.inc(i, "LDG", 16)
            ldg_rw = S.n["LDG"]

            rmm = []
            rev = []
            for sl in range(4):
                @blk.tensor
                def _(t):
                    if sl == 0:
                        t.wait_ge(S.h["LDG"], ldg_rw)
                    t.wait_ge(S.h["VS"], xn1_done[sl])
                    if sl >= 2:
                        t.wait_ge(S.h["VS"], rev[sl - 2])
                    for c in range(NCH):
                        i = t.matmul(p4l[sl % 2][:], RW[:, c * 8:(c + 1) * 8],
                                     XN2[:, c * T + sl * 512:
                                         c * T + (sl + 1) * 512],
                                     start=(c == 0), stop=(c == NCH - 1))
                    S.inc(i, "TS")
                    rmm.append(S.n["TS"])

                @blk.vector
                def _(v):
                    v = AV.bind(v)
                    v.wait_ge(S.h["TS"], rmm[sl])
                    i = v.tensor_copy(LGF[:, sl * 512:(sl + 1) * 512],
                                      p4l[sl % 2][:])
                    rev.append(S.n["VS"])

            @blk.tensor
            def _(t):
                S.wait(t, "VS")
                i = t.matmul(p4t[0][:], ones_rep[:], rbt[:], start=True,
                             stop=True)
                S.inc(i, "TS")
            ts_rb = S.n["TS"]

            @blk.vector
            def _(v):
                v = AV.bind(v)
                v.wait_ge(S.h["TS"], ts_rb)
                i = v.tensor_copy(RB128[:], p4t[0][:])
            vs_rb = S.n["VS"]

            ltr = []
            lev = []
            for tc in range(16):
                @blk.tensor
                def _(t):
                    if tc == 0:
                        t.wait_ge(S.h["VS"], vs_rb)
                    t.wait_ge(S.h["VS"], rev[tc // 4])
                    if tc >= 2:
                        t.wait_ge(S.h["VS"], lev[tc - 2])
                    i = t.transpose(p4t[tc % 2][:],
                                    LGF[0:8, tc * P:(tc + 1) * P],
                                    idf[0:8, 0:8])
                    S.inc(i, "TS")
                    ltr.append(S.n["TS"])

                @blk.vector
                def _(v):
                    v = AV.bind(v)
                    v.wait_ge(S.h["TS"], ltr[tc])
                    i = v.tensor_tensor(LGT[:, tc * 8:(tc + 1) * 8],
                                        p4t[tc % 2][:], RB128[:], Alu.add)
                    lev.append(S.n["VS"])

            cmb_done = []
            for tc in range(16):
                @blk.vector
                def _(v):
                    v = AV.bind(v)
                    lg = LGT[:, tc * 8:(tc + 1) * 8]
                    v.tensor_reduce(RC1[:], lg, AX.X, Alu.max)
                    i = v.tensor_scalar(RC1[:], RC1[:], -1.0, None, Alu.mult)
                vs_m = S.n["VS"]

                @blk.scalar
                def _(s):
                    s = ASS.bind(s)
                    s.wait_ge(S.h["VS"], vs_m)
                    i = s.activation(RC2[:],
                                     LGT[:, tc * 8:(tc + 1) * 8], Act.Exp,
                                     bias=RC1[:])
                ss_e = S.n["SS"]

                @blk.vector
                def _(v):
                    v = AV.bind(v)
                    v.wait_ge(S.h["SS"], ss_e)
                    cs = CMB[:, tc * 8:(tc + 1) * 8]
                    v.tensor_reduce(RC1[:], RC2[:], AX.X, Alu.add)
                    v.reciprocal(RC1[:], RC1[:])
                    v.tensor_scalar(RC2[:], RC2[:], RC1[:], None, Alu.mult)
                    v.tensor_reduce(RC1[:], RC2[:], AX.X, Alu.max)
                    v.tensor_tensor(cs, RC2[:],
                                    RC1[:].broadcast_to((P, 8)), Alu.is_ge)
                    v.scalar_tensor_tensor(RC3[:], cs, -3.0, RC2[:],
                                           Alu.mult, Alu.add)
                    v.tensor_reduce(RC1[:], RC3[:], AX.X, Alu.max)
                    v.tensor_tensor(RC3[:], RC3[:],
                                    RC1[:].broadcast_to((P, 8)), Alu.is_ge)
                    v.tensor_tensor(RC3[:], RC3[:], cs, Alu.add)
                    i = v.tensor_tensor(cs, RC2[:], RC3[:], Alu.mult)
                cmb_done.append(S.n["VS"])

            comb_all = S.n["VS"]


            p4r.close()
            open_stacks.remove(p4r)

            p4b = contextlib.ExitStack()
            open_stacks.append(p4b)
            HM = sb("HM", [P, 32 * 1024], BF16, p4b)
            UPS = [sb(f"UPS{i}", [P, NCH * P], BF16, p4b) for i in range(3)]
            DNS = [sb(f"DNS{i}", [P, 32 * P], BF16, p4b) for i in range(2)]
            REPC = [sb(f"REPC{i}", [P, 512], F32, p4b) for i in range(2)]
            TMPD = TMP0
            pup = [psum(f"pup{i}", [P, 512], F32, p4b) for i in range(4)]
            pdn = [psum(f"pdn{i}", [P, 512], F32, p4b) for i in range(2)]
            prc = psum("prc", [P, 512], F32, p4b)
            prow = psum("prow", [1, 512], F32, p4b)

            last_repc = [0]
            rp_rowv = [0]
            rpt = {}
            prev_dn = [None]
            upld = {}
            dnld = {}
            upmm = {}
            gel = {}
            dnmm = {}
            dnev = {}
            rpl = {}
            rpe = {}

            nld = [0]

            def strip_sem():
                nld[0] += 1
                return "LDG" if nld[0] % 2 == 0 else "LG2"

            for e in range(8):
              for th in range(2):
                for f in range(32):
                    semn = strip_sem()

                    @blk.gpsimd
                    def _(g):
                        g.wait_ge(S.h[semn], S.n[semn])
                        gi = (e * 2 + th) * 32 + f
                        if gi == 0:
                            g.wait_ge(S.h["TS"], g_p3["TS"])
                            g.wait_ge(S.h["SS"], g_p3["SS"])
                            g.wait_ge(S.h["VS"], comb_all)
                        if gi >= 3:
                            pi = gi - 3
                            t_ = upmm.get((pi // 64, (pi // 32) % 2,
                                           pi % 32))
                            if t_ is not None:
                                g.wait_ge(S.h["TS"], t_)
                        i = g.dma_start(
                            out=UPS[f % 3][:].rearrange(
                                "p (dc j) -> p dc j", j=P),
                            in_=upT[e, f].rearrange("(dc p) j -> p dc j",
                                                    p=P))
                        S.inc(i, semn, 16)
                    upld[(e, th, f)] = (semn, S.n[semn])

                    @blk.tensor
                    def _(t):
                        t.wait_ge(S.h[upld[(e, th, f)][0]],
                                  upld[(e, th, f)][1])
                        gi = (e * 2 + th) * 32 + f
                        if gi >= 2:
                            pi = gi - 2
                            t.wait_ge(S.h["SS"], gel[(pi // 64,
                                                      (pi // 32) % 2,
                                                      pi % 32)])
                        for hf in range(2):
                            for c in range(NCH):
                                i = t.matmul(
                                    pup[(f % 2) * 2 + hf][:],
                                    UPS[f % 3][:, c * P:(c + 1) * P],
                                    XN2[:, c * T + th * 1024 + hf * 512:
                                        c * T + th * 1024 + hf * 512 + 512],
                                    start=(c == 0), stop=(c == NCH - 1))
                            S.inc(i, "TS")
                        upmm[(e, th, f)] = S.n["TS"]

                    @blk.scalar
                    def _(s):
                        s = ASS.bind(s)
                        if f == 0 and (e, th) != (0, 0):
                            pe, pt = (e, th - 1) if th == 1 else (e - 1, 1)
                            s.wait_ge(S.h["TS"], dnmm[(pe, pt, NCH - 1, 1)])
                        s.wait_ge(S.h["TS"], upmm[(e, th, f)])
                        for hf in range(2):
                            i = s.activation(
                                HM[:, f * 1024 + hf * 512:
                                   f * 1024 + (hf + 1) * 512],
                                pup[(f % 2) * 2 + hf][:], Act.Gelu)
                        gel[(e, th, f)] = S.n["SS"]

                for hf in range(2):
                    tc0 = th * 8 + hf * 4

                    @blk.tensor
                    def _(t):
                        t.wait_ge(S.h["VS"], comb_all)
                        if (e, th, hf) != (0, 0, 0):
                            t.wait_ge(S.h["VS"], rp_rowv[0])
                        for j in range(4):
                            i = t.transpose(prow[0:1, j * P:(j + 1) * P],
                                            CMB[:, (tc0 + j) * 8 + e:
                                                (tc0 + j) * 8 + e + 1],
                                            idf[:])
                        S.inc(i, "TS")
                        rpt[(e, th, hf)] = S.n["TS"]

                    @blk.vector
                    def _(v):
                        v = AV.bind(v)
                        v.wait_ge(S.h["TS"], rpt[(e, th, hf)])
                        i = v.tensor_copy(R[3][:], prow[0:1, :])
                        rp_rowv[0] = S.n["VS"]
                    rp_row = S.n["VS"]

                    @blk.tensor
                    def _(t):
                        t.wait_ge(S.h["VS"], rp_row)
                        if (e, th, hf) != (0, 0, 0):
                            t.wait_ge(S.h["VS"], last_repc[0])
                        i = t.matmul(prc[:], ones_rep[:], R[3][:],
                                     start=True, stop=True)
                        S.inc(i, "TS")
                        rpl[(e, th, hf)] = S.n["TS"]

                    @blk.vector
                    def _(v):
                        v = AV.bind(v)
                        v.wait_ge(S.h["TS"], rpl[(e, th, hf)])
                        i = v.tensor_copy(REPC[hf][:], prc[:])
                        last_repc[0] = S.n["VS"]
                        rpe[(e, th, hf)] = S.n["VS"]

                for dc in range(NCH):
                    semn = strip_sem()

                    @blk.gpsimd
                    def _(g):
                        g.wait_ge(S.h[semn], S.n[semn])
                        if dc >= 2:
                            g.wait_ge(S.h["TS"], dnmm[(e, th, dc - 2, 1)])
                        elif (e, th) != (0, 0):
                            pe, pt = (e, 0) if th == 1 else (e - 1, 1)
                            g.wait_ge(S.h["TS"], dnmm[(pe, pt, dc + 6, 1)])
                        i = g.dma_start(
                            out=DNS[dc % 2][:].rearrange(
                                "p (fc j) -> p fc j", j=P),
                            in_=dnT[e, dc].rearrange(
                                "(fc p) j -> p fc j", p=P))
                        S.inc(i, semn, 16)
                    dnld[(e, th, dc)] = (semn, S.n[semn])

                    for hf in range(2):
                        @blk.tensor
                        def _(t):
                            t.wait_ge(S.h[dnld[(e, th, dc)][0]],
                                      dnld[(e, th, dc)][1])
                            t.wait_ge(S.h["SS"], gel[(e, th, 31)])
                            pk = (e, th, dc - 1, hf)
                            if dc >= 1 and pk in dnev:
                                t.wait_ge(S.h["VS"], dnev[pk])
                            elif dc == 0 and (e, th) != (0, 0):
                                pe, pt = (e, 0) if th == 1 else (e - 1, 1)
                                t.wait_ge(S.h["VS"],
                                          dnev[(pe, pt, NCH - 1, hf)])
                            for fc in range(32):
                                i = t.matmul(
                                    pdn[hf][:],
                                    DNS[dc % 2][:, fc * P:(fc + 1) * P],
                                    HM[:, fc * 1024 + hf * 512:
                                       fc * 1024 + (hf + 1) * 512],
                                    start=(fc == 0), stop=(fc == 31))
                            S.inc(i, "TS")
                            dnmm[(e, th, dc, hf)] = S.n["TS"]

                        @blk.vector
                        def _(v):
                            v = AV.bind(v)
                            v.wait_ge(S.h["TS"], dnmm[(e, th, dc, hf)])
                            v.wait_ge(S.h["VS"], rpe[(e, th, hf)])
                            xo = dc * T + th * 1024 + hf * 512
                            v.tensor_tensor(TMPD[:], pdn[hf][:], REPC[hf][:],
                                            Alu.mult)
                            i = v.tensor_tensor(X[:, xo:xo + 512],
                                                X[:, xo:xo + 512],
                                                TMPD[:], Alu.add)
                            dnev[(e, th, dc, hf)] = S.n["VS"]

            p4b.close()
            open_stacks.remove(p4b)
            p4.close()
            open_stacks.remove(p4)


          if PHASES >= 5:
            g_p4 = {k: S.n[k] for k in ("TS", "VS", "SS")}
            p5 = contextlib.ExitStack()
            open_stacks.append(p5)
            QG = sb("QG", [P, 4096], BF16, p5)
            WB5 = sb("WB5", [P, NCH * D], BF16, p5)
            WQ5 = sb("WQ5", [P, NCH * D], BF16, p5)
            WG5 = sb("WG5", [P, NCH * D], BF16, p5)
            SREC = sb("SREC", [P, 512], F32, p5)
            SRECB = sb("SRECB", [P, 512], BF16, p5)
            GFX = sb("GFX", [P, 128], BF16, p5)
            SGC = sb("SGC", [P, 1], F32, p5)
            GP5 = sb("GP5", [P, NCH * CH], BF16, p5)
            GD5 = sb("GD5", [P, NCH * CH], BF16, p5)
            pq5 = [psum(f"pq5{i}", [P, 512], F32, p5) for i in range(2)]
            p5a = contextlib.ExitStack()
            open_stacks.append(p5a)
            KG = sb("KG", [P, 4096], BF16, p5a)
            KT = sb("KT", [P, 4096], BF16, p5a)
            VT5 = sb("VT5", [P, 4096], BF16, p5a)
            M3 = sb("M3", [P, NH1 * CH], BF16, p5a)
            MSK2 = [sb(f"MSK{i}", [P, CH], BF16, p5a) for i in range(2)]
            RB52 = [sb(f"RB5{i}", [64, CH], F32, p5a) for i in range(2)]
            SMAT = sb("SMAT", [P, 512], F32, p5a)
            SBF = sb("SBF", [P, 512], BF16, p5a)
            psc2 = [psum(f"psc{i}", [P, 2 * CH], F32, p5a) for i in range(2)]
            PB = [psum(f"pb5{i}", [64, CH + 64], F32, p5a)
                  for i in range(2)]
            pkt2 = [psum(f"pkt{i}", [P, 64], BF16, p5a) for i in range(2)]

            @blk.gpsimd
            def _(q):
                q.wait_ge(S.h["LDG"], S.n["LDG"])
                q.wait_ge(S.h["TS"], g_p4["TS"])
                q.wait_ge(S.h["VS"], g_p4["VS"])
                q.wait_ge(S.h["SS"], g_p4["SS"])
                i = q.dma_start(out=M3[:].rearrange("p (h i) -> p h i", i=CH),
                                in_=mask3[:].rearrange("h j i -> j h i"))
                S.inc(i, "LDG", 16)
                i = q.dma_start(out=SGC[:], in_=sgate[:])
                S.inc(i, "LDG", 16)
                i = q.dma_start(out=GP5[:].rearrange("p (c i) -> p c i", i=CH),
                                in_=gpos[:].rearrange("(c p) i -> p c i", p=P))
                S.inc(i, "LDG", 16)
                i = q.dma_start(out=GD5[:].rearrange("p (c i) -> p c i", i=CH),
                                in_=gdecC[:].rearrange("(c p) i -> p c i", p=P))
                S.inc(i, "LDG", 16)
                i = q.dma_start(out=GFX[:], in_=gfix[:])
                S.inc(i, "LDG", 16)
            ld_m3 = S.n["LDG"]

            @blk.vector
            def _(v):
                v = AV.bind(v)
                v.wait_ge(S.h["TS"], g_p4["TS"])
                v.wait_ge(S.h["SS"], g_p4["SS"])
                v.wait_ge(S.h["LDG"], S.n["LDG"])
                i = v.memset(SMAT[:], 0.0)
                i = v.memset(SBF[:], 0.0)
            vs_s0 = S.n["VS"]

            gs5_b = S.n["GS"]
            rec_list = []
            fix_list = []
            retst = {}
            srbv = {}
            fixt = {}
            rsq5 = {}
            swv = {}
            oproj5 = {}
            pq5_hist = []
            qev5 = {}
            kev5 = {}
            vev5 = {}
            ktv = {}
            mskv = {}
            rbv = {}
            retv = {}
            stv = {}
            sbv = {}
            ldw5 = {}
            pmm5 = {}
            ktr = {}
            scv = {}
            suv = {}
            riv = {}
            rjv = {}

            for g5 in range(4):
                gt0 = g5 * 512
                for wi, (wt, dst, pat) in enumerate(
                        ((wq1, QG, GP5), (wk1, KG, GD5), (wv1, VT5, None))):
                    semn = "LDG" if (g5 * 3 + wi) % 2 == 0 else "LG2"

                    @blk.gpsimd
                    def _(q):
                        q.wait_ge(S.h[semn], S.n[semn])
                        if g5 == 0 and wi == 0:
                            q.wait_ge(S.h["LDG"], ld_m3)
                            q.wait_ge(S.h["TS"], g_p4["TS"])
                            q.wait_ge(S.h["VS"], g_p4["VS"])
                            q.wait_ge(S.h["SS"], g_p4["SS"])
                        if (g5, wi) != (0, 0):
                            t_ = pmm5[(g5 * 3 + wi) - 1]
                            q.wait_ge(S.h["TS"], t_)
                        for c in range(NCH):
                            i = q.dma_start(out=WB5[:, c * D:(c + 1) * D],
                                            in_=wt[c * P:(c + 1) * P, :])
                            S.inc(i, semn, 16)
                    ldw5[(g5, wi)] = (semn, S.n[semn])

                    if pat is not None:
                        for oc in range(NCH):
                            @blk.tensor
                            def _(t):
                                if oc == 0:
                                    t.wait_ge(S.h[ldw5[(g5, wi)][0]],
                                              ldw5[(g5, wi)][1])
                                    if g5 == 0 and wi == 0:
                                        t.wait_ge(S.h["VS"], vs_s0)
                                    if g5 >= 1 and wi == 0:
                                        t.wait_ge(S.h["VS"], retv[
                                            (g5 - 1, 3, NH1 - 1)])
                                        t.wait_ge(S.h["VS"], stv[
                                            (g5 - 1, 3, NH1 - 1)])
                                        t.wait_ge(S.h["LDG"], retst[
                                            (g5 - 1, 3, NH1 - 1)])
                                if len(pq5_hist) >= 2:
                                    t.wait_ge(S.h[pq5_hist[-2][0]],
                                              pq5_hist[-2][1])
                                for c in range(NCH):
                                    i = t.matmul(
                                        pq5[oc % 2][:],
                                        WB5[:, c * D + oc * P:
                                            c * D + (oc + 1) * P],
                                        XN2[:, c * T + gt0:c * T + gt0 + 512],
                                        start=(c == 0), stop=(c == NCH - 1))
                                S.inc(i, "TS")
                                pmm5[g5 * 3 + wi] = S.n["TS"]
                                pmm5[(g5, wi, oc)] = S.n["TS"]

                            @blk.vector
                            def _(v):
                                v = AV.bind(v)
                                v.wait_ge(S.h["TS"], pmm5[(g5, wi, oc)])
                                pp = pat[:, oc * CH:(oc + 1) * CH]
                                bc = pp.unsqueeze(1).broadcast_to((P, 4, CH))
                                ps3 = pq5[oc % 2][:].rearrange(
                                    "p (c i) -> p c i", i=CH)
                                d3 = dst[:, oc * 512:(oc + 1) * 512].rearrange(
                                    "p (c i) -> p c i", i=CH)
                                i = v.tensor_tensor(d3, ps3, bc, Alu.mult)
                                qev5[(g5, wi, oc)] = S.n["VS"]
                                pq5_hist.append(("VS", S.n["VS"]))
                    else:
                        for tc4 in range(4):
                            for sl2 in range(2):
                                @blk.tensor
                                def _(t):
                                    if tc4 == 0 and sl2 == 0:
                                        t.wait_ge(S.h[ldw5[(g5, wi)][0]],
                                                  ldw5[(g5, wi)][1])
                                    if len(pq5_hist) >= 2:
                                        t.wait_ge(S.h[pq5_hist[-2][0]],
                                                  pq5_hist[-2][1])
                                    for c in range(NCH):
                                        i = t.matmul(
                                            pq5[(tc4 * 2 + sl2) % 2][:],
                                            XN2[:, c * T + gt0 + tc4 * P:
                                                c * T + gt0 + (tc4 + 1) * P],
                                            WB5[:, c * D + sl2 * 512:
                                                c * D + (sl2 + 1) * 512],
                                            start=(c == 0),
                                            stop=(c == NCH - 1))
                                    S.inc(i, "TS")
                                    pmm5[g5 * 3 + wi] = S.n["TS"]
                                    pmm5[(g5, wi, tc4, sl2)] = S.n["TS"]

                                @blk.vector
                                def _(v):
                                    v = AV.bind(v)
                                    v.wait_ge(S.h["TS"],
                                              pmm5[(g5, wi, tc4, sl2)])
                                    i = v.tensor_copy(
                                        VT5[:, tc4 * D + sl2 * 512:
                                            tc4 * D + (sl2 + 1) * 512],
                                        pq5[(tc4 * 2 + sl2) % 2][:])
                                    vev5[(g5, tc4, sl2)] = S.n["VS"]
                                    pq5_hist.append(("VS", S.n["VS"]))

                if g5 == 3:
                    @blk.gpsimd
                    def _(g):
                        g.wait_ge(S.h["LG2"], S.n["LG2"])
                        for wt, dst in ((wq1, WQ5), (wg1, WG5)):
                            for c in range(NCH):
                                i = g.dma_start(
                                    out=dst[:, c * D:(c + 1) * D],
                                    in_=wt[c * P:(c + 1) * P, :])
                                S.inc(i, "LG2", 16)
                    ldg_wqg = S.n["LG2"]

                for cl in range(4):
                    for h in range(NH1):
                        idx5 = cl * NH1 + h

                        @blk.tensor
                        def _(t):
                            t.wait_ge(S.h["VS"], qev5[(g5, 1, h // 2)])
                            if g5 * 64 + idx5 >= 2:
                                t.wait_ge(S.h["GS"],
                                          gs5_b + g5 * 64 + idx5 - 1)
                            b5 = 64 * (h % 2)
                            i = t.transpose(
                                pkt2[idx5 % 2][:, 0:64],
                                KG[b5:b5 + 64,
                                   (h // 2) * 512 + cl * CH:
                                   (h // 2) * 512 + (cl + 1) * CH],
                                idb[b5:b5 + 64, b5:b5 + 64])
                            S.inc(i, "TS")
                            ktr[(g5, cl, h)] = S.n["TS"]

                        @blk.vector
                        def _(v):
                            v = AV.bind(v)
                            v.wait_ge(S.h["TS"], ktr[(g5, cl, h)])
                            i = v.tensor_copy(
                                KT[:, cl * D + h * 64:cl * D + (h + 1) * 64],
                                pkt2[idx5 % 2][:, 0:64], inc="GS")

                def emit_psc(key, ridx):
                    g5_, cl, h = key
                    b5 = 64 * (h % 2)
                    hc = (h // 2) * 512 + cl * CH
                    pcb = psc2[ridx % 2]
                    msb = MSK2[ridx % 2]

                    @blk.tensor
                    def _(t):
                        t.wait_ge(S.h["VS"], qev5[(g5_, 0, h // 2)])
                        t.wait_ge(S.h["VS"], qev5[(g5_, 1, h // 2)])
                        if ridx == 0:
                            t.wait_ge(S.h["VS"], vs_s0)
                        if ridx >= 2:
                            t.wait_ge(S.h["VS"],
                                      mskv[rec_list[ridx - 2]])
                        i = t.matmul(pcb[:, 0:CH],
                                     KG[b5:b5 + 64, hc:hc + CH],
                                     QG[b5:b5 + 64, hc:hc + CH],
                                     start=True, stop=True)
                        S.inc(i, "TS")
                        scv[key] = S.n["TS"]

                    @blk.vector
                    def _(v):
                        v = AV.bind(v)
                        v.wait_ge(S.h["TS"], scv[key])
                        i = v.tensor_tensor(msb[:], pcb[:, 0:CH],
                                            M3[:, h * CH:(h + 1) * CH],
                                            Alu.mult)
                        mskv[key] = S.n["VS"]
                    rec_list.append(key)

                def emit_rec(key, ridx):
                    g5_, cl, h = key
                    b5 = 64 * (h % 2)
                    hc = (h // 2) * 512 + cl * CH
                    pcb = psc2[ridx % 2]
                    pbb = PB[ridx % 2]
                    rbb = RB52[ridx % 2]
                    msb = MSK2[ridx % 2]
                    prev_state = ((g5_, cl - 1, h) if cl > 0 else
                                  ((g5_ - 1, 3, h) if g5_ > 0 else None))

                    @blk.tensor
                    def _(t):
                        t.wait_ge(S.h["VS"], mskv[key])
                        t.wait_ge(S.h["VS"], vev5[(g5_, cl, 0)])
                        t.wait_ge(S.h["VS"], vev5[(g5_, cl, 1)])
                        if ridx >= 2:
                            t.wait_ge(S.h["VS"], stv[rec_list[ridx - 2]])
                        if prev_state is not None:
                            t.wait_ge(S.h["VS"], stv[prev_state])
                        i = t.matmul(pbb[0:64, 0:CH],
                                     VT5[:, cl * D + h * 64:
                                         cl * D + (h + 1) * 64],
                                     msb[:], start=True, stop=True)
                        S.inc(i, "TS")
                        i = t.matmul(pbb[0:64, CH:CH + 64],
                                     KT[:, cl * D + h * 64:
                                        cl * D + (h + 1) * 64],
                                     VT5[:, cl * D + h * 64:
                                         cl * D + (h + 1) * 64],
                                     start=True, stop=True)
                        S.inc(i, "TS")
                        i = t.matmul(pcb[0:64, CH:2 * CH],
                                     SBF[b5:b5 + 64,
                                         (h // 2) * 64:(h // 2) * 64 + 64],
                                     QG[b5:b5 + 64, hc:hc + CH],
                                     start=True, stop=True)
                        S.inc(i, "TS")
                        rjv[key] = S.n["TS"]

                    @blk.vector
                    def _(v):
                        v = AV.bind(v)
                        if ridx >= 2:
                            v.wait_ge(S.h["LDG"],
                                      retst[rec_list[ridx - 2]])
                        v.wait_ge(S.h["TS"], rjv[key])
                        i = v.tensor_copy(rbb[:], pbb[0:64, 0:CH])
                        i = v.tensor_tensor(rbb[:], rbb[:],
                                            pcb[0:64, CH:2 * CH], Alu.add)
                        retv[key] = S.n["VS"]
                        gC = float(GAMMAS[h] ** CH)
                        v.scalar_tensor_tensor(
                            SMAT[b5:b5 + 64,
                                 (h // 2) * 64:(h // 2) * 64 + 64],
                            SMAT[b5:b5 + 64,
                                 (h // 2) * 64:(h // 2) * 64 + 64],
                            gC, pbb[0:64, CH:CH + 64],
                            Alu.mult, Alu.add)
                        i = v.tensor_copy(
                            SBF[b5:b5 + 64,
                                (h // 2) * 64:(h // 2) * 64 + 64],
                            SMAT[b5:b5 + 64,
                                 (h // 2) * 64:(h // 2) * 64 + 64])
                        stv[key] = S.n["VS"]

                    @blk.gpsimd
                    def _(g):
                        g.wait_ge(S.h["LDG"], S.n["LDG"])
                        g.wait_ge(S.h["VS"], retv[key])
                        i = g.dma_start(
                            out=ret_dram[64 * h:64 * (h + 1),
                                         gt0 + cl * CH:
                                         gt0 + (cl + 1) * CH],
                            in_=rbb[:])
                        S.inc(i, "LDG", 16)
                    retst[key] = S.n["LDG"]

                pending = None
                for cl in range(4):
                    for h in range(NH1):
                        key = (g5, cl, h)
                        emit_psc(key, len(rec_list))
                        if pending is not None:
                            emit_rec(*pending)
                        pending = (key, len(rec_list) - 1)
                emit_rec(*pending)

            ldg_ret = S.n["LDG"]

            @blk.gpsimd
            def _(g):
                g.wait_ge(S.h["LG2"], S.n["LG2"])
                g.wait_ge(S.h["TS"], pmm5[11])
                for c in range(NCH):
                    i = g.dma_start(out=WB5[:, c * D:(c + 1) * D],
                                    in_=wo1[c * P:(c + 1) * P, :])
                    S.inc(i, "LG2", 16)
            ldg_wb = S.n["LG2"]

            st3o = st_contrib[:].rearrange("(h p j) -> h p j", p=64, j=64)
            sta3 = st_all[0].rearrange("(h p j) -> h p j", p=64, j=64)

            @blk.gpsimd
            def _(g):
                g.wait_ge(S.h["LDG"], S.n["LDG"])
                g.wait_ge(S.h["VS"], stv[(3, 3, NH1 - 1)])
                i = g.dma_start(
                    out=st3o[0::2].transpose([1, 0, 2]),
                    in_=SMAT[0:64, :].rearrange("p (q j) -> p q j", j=64))
                S.inc(i, "LDG", 16)
                i = g.dma_start(
                    out=st3o[1::2].transpose([1, 0, 2]),
                    in_=SMAT[64:128, :].rearrange("p (q j) -> p q j", j=64))
                S.inc(i, "LDG", 16)
                g.wait_ge(S.h["LDG"], S.n["LDG"])
                i = g.collective_compute(
                    "AllGather", Alu.bypass,
                    replica_groups=[[0, 1], [2, 3], [4, 5], [6, 7]],
                    ins=[st_contrib[:]], outs=[st_all[:]])
                S.inc(i, "CC", 1)
                g.wait_ge(S.h["CC"], S.n["CC"])
                i = g.dma_start(
                    out=SREC[0:64, :].rearrange("p (q j) -> p q j", j=64),
                    in_=sta3[0::2].transpose([1, 0, 2]))
                S.inc(i, "LDG", 16)
                i = g.dma_start(
                    out=SREC[64:128, :].rearrange("p (q j) -> p q j", j=64),
                    in_=sta3[1::2].transpose([1, 0, 2]))
                S.inc(i, "LDG", 16)
            ldg_srec = S.n["LDG"]

            @blk.vector
            def _(v):
                v = AV.bind(v)
                v.wait_ge(S.h["LDG"], ldg_srec)
                i = v.tensor_scalar(SREC[:], SREC[:], SGC[:, 0:1], None,
                                    Alu.mult)
                i = v.tensor_copy(SRECB[:], SREC[:])
            vs_srec = S.n["VS"]

            g_p5a = {k: S.n[k] for k in ("TS", "VS", "SS")}
            p5a.close()
            open_stacks.remove(p5a)

            p5b = contextlib.ExitStack()
            open_stacks.append(p5b)
            RSL = sb("RSL", [P, NCH * 512], BF16, p5b)
            RSQ = sb("RSQ", [P, 2 * 512], BF16, p5b)
            SW = sb("SW", [P, NCH * 512], BF16, p5b)
            LNG = sb("LNG", [P, NCH * 512], BF16, p5b)
            QGS = sb("QGS", [P, NCH * CH], BF16, p5b)
            pst5 = [psum(f"pst5{i}", [1, 512], F32, p5b) for i in range(2)]
            prep5 = [psum(f"prep5{i}", [P, 512], F32, p5b) for i in range(2)]
            pr22 = [psum(f"pr2{i}", [64, 4 * CH], F32, p5b) for i in range(2)]

            fixw = {}
            fixa = {}
            qgsv = {}
            st5d = {}
            lng5d = {}
            rsqc = {}
            stc = {}
            for ts5 in range(4):
                gt0 = ts5 * 512

                semn = "LD" if ts5 % 2 == 0 else "LD2"

                @blk.sync
                def _(q):
                    q.wait_ge(S.h[semn], S.n[semn])
                    if ts5 == 0:
                        q.wait_ge(S.h["TS"], g_p5a["TS"])
                        q.wait_ge(S.h["VS"], g_p5a["VS"])
                        q.wait_ge(S.h["SS"], g_p5a["SS"])
                    q.wait_ge(S.h["LDG"], retst[(ts5, 3, NH1 - 1)])
                    if ts5 >= 1:
                        q.wait_ge(S.h["VS"], lng5d[ts5 - 1])
                        q.wait_ge(S.h["TS"], st5d[ts5 - 1])
                        q.wait_ge(S.h["SS"], rsq5[ts5 - 1])
                    for c in range(NCH):
                        i = q.dma_start(
                            out=RSL[:, c * 512:(c + 1) * 512],
                            in_=ret_dram[c * P:(c + 1) * P, gt0:gt0 + 512])
                        S.inc(i, semn, 16)
                rsl_ld = (semn, S.n[semn])

                for oc in range(NCH):
                    @blk.tensor
                    def _(t):
                        if ts5 == 0 and oc == 0:
                            t.wait_ge(S.h["LG2"], ldg_wqg)
                        if len(pq5_hist) >= 2:
                            t.wait_ge(S.h[pq5_hist[-2][0]],
                                      pq5_hist[-2][1])
                        for c in range(NCH):
                            i = t.matmul(
                                pq5[oc % 2][:],
                                WQ5[:, c * D + oc * P:c * D + (oc + 1) * P],
                                XN2[:, c * T + gt0:c * T + gt0 + 512],
                                start=(c == 0), stop=(c == NCH - 1))
                        S.inc(i, "TS")
                        pmm5[(9, ts5, oc)] = S.n["TS"]

                    @blk.vector
                    def _(v):
                        v = AV.bind(v)
                        v.wait_ge(S.h["TS"], pmm5[(9, ts5, oc)])
                        pp = GP5[:, oc * CH:(oc + 1) * CH]
                        bc = pp.unsqueeze(1).broadcast_to((P, 4, CH))
                        ps3 = pq5[oc % 2][:].rearrange("p (c i) -> p c i",
                                                       i=CH)
                        d3 = QG[:, oc * 512:(oc + 1) * 512].rearrange(
                            "p (c i) -> p c i", i=CH)
                        i = v.tensor_tensor(d3, ps3, bc, Alu.mult)
                        pq5_hist.append(("VS", S.n["VS"]))
                qg_done = S.n["VS"]

                for cl in range(4):
                    gc = ts5 * 4 + cl
                    fi = ts5 * 4 + cl

                    @blk.vector
                    def _(v):
                        v = AV.bind(v)
                        if fi == 0:
                            v.wait_ge(S.h["TS"], g_p5a["TS"])
                            v.wait_ge(S.h["LDG"], ld_m3)
                        if len(fix_list) >= 1:
                            v.wait_ge(S.h["TS"], fixw[fix_list[-1]])
                        qg3 = QG[:].rearrange("p (oc c j) -> p oc c j",
                                              c=4, j=CH)[:, :, cl, :]
                        gf3 = GFX[:, gc * 8:(gc + 1) * 8].unsqueeze(
                            2).broadcast_to((P, NCH, CH))
                        i = v.tensor_tensor(
                            QGS[:].rearrange("p (oc j) -> p oc j", j=CH),
                            qg3, gf3, Alu.mult)
                        qgsv[fi] = S.n["VS"]

                    for w in range(2):
                        wkey = (fi, w)

                        @blk.tensor
                        def _(t):
                            t.wait_ge(S.h["VS"], qgsv[fi])
                            if fi == 0 and w == 0:
                                t.wait_ge(S.h["VS"], qg_done)
                            if len(fix_list) >= 1:
                                t.wait_ge(S.h["VS"],
                                          fixa[fix_list[-1]])
                            for h in range(8 * w, 8 * w + 8):
                                b5 = 64 * (h % 2)
                                i = t.matmul(
                                    pr22[h % 2][0:64,
                                                ((h // 2) % 4) * CH:
                                                ((h // 2) % 4 + 1) * CH],
                                    SRECB[b5:b5 + 64,
                                          (h // 2) * 64:(h // 2) * 64 + 64],
                                    QGS[b5:b5 + 64,
                                        (h // 2) * CH:(h // 2 + 1) * CH],
                                    start=True, stop=True)
                                S.inc(i, "TS")
                            fixw[wkey] = S.n["TS"]

                        @blk.vector
                        def _(v):
                            v = AV.bind(v)
                            v.wait_ge(S.h["TS"], fixw[wkey])
                            if fi == 0 and w == 0:
                                v.wait_ge(S.h[rsl_ld[0]], rsl_ld[1])
                            for par in range(2):
                                rsl3 = RSL[64 * par:64 * par + 64,
                                           :].rearrange(
                                    "p (m c j) -> p m c j", c=4,
                                    j=CH)[:, 4 * w:4 * w + 4, cl, :]
                                pr3 = pr22[par][0:64, :].rearrange(
                                    "p (k j) -> p k j", j=CH)
                                i = v.tensor_tensor(rsl3, rsl3, pr3,
                                                    Alu.add)
                            fixa[wkey] = S.n["VS"]
                        fix_list.append(wkey)
                fix_all = S.n["VS"]

                for c in range(NCH):
                    gi = ts5 * NCH + c

                    @blk.scalar
                    def _(s):
                        s = ASS.bind(s)
                        if c == 0:
                            s.wait_ge(S.h["VS"], fix_all)
                            if ts5 == 0:
                                s.wait_ge(S.h["TS"], g_p5a["TS"])
                        if gi >= 2:
                            s.wait_ge(S.h["TS"], stc[gi - 2])
                        i = s.activation(RSQ[:, (c % 2) * 512:
                                              (c % 2) * 512 + 512],
                                         RSL[:, c * 512:(c + 1) * 512],
                                         Act.Square)
                        rsqc[gi] = S.n["SS"]

                    @blk.tensor
                    def _(t):
                        if c == 0:
                            t.wait_ge(S.h["VS"], fix_all)
                        t.wait_ge(S.h["SS"], rsqc[gi])
                        i = t.matmul(pst5[0][:], ones_st[:],
                                     RSL[:, c * 512:(c + 1) * 512],
                                     start=(c == 0), stop=(c == NCH - 1))
                        i = t.matmul(pst5[1][:], ones_st[:],
                                     RSQ[:, (c % 2) * 512:(c % 2) * 512 + 512],
                                     start=(c == 0), stop=(c == NCH - 1))
                        S.inc(i, "TS")
                        stc[gi] = S.n["TS"]
                rsq5[ts5] = S.n["SS"]
                st5d[ts5] = S.n["TS"]

                @blk.vector
                def _(v):
                    v = AV.bind(v)
                    v.wait_ge(S.h["TS"], st5d[ts5])
                    v.tensor_scalar(R[4][:], pst5[0][:], 1.0 / D, None,
                                    Alu.mult)
                    v.tensor_scalar(R[0][:], pst5[1][:], 1.0 / D, None,
                                    Alu.mult)
                    v.tensor_tensor(R[1][:], R[4][:], R[4][:], Alu.mult)
                    v.tensor_tensor(R[0][:], R[0][:], R[1][:], Alu.subtract)
                    i = v.tensor_scalar(R[1][:], R[0][:], EPS1, None, Alu.add)

                row_rsqrt(R[1], R[5], R[2], n=512)

                for oc in range(NCH):
                    @blk.tensor
                    def _(t):
                        if len(pq5_hist) >= 2:
                            t.wait_ge(S.h[pq5_hist[-2][0]],
                                      pq5_hist[-2][1])
                        for c in range(NCH):
                            i = t.matmul(
                                pq5[oc % 2][:],
                                WG5[:, c * D + oc * P:c * D + (oc + 1) * P],
                                XN2[:, c * T + gt0:c * T + gt0 + 512],
                                start=(c == 0), stop=(c == NCH - 1))
                        S.inc(i, "TS")
                        pmm5[(8, ts5, oc)] = S.n["TS"]

                    @blk.scalar
                    def _(s):
                        s = ASS.bind(s)
                        s.wait_ge(S.h["TS"], pmm5[(8, ts5, oc)])
                        if oc == 0 and ts5 >= 1:
                            s.wait_ge(S.h["VS"], lng5d[ts5 - 1])
                        i = s.activation(SW[:, oc * 512:(oc + 1) * 512],
                                         pq5[oc % 2][:], Act.Silu,
                                         bias=nv1[:, oc:oc + 1])
                        swv[(ts5, oc)] = S.n["SS"]
                        pq5_hist.append(("SS", S.n["SS"]))

                @blk.tensor
                def _(t):
                    S.wait(t, "VS")
                    i = t.matmul(prep5[0][:], ones_rep[:], R[4][:],
                                 start=True, stop=True)
                    i = t.matmul(prep5[1][:], ones_rep[:], R[5][:],
                                 start=True, stop=True)
                    S.inc(i, "TS")
                rep5 = S.n["TS"]

                @blk.vector
                def _(v):
                    v = AV.bind(v)
                    v.wait_ge(S.h["TS"], rep5)
                    v.wait_ge(S.h["SS"], swv[(ts5, NCH - 1)])
                    if ts5 >= 1:
                        v.wait_ge(S.h["TS"],
                                  oproj5[(ts5 - 1, NCH - 1)])
                    for c in range(NCH):
                        co = c * 512
                        v.tensor_tensor(TMP0[:],
                                        RSL[:, co:co + 512],
                                        prep5[0][:], Alu.subtract)
                        v.tensor_tensor(TMP0[:], TMP0[:], prep5[1][:],
                                        Alu.mult)
                        v.scalar_tensor_tensor(
                            LNG[:, co:co + 512], TMP0[:],
                            nv1[:, 2 * NCH + c:2 * NCH + c + 1],
                            nv1[:, 3 * NCH + c:3 * NCH + c + 1
                                ].broadcast_to((P, 512)),
                            Alu.mult, Alu.add)
                        i = v.tensor_tensor(LNG[:, co:co + 512],
                                            LNG[:, co:co + 512],
                                            SW[:, co:co + 512], Alu.mult)
                lng5d[ts5] = S.n["VS"]

                for oc in range(NCH):
                    @blk.tensor
                    def _(t):
                        if oc == 0:
                            if ts5 == 0:
                                t.wait_ge(S.h["LG2"], ldg_wb)
                            t.wait_ge(S.h["VS"], lng5d[ts5])
                        if len(pq5_hist) >= 2:
                            t.wait_ge(S.h[pq5_hist[-2][0]],
                                      pq5_hist[-2][1])
                        for c in range(NCH):
                            i = t.matmul(
                                pq5[oc % 2][:],
                                WB5[:, c * D + oc * P:c * D + (oc + 1) * P],
                                LNG[:, c * 512:(c + 1) * 512],
                                start=(c == 0), stop=(c == NCH - 1))
                        S.inc(i, "TS")
                        oproj5[(ts5, oc)] = S.n["TS"]

                    @blk.vector
                    def _(v):
                        v = AV.bind(v)
                        v.wait_ge(S.h["TS"], oproj5[(ts5, oc)])
                        xo = oc * T + gt0
                        v.tensor_tensor(X[:, xo:xo + 512], X[:, xo:xo + 512],
                                        pq5[oc % 2][:], Alu.add)
                        i = v.tensor_scalar(X[:, xo:xo + 512],
                                            X[:, xo:xo + 512],
                                            nv1[:, NCH + oc:NCH + oc + 1],
                                            None, Alu.add)
                        pq5_hist.append(("VS", S.n["VS"]))

            p5b.close()
            open_stacks.remove(p5b)
            p5.close()
            open_stacks.remove(p5)
        except StopEmission:
            for stk in reversed(open_stacks):
                stk.close()


        if TAP in ("xn2", "xq"):
            tap_marks = []
            for c in range(NCH):
                for sl4 in range(4):
                    @blk.vector
                    def _(v):
                        v = AV.bind(v)
                        S.wait(v, "TS")
                        S.wait(v, "SS")
                        if tap_marks:
                            v.wait_ge(S.h["LDG"], tap_marks[-1])
                        v.tensor_copy(
                            TMP0[:],
                            XN2[:, c * T + sl4 * 512:c * T + (sl4 + 1) * 512])
                    m = S.n["VS"]

                    @blk.gpsimd
                    def _(g):
                        g.wait_ge(S.h["LDG"], S.n["LDG"])
                        g.wait_ge(S.h["VS"], m)
                        i = g.dma_start(
                            out=y_out[c * P:(c + 1) * P,
                                      sl4 * 512:(sl4 + 1) * 512],
                            in_=TMP0[:])
                        S.inc(i, "LDG", 16)
                    tap_marks.append(S.n["LDG"])

            @blk.gpsimd
            def _(g):
                g.wait_ge(S.h["LDG"], S.n["LDG"])
        else:
            @blk.gpsimd
            def _(g):
                g.wait_ge(S.h["LDG"], S.n["LDG"])
                S.wait(g, "LD", ld_x)
                S.wait(g, "VS")
                S.wait(g, "TS")
                S.wait(g, "SS")
                for c in range(NCH):
                    i = g.dma_start(out=y_out[c * P:(c + 1) * P, :],
                                    in_=X[:, c * T:(c + 1) * T])
                    S.inc(i, "LDG", 16)
                g.wait_ge(S.h["LDG"], S.n["LDG"])

    return nc


def _prep_inputs(inputs):
    f32 = np.float32
    x = np.ascontiguousarray(inputs["L0_attn_x"], dtype=f32)

    def col8(vname):
        return np.ascontiguousarray(
            inputs[vname].astype(f32).reshape(NCH, P).T)

    nvec0 = np.concatenate([col8("L0_attn_norm_w"), col8("L0_attn_norm_b"),
                            col8("L0_attn_out_norm_w"),
                            col8("L0_attn_out_norm_b")], axis=1)
    nvec1 = np.concatenate([col8("L1_attn_bg"), col8("L1_attn_bo"),
                            col8("L1_attn_ln_gamma"),
                            col8("L1_attn_ln_beta")], axis=1)

    wg = inputs["L0_ffn_w_gate"].astype(f32)
    wu = inputs["L0_ffn_w_up"].astype(f32)
    wd = inputs["L0_ffn_w_down"].astype(f32)
    wg_p = np.zeros((8 * 384, D), f32)
    wg_p[:FF0] = wg
    wu_p = np.zeros((8 * 384, D), f32)
    wu_p[:FF0] = wu

    up = inputs["L1_ffn_expert_up"].astype(f32)
    dn = inputs["L1_ffn_expert_down"].astype(f32)
    upT = np.ascontiguousarray(
        up.transpose(0, 2, 1).reshape(8, D, 32, P).transpose(0, 2, 1, 3))
    dnT = np.ascontiguousarray(
        dn.transpose(0, 2, 1).reshape(8, FFE, NCH, P).transpose(0, 2, 1, 3))

    identb = np.eye(P, dtype=f32).astype(BF)
    identf = np.eye(P, dtype=f32)

    gam = np.array(GAMMAS, dtype=np.float64)
    i_idx = np.arange(CH, dtype=np.float64)
    gpos = np.zeros((D, CH), f32)
    gdecC = np.zeros((D, CH), f32)
    for h in range(NH1):
        gpos[h * DH:(h + 1) * DH, :] = (gam[h] ** i_idx / 8.0).astype(f32)
        gdecC[h * DH:(h + 1) * DH, :] = (gam[h] ** (CH - i_idx)).astype(f32)
    ii = i_idx[None, :]
    jj = i_idx[:, None]
    mask3 = np.zeros((NH1, CH, CH), f32)
    for h in range(NH1):
        mask3[h] = ((ii >= jj) * gam[h] ** (-float(CH))).astype(f32)
    gchunk = np.zeros((NH1, 16), f32)
    for h in range(NH1):
        gchunk[h] = (gam[h] ** (CH * np.arange(16, dtype=np.float64))).astype(f32)
    gfix_h = np.zeros((P, 16, 8), f32)
    for p in range(P):
        for oc in range(8):
            hh = 2 * oc + p // 64
            gfix_h[p, :, oc] = (gam[hh] ** (CH * np.arange(16, dtype=np.float64))).astype(f32)
    gfix_h = np.ascontiguousarray(gfix_h.reshape(P, 128))

    shared = {
        "wq0": inputs["L0_attn_Wq"].astype(f32),
        "wkv0": inputs["L0_attn_Wkv"].astype(f32),
        "wo0": inputs["L0_attn_Wo"].astype(f32),
        "nvec0": nvec0, "nvec1": nvec1,
        "wq1": inputs["L1_attn_Wq"].astype(f32),
        "wk1": inputs["L1_attn_Wk"].astype(f32),
        "wv1": inputs["L1_attn_Wv"].astype(f32),
        "wg1": inputs["L1_attn_Wg"].astype(f32),
        "wo1": inputs["L1_attn_Wo"].astype(f32),
        "rw": inputs["L1_ffn_router_w"].astype(f32),
        "rbias": inputs["L1_ffn_router_b"].astype(f32).reshape(1, 8),
        "upT": upT, "dnT": dnT,
        "identb": identb, "identf": identf,
        "gpos": gpos, "gdecC": gdecC, "mask3": mask3, "gchunk": gchunk,
        "gfix": gfix_h,
    }

    in_maps = []
    for c in range(8):
        s, h = c // 2, c % 2
        own = np.ascontiguousarray(x[s, h * T:(h + 1) * T, :].T)
        part = np.ascontiguousarray(x[s, (1 - h) * T:(2 - h) * T, :].T)
        m = dict(shared)
        m["x_own"] = own
        m["x_part"] = part
        m["sgate"] = np.full((P, 1), float(h), f32)
        m["wg_sl"] = np.ascontiguousarray(wg_p[c * 384:(c + 1) * 384])
        m["wu_sl"] = np.ascontiguousarray(wu_p[c * 384:(c + 1) * 384])
        m["wd_sl"] = np.ascontiguousarray(wd[c * P:(c + 1) * P])
        in_maps.append(m)
    return in_maps


_NC_CACHE = {}


def kernel(**inputs):
    in_maps = _prep_inputs(inputs)
    if PHASES not in _NC_CACHE:
        _NC_CACHE[PHASES] = build_program()
    nc = _NC_CACHE[PHASES]
    res = run_bass_kernel_spmd(nc, in_maps, core_ids=list(range(8)))
    out = np.empty((4, SEQ, D), np.float32)
    for c in range(8):
        s, h = c // 2, c % 2
        out[s, h * T:(h + 1) * T, :] = res.results[c]["y"].T
    return out
